# version 07
# baseline (speedup 1.0000x reference)
import numpy as np
import contextlib
import concourse.bass as bass
import concourse.mybir as mybir
from concourse.bass_utils import run_bass_kernel_spmd
from concourse.replica_groups import maybe_share_collective_output_space

F32 = mybir.dt.float32
BF16 = mybir.dt.bfloat16
FP16 = mybir.dt.float16
AF = mybir.ActivationFunctionType
ALU = mybir.AluOpType
AX = mybir.AxisListType

NCORE = 8
TOK = 2048
NCH = 16
D = 1024
DCH = 8
EPS = 1e-5
C_BN = 1.0 / 16384.0
C_LN = 1.0 / 1024.0

ENGS = ("sync", "scalar", "vector", "tensor", "gpsimd")


class Prog:

    def __init__(self, sems):
        self.ops = {e: [] for e in ENGS}
        self.sems = sems
        self.cnt = {n: 0 for n in sems}
        self.last_wait = {}

    def wait(self, eng, sem, val):
        assert val <= self.cnt[sem], f"wait {sem}>={val} but only {self.cnt[sem]}"
        key = (eng, sem)
        if self.last_wait.get(key, -1) >= val:
            return
        self.last_wait[key] = val
        h = self.sems[sem]
        self.ops[eng].append(lambda e, h=h, v=val: e.wait_ge(h, v))

    AUTO = {"vector": "vE", "scalar": "sE"}

    def run(self, eng, fn, inc=None, n=1):
        own = self.AUTO.get(eng)
        if own is not None:
            self.wait(eng, own, self.cnt[own])
            h = self.sems[own]
            self.cnt[own] += 1
            self.ops[eng].append(lambda e, f=fn, h=h: f(e).then_inc(h, 1))
            return self.cnt[own]
        if inc is None:
            self.ops[eng].append(fn)
            return None
        h = self.sems[inc]
        self.cnt[inc] += n
        self.ops[eng].append(lambda e, f=fn, h=h, n=n: f(e).then_inc(h, n))
        return self.cnt[inc]

    def dma(self, eng, out, in_, sem="dS", **kw):
        return self.run(eng, lambda e: e.dma_start(out=out, in_=in_, **kw),
                        inc=sem, n=16)


class _StopBuild(Exception):
    pass


def build(taps=None, stop=None):
    nc = bass.Bass("TRN2", target_bir_lowering=False)
    tap_list = list(taps or [])
    tap_done = []

    x_in = nc.dram_tensor("x_in", [TOK, D], F32, kind="ExternalInput").ap()
    wqT = nc.dram_tensor("wqT", [D, D], F32, kind="ExternalInput").ap()
    wkT = nc.dram_tensor("wkT", [D, D], F32, kind="ExternalInput").ap()
    wvT = nc.dram_tensor("wvT", [D, D], F32, kind="ExternalInput").ap()
    wpT = nc.dram_tensor("wpT", [D, D], F32, kind="ExternalInput").ap()
    wT = nc.dram_tensor("wT", [D, D], F32, kind="ExternalInput").ap()
    gq_d = nc.dram_tensor("gq", [128, 8], F32, kind="ExternalInput").ap()
    bq_d = nc.dram_tensor("bq", [128, 8], F32, kind="ExternalInput").ap()
    gk_d = nc.dram_tensor("gk", [128, 8], F32, kind="ExternalInput").ap()
    bk_d = nc.dram_tensor("bk", [128, 8], F32, kind="ExternalInput").ap()
    gv_d = nc.dram_tensor("gv", [1, D], F32, kind="ExternalInput").ap()
    bv_d = nc.dram_tensor("bv", [1, D], F32, kind="ExternalInput").ap()
    bp_d = nc.dram_tensor("bp", [1, D], F32, kind="ExternalInput").ap()
    bb_d = nc.dram_tensor("bb", [1, D], F32, kind="ExternalInput").ap()
    ldT_d = nc.dram_tensor("ldT", [100, 4], F32, kind="ExternalInput").ap()
    luT_d = nc.dram_tensor("luT", [4, 50], F32, kind="ExternalInput").ap()
    dax_d = nc.dram_tensor("dax", [100, D], F32, kind="ExternalInput").ap()
    uaxT_d = nc.dram_tensor("uaxT", [50, D], F32, kind="ExternalInput").ap()
    y_out = nc.dram_tensor("y", [TOK, D], F32, kind="ExternalOutput").ap()

    rg8 = [list(range(8))]
    rg4 = [[0, 1, 2, 3], [4, 5, 6, 7]]
    sp8 = maybe_share_collective_output_space("AllReduce", rg8)
    sp4 = maybe_share_collective_output_space("AllReduce", rg4)
    arq_i = nc.dram_tensor("arq_i", [128, 16], F32)
    arq_o = nc.dram_tensor("arq_o", [128, 16], F32, addr_space=sp8)
    ark_i = nc.dram_tensor("ark_i", [128, 16], F32)
    ark_o = nc.dram_tensor("ark_o", [128, 16], F32, addr_space=sp8)
    arv_i = nc.dram_tensor("arv_i", [2, D], F32)
    arv_o = nc.dram_tensor("arv_o", [2, D], F32, addr_space=sp8)
    arw_i = nc.dram_tensor("arw_i", [128, D], F32)
    arw_o = nc.dram_tensor("arw_o", [128, D], F32, addr_space=sp4)

    A0 = (nc.sbuf_base + 31) // 32 * 32
    NEED = 0x33080
    assert A0 + NEED <= nc.sbuf_top, (A0, NEED, nc.sbuf_top)
    nc.alloc_sbuf_tensor("arena", [128, NEED + 64], mybir.dt.uint8)

    def at(name, shape, dtype, off):
        return nc.alloc_sbuf_tensor_at(name, shape, dtype, offset=A0 + off)

    LNT = at("LNT", [128, 8, 16, 128], FP16, 0x00000)
    RING = at("RING", [128, 4, D], FP16, 0x08000)
    SREP = at("SREP", [128, D], FP16, 0x08000)
    BREP = at("BREP", [128, D], FP16, 0x08800)
    HQ = at("HQ", [128, 8, TOK], FP16, 0x0A000)
    QK = at("QK", [128, 8, TOK], FP16, 0x0A000)
    HK = at("HK", [128, 8, TOK], FP16, 0x12000)
    KB = at("KB", [128, 8, TOK], FP16, 0x12000)
    M_sb = at("M_sb", [128, 8, D], FP16, 0x12000)
    FROLL = at("FROLL", [128, 4, D], F32, 0x16000)
    HV = at("HV", [128, NCH, D], FP16, 0x1A000)
    VTOK = at("VTOK", [128, NCH, D], FP16, 0x1A000)
    XS = at("XS", [128, 2, D], F32, 0x1A000)
    VVF = at("VVF", [128, 8, 128], F32, 0x1A000)
    VVB = at("VVB", [128, 8, 128], FP16, 0x1C000)
    DAXB = at("DAXB", [100, D], FP16, 0x1C800)
    UAXB = at("UAXB", [50, D], FP16, 0x1D000)
    AB_ = at("AB_", [4, D], FP16, 0x1D800)
    BB_ = at("BB_", [50, D], FP16, 0x1E000)
    XP = at("XP", [128, 7, D], FP16, 0x22000)
    XI = at("XI", [128, 3, D], F32, 0x22000)
    WA = at("WA", [128, 8, D], FP16, 0x25800)
    WB = at("WB", [128, 8, D], FP16, 0x29800)
    WS = at("WS", [128, 2, D], F32, 0x2D800)
    JK1 = at("JK1", [128, D], FP16, 0x2F800)
    JK2 = at("JK2", [128, D], FP16, 0x30000)
    VS0 = at("VS0", [1, D], F32, 0x2A800)
    VS1 = at("VS1", [1, D], F32, 0x29800)
    VTMP = at("VTMP", [1, D], F32, 0x22000)
    VGAM = at("VGAM", [1, D], F32, 0x23000)
    VBET = at("VBET", [1, D], F32, 0x24000)
    SVB0 = at("SVB0", [1, D], FP16, 0x30800)
    SVB1 = at("SVB1", [1, D], FP16, 0x31000)
    ROWP = at("ROWP", [1, D], FP16, 0x31800)
    ROWB = at("ROWB", [1, D], FP16, 0x32000)
    GQ = at("GQ", [128, 8], F32, 0x32800)
    BQ = at("BQ", [128, 8], F32, 0x32820)
    GK = at("GK", [128, 8], F32, 0x32840)
    BK = at("BK", [128, 8], F32, 0x32860)
    QST = at("QST", [128, 16], F32, 0x32880)
    KST = at("KST", [128, 16], F32, 0x328c0)
    ACC4 = at("ACC4", [128, 8, 4], F32, 0x32900)
    QSV = at("QSV", [128, 8], F32, 0x32980)
    QBV = at("QBV", [128, 8], F32, 0x329a0)
    KSV = at("KSV", [128, 8], F32, 0x329c0)
    KBV = at("KBV", [128, 8], F32, 0x329e0)
    LSUM = at("LSUM", [128, 16], F32, 0x32a00)
    LSQ = at("LSQ", [128, 16], F32, 0x32a40)
    LMU = at("LMU", [128, 16], F32, 0x32a80)
    LR = at("LR", [128, 16], F32, 0x32ac0)
    T8A = at("T8A", [128, 8], F32, 0x32b00)
    T8B = at("T8B", [128, 8], F32, 0x32b20)
    ONE1 = at("ONE1", [1, 128], FP16, 0x32b40)
    ONEC = at("ONEC", [128, 1], FP16, 0x32c40)
    LDTB = at("LDTB", [100, 4], FP16, 0x32c60)
    LUTB = at("LUTB", [4, 50], FP16, 0x32c80)
    LSUM2 = at("LSUM2", [128, 16], F32, 0x32d00)
    LSQ2 = at("LSQ2", [128, 16], F32, 0x32d40)
    LMU2 = at("LMU2", [128, 16], F32, 0x32d80)
    LR2 = at("LR2", [128, 16], F32, 0x32dc0)
    LSUM3 = at("LSUM3", [128, 16], F32, 0x32e00)
    LSQ3 = at("LSQ3", [128, 16], F32, 0x32e40)
    LMU3 = at("LMU3", [128, 16], F32, 0x32e80)
    LR3 = at("LR3", [128, 16], F32, 0x32ec0)
    LSUM4 = at("LSUM4", [128, 16], F32, 0x32f00)
    LSQ4 = at("LSQ4", [128, 16], F32, 0x32f40)
    LMU4 = at("LMU4", [128, 16], F32, 0x32f80)
    LR4 = at("LR4", [128, 16], F32, 0x32fc0)

    ctx = contextlib.ExitStack()
    pt = [ctx.enter_context(nc.psum_tensor(f"pt{i}", [128, D], F32)) for i in range(4)]
    DMA_SEMS = ("dS", "dG", "dT", "dW0", "dW1", "dX0", "dX1", "dX2",
                "dXS0", "dXS1", "dSt0", "dSt1", "dSt2", "dSt3", "dR")
    sems = {n: ctx.enter_context(nc.semaphore(n)) for n in
            DMA_SEMS + ("cc", "tE", "sE", "vE")}
    P = Prog(sems)

    def tap(name, ap, shape, dtype):
        if name not in tap_list:
            return
        for s in DMA_SEMS + ("vE", "sE", "tE"):
            P.wait("sync", s, P.cnt[s])
        t = nc.dram_tensor("tap_" + name, shape, dtype, kind="ExternalOutput").ap()
        P.dma("sync", t[...], ap)
        tap_done.append(name)

    def ckpt(name):
        if stop == name:
            raise _StopBuild

    def mm(psum_ap, lhsT, rhs, start, stop, inc=False):
        return P.run("tensor",
                     lambda e, o=psum_ap, l=lhsT, r=rhs, s1=start, s2=stop:
                     e.matmul(out=o, lhsT=l, rhs=r, start=s1, stop=s2),
                     inc=("tE" if inc else None))

    def lnT_rhs(dc, nt):
        return LNT.ap()[:, dc, 4 * nt:4 * nt + 4, :].rearrange("p a b -> p (a b)")

    try:
        P.run("vector", lambda e: e.memset(ONE1[:, :], 1.0))
        P.run("vector", lambda e: e.memset(ONEC[:, :], 1.0), inc="vE")

        for dst, src in ((GQ, gq_d), (BQ, bq_d), (GK, gk_d), (BK, bk_d)):
            P.dma("sync", dst[:, :], src[:, :], "dW0")
        P.dma("sync", WS[0:1, 0, :], bp_d[:, :], "dW0")
        P.dma("sync", WS[0:1, 1, :], bb_d[:, :], "dW0")
        P.wait("vector", "dW0", P.cnt["dW0"])
        P.run("vector", lambda e: e.tensor_copy(out=ROWP.ap()[0:1, :], in_=WS[0:1, 0, :]))
        P.run("vector", lambda e: e.tensor_copy(out=ROWB.ap()[0:1, :], in_=WS[0:1, 1, :]),
              inc="vE")
        rows_vE = P.cnt["vE"]

        ws_free = {0: rows_vE, 1: rows_vE}

        def load_weight(dst, src_dram, scale=None, pre_tE=0):
            for c in range(DCH):
                slot = c % 2
                P.wait("sync", "vE", ws_free[slot])
                P.dma("sync", WS[:, slot, :], src_dram[128 * c:128 * (c + 1), :],
                      f"dW{slot}")
                P.wait("vector", f"dW{slot}", P.cnt[f"dW{slot}"])
                if pre_tE:
                    P.wait("vector", "tE", pre_tE)
                if scale is None:
                    P.run("vector",
                          lambda e, s=slot, c=c: e.tensor_copy(out=dst[:, c, :],
                                                               in_=WS[:, s, :]), inc="vE")
                else:
                    P.run("vector",
                          lambda e, s=slot, c=c: e.tensor_scalar(
                              out=dst[:, c, :], in0=WS[:, s, :], scalar1=scale,
                              scalar2=None, op0=ALU.mult), inc="vE")
                ws_free[slot] = P.cnt["vE"]
            return P.cnt["vE"]

        wq_vE = load_weight(WB, wqT)

        xload = {}
        app_ev = {}
        trj_ev = {}
        tr_ev = {}
        for j in range(3):
            P.dma("sync", XI[:, j, :], x_in[128 * j:128 * (j + 1), :], f"dX{j}")
        groups = [list(range(a, min(a + 3, NCH))) for a in range(0, NCH, 3)]
        for grp in groups:
            for j in grp:
                P.wait("vector", f"dX{j % 3}", P.cnt[f"dX{j % 3}"])
                xc = XI[:, j % 3, :]
                P.run("vector", lambda e, xc=xc, j=j: e.tensor_scalar(
                    out=JK2[:, :], in0=xc, scalar1=1.0, scalar2=0.0, op0=ALU.mult,
                    op1=ALU.add, accum_out=LSUM[:, j:j + 1]), inc="vE")
                P.wait("scalar", f"dX{j % 3}", P.cnt[f"dX{j % 3}"])
                P.run("scalar", lambda e, xc=xc, j=j: e.activation(
                    out=JK1[:, :], in_=xc, func=AF.Square,
                    accum_out=LSQ[:, j:j + 1]), inc="sE")
            sl = slice(grp[0], grp[-1] + 1)
            swidth = len(grp)
            P.wait("vector", "sE", P.cnt["sE"])
            P.run("vector", lambda e, sl=sl: e.tensor_scalar(
                out=LMU[:, sl], in0=LSUM[:, sl], scalar1=C_LN, scalar2=None,
                op0=ALU.mult))
            P.run("vector", lambda e, sl=sl: e.tensor_scalar(
                out=LR[:, sl], in0=LSQ[:, sl], scalar1=C_LN, scalar2=None,
                op0=ALU.mult))
            P.run("vector", lambda e, sl=sl, w=swidth: e.tensor_tensor(
                out=T8A[:, 0:w], in0=LMU[:, sl], in1=LMU[:, sl], op=ALU.mult))
            P.run("vector", lambda e, sl=sl, w=swidth: e.tensor_tensor(
                out=LR[:, sl], in0=LR[:, sl], in1=T8A[:, 0:w], op=ALU.subtract))
            P.run("vector", lambda e, sl=sl: e.tensor_scalar(
                out=LR[:, sl], in0=LR[:, sl], scalar1=EPS, scalar2=None,
                op0=ALU.add), inc="vE")
            P.wait("scalar", "vE", P.cnt["vE"])
            P.run("scalar", lambda e, sl=sl: e.activation(
                out=LR[:, sl], in_=LR[:, sl], func=AF.Sqrt), inc="sE")
            P.wait("vector", "sE", P.cnt["sE"])
            P.run("vector", lambda e, sl=sl: e.reciprocal(
                out=LR[:, sl], in_=LR[:, sl]), inc="vE")
            for j in grp:
                xc = XI[:, j % 3, :]
                if j >= 4:
                    P.wait("vector", "dT", trj_ev[j - 4])
                P.run("vector", lambda e, xc=xc, j=j: e.tensor_scalar(
                    out=RING[:, j % 4, :], in0=xc, scalar1=LMU[:, j:j + 1],
                    scalar2=LR[:, j:j + 1], op0=ALU.subtract, op1=ALU.mult),
                    inc="vE")
                app_ev[j] = P.cnt["vE"]
            for j in grp:
                P.wait("sync", "vE", app_ev[j])
                for cl in DMA_SEMS:
                    P.wait("sync", cl, P.cnt[cl])
                P.dma("sync", LNT.ap()[:, :, j, :], RING[:, j % 4, :], "dT",
                      transpose=True)
                trj_ev[j] = P.cnt["dT"]
                P.wait("sync", "dT", P.cnt["dT"])
                if j % 4 == 3:
                    tr_ev[j // 4] = P.cnt["dT"]
                if j + 3 < NCH:
                    P.wait("sync", "vE", app_ev[j])
                    P.dma("sync", XI[:, (j + 3) % 3, :],
                          x_in[128 * (j + 3):128 * (j + 4), :], f"dX{j % 3}")
        tr_ev[3] = P.cnt["dT"]

        tap("ln1T", LNT.ap()[:, :, :, :], [128, 8, 16, 128], FP16)
        ckpt("ln1")

        def emit_branch_stats(h_sb, state, ecs):
            for ec in ecs:
                P.wait("vector", "sE", state["ec_sE"][ec])
                P.run("vector", lambda e, ec=ec: e.scalar_tensor_tensor(
                    out=JK2[:, :], in0=h_sb.ap()[:, ec, 0:1024], scalar=1.0,
                    in1=h_sb.ap()[:, ec, 0:1024], op0=ALU.mult, op1=ALU.mult,
                    accum_out=T8A[:, ec:ec + 1]))
                P.run("vector", lambda e, ec=ec: e.scalar_tensor_tensor(
                    out=JK2[:, :], in0=h_sb.ap()[:, ec, 1024:2048], scalar=1.0,
                    in1=h_sb.ap()[:, ec, 1024:2048], op0=ALU.mult, op1=ALU.mult,
                    accum_out=T8B[:, ec:ec + 1]), inc="vE")

        def branch_fm(w_sb, w_vE, h_sb, stat_sb, pre_sE, ecs=None, state=None,
                      defer_stats=False):
            regions = [(pt[0][:, 0:512], 0), (pt[0][:, 512:1024], 1),
                       (pt[1][:, 0:512], 2), (pt[1][:, 512:1024], 3)]
            if state is None:
                state = {"evac_done": {}, "k": 0}
            evac_done = state["evac_done"]
            for ec in (ecs if ecs is not None else range(DCH)):
                for nt in range(4):
                    k = state["k"]
                    ps, ridx = regions[k % 4]
                    if k >= 4:
                        P.wait("tensor", "sE", evac_done[ridx])
                    elif pre_sE:
                        P.wait("tensor", "sE", pre_sE)
                    P.wait("tensor", "vE", w_vE)
                    P.wait("tensor", "dT", tr_ev[nt])
                    for dc in range(DCH):
                        mm(ps, w_sb[:, dc, 128 * ec:128 * (ec + 1)], lnT_rhs(dc, nt),
                           start=(dc == 0), stop=(dc == DCH - 1), inc=(dc == DCH - 1))
                    mm_ev = P.cnt["tE"]
                    P.wait("scalar", "tE", mm_ev)
                    P.run("scalar", lambda e, ps=ps, ec=ec, nt=nt: e.activation(
                        out=h_sb.ap()[:, ec, 512 * nt:512 * (nt + 1)], in_=ps,
                        func=AF.Copy, accum_out=ACC4[:, ec, nt:nt + 1]), inc="sE")
                    evac_done[ridx] = P.cnt["sE"]
                    state["k"] = k = k + 1
                state.setdefault("ec_sE", {})[ec] = P.cnt["sE"]
                if not defer_stats:
                    emit_branch_stats(h_sb, state, [ec])
            if not defer_stats and (ecs is None or max(ecs) == DCH - 1):
                P.run("vector", lambda e: e.tensor_reduce(
                    out=stat_sb[:, 0:8], in_=ACC4[:, :, :], axis=AX.X, op=ALU.add))
                P.run("vector", lambda e: e.tensor_tensor(
                    out=stat_sb[:, 8:16], in0=T8A[:, :], in1=T8B[:, :], op=ALU.add),
                    inc="vE")
            return {"vE": P.cnt["vE"], "tE": P.cnt["tE"], "sE": P.cnt["sE"],
                    "state": state}

        def allreduce(in_ap, out_ap, sb_ap, ready, rg):
            P.wait("gpsimd", ready[0], ready[1])
            P.wait("gpsimd", "dT", P.cnt["dT"])
            P.dma("gpsimd", in_ap, sb_ap, "dG")
            P.wait("gpsimd", "dG", P.cnt["dG"])
            P.run("gpsimd", lambda e, i=in_ap, o=out_ap, rg=rg: e.collective_compute(
                "AllReduce", ALU.add, replica_groups=rg, ins=[i], outs=[o]), inc="cc")
            P.wait("gpsimd", "cc", P.cnt["cc"])
            P.dma("gpsimd", sb_ap, out_ap, "dG")
            P.wait("gpsimd", "dG", P.cnt["dG"])
            return ("dG", P.cnt["dG"])

        def bn_prep(stat, g_sb, b_sb, sv, bv, ar_ev):
            P.wait("vector", ar_ev[0], ar_ev[1])
            P.run("vector", lambda e: e.tensor_scalar(
                out=T8A[:, :], in0=stat[:, 0:8], scalar1=C_BN, scalar2=None, op0=ALU.mult))
            P.run("vector", lambda e: e.tensor_scalar(
                out=T8B[:, :], in0=stat[:, 8:16], scalar1=C_BN, scalar2=None, op0=ALU.mult))
            P.run("vector", lambda e: e.tensor_tensor(
                out=sv, in0=T8A[:, :], in1=T8A[:, :], op=ALU.mult))
            P.run("vector", lambda e: e.tensor_tensor(
                out=T8B[:, :], in0=T8B[:, :], in1=sv, op=ALU.subtract))
            P.run("vector", lambda e: e.tensor_scalar(
                out=sv, in0=T8B[:, :], scalar1=EPS, scalar2=None, op0=ALU.add), inc="vE")
            P.wait("scalar", "vE", P.cnt["vE"])
            P.run("scalar", lambda e: e.activation(out=sv, in_=sv, func=AF.Sqrt), inc="sE")
            P.wait("vector", "sE", P.cnt["sE"])
            P.run("vector", lambda e: e.reciprocal(out=sv, in_=sv))
            P.run("vector", lambda e: e.tensor_tensor(
                out=sv, in0=sv, in1=g_sb[:, :], op=ALU.mult))
            P.run("vector", lambda e: e.tensor_scalar(
                out=sv, in0=sv, scalar1=4.0, scalar2=None, op0=ALU.mult))
            P.run("vector", lambda e: e.tensor_tensor(
                out=bv, in0=T8A[:, :], in1=sv, op=ALU.mult))
            P.run("vector", lambda e: e.tensor_scalar(
                out=T8B[:, :], in0=b_sb[:, :], scalar1=4.0, scalar2=None, op0=ALU.mult))
            P.run("vector", lambda e: e.tensor_tensor(
                out=T8B[:, :], in0=T8B[:, :], in1=bv, op=ALU.subtract))
            P.run("vector", lambda e: e.tensor_scalar(
                out=bv, in0=T8B[:, :], scalar1=-4.0, scalar2=None, op0=ALU.add), inc="vE")
            return P.cnt["vE"]

        qb = branch_fm(WB, wq_vE, HQ, QST[:, :], pre_sE=0)
        tap("hq", HQ.ap()[:, :, :], [128, 8, TOK], FP16)
        tap("qstat_loc", QST[:, :], [128, 16], F32)
        ckpt("q_mm")
        arq_ev = allreduce(arq_i.ap(), arq_o.ap(), QST[:, :], ("vE", qb["vE"]), rg8)
        if "qstat_g" in tap_list:
            P.wait("sync", arq_ev[0], arq_ev[1])
        tap("qstat_g", QST[:, :], [128, 16], F32)
        ckpt("arq")

        wv_vE = load_weight(WA, wvT)

        vprod = [pt[0], pt[1]]
        evac_done_v = {}
        sqmm_ev = {}
        for j in range(NCH):
            ps = vprod[j % 2]
            if j >= 2:
                P.wait("tensor", "sE", evac_done_v[j % 2])
            else:
                P.wait("tensor", "sE", qb["sE"])
            P.wait("tensor", "vE", wv_vE)
            P.wait("tensor", "dT", tr_ev[j // 4])
            for dc in range(DCH):
                for h in range(2):
                    mm(ps[:, 512 * h:512 * (h + 1)], LNT.ap()[:, dc, j, :],
                       WA[:, dc, 512 * h:512 * (h + 1)],
                       start=(dc == 0), stop=(dc == DCH - 1),
                       inc=(dc == DCH - 1 and h == 1))
            mm_ev = P.cnt["tE"]
            P.wait("scalar", "tE", mm_ev)
            P.run("scalar", lambda e, ps=ps, j=j: e.activation(
                out=HV.ap()[:, j, :], in_=ps[:, :], func=AF.Copy), inc="sE")
            evac_done_v[j % 2] = P.cnt["sE"]
            hv_ev = P.cnt["sE"]
            jk = JK1 if j % 2 == 0 else JK2
            if j >= 2:
                P.wait("vector", "tE", sqmm_ev[j - 2])
            P.wait("vector", "sE", hv_ev)
            P.run("vector", lambda e, j=j, jk=jk: e.scalar_tensor_tensor(
                out=jk[:, :], in0=HV.ap()[:, j, :], scalar=1.0,
                in1=HV.ap()[:, j, :], op0=ALU.mult, op1=ALU.mult), inc="vE")
            sq_ev = P.cnt["vE"]
            P.wait("tensor", "sE", hv_ev)
            for h in range(2):
                mm(pt[2][0:1, 512 * h:512 * (h + 1)], ONEC[:, 0:1],
                   HV.ap()[:, j, 512 * h:512 * (h + 1)],
                   start=(j == 0), stop=(j == NCH - 1))
            P.wait("tensor", "vE", sq_ev)
            for h in range(2):
                mm(pt[3][0:1, 512 * h:512 * (h + 1)], ONEC[:, 0:1],
                   jk[:, 512 * h:512 * (h + 1)],
                   start=(j == 0), stop=(j == NCH - 1), inc=(h == 1))
            sqmm_ev[j] = P.cnt["tE"]
        v_mm_tE = P.cnt["tE"]
        P.wait("vector", "tE", v_mm_tE)
        P.run("vector", lambda e: e.tensor_copy(out=VS0.ap()[0:1, :], in_=pt[2][0:1, :]))
        P.run("vector", lambda e: e.tensor_copy(out=VS1.ap()[0:1, :], in_=pt[3][0:1, :]),
              inc="vE")
        vstat_vE = P.cnt["vE"]
        ckpt("v_mm")
        tap("hv", HV.ap()[:, :, :], [128, NCH, D], FP16)
        tap("vstat_loc", VS0.ap()[0:1, :], [1, D], F32)
        P.wait("gpsimd", "vE", vstat_vE)
        P.wait("gpsimd", "dT", P.cnt["dT"])
        P.dma("gpsimd", arv_i.ap()[0:1, :], VS0.ap()[0:1, :], "dG")
        P.dma("gpsimd", arv_i.ap()[1:2, :], VS1.ap()[0:1, :], "dG")
        P.wait("gpsimd", "dG", P.cnt["dG"])
        P.run("gpsimd", lambda e: e.collective_compute(
            "AllReduce", ALU.add, replica_groups=rg8, ins=[arv_i.ap()],
            outs=[arv_o.ap()]), inc="cc")
        P.wait("gpsimd", "cc", P.cnt["cc"])
        P.dma("gpsimd", VS0.ap()[0:1, :], arv_o.ap()[0:1, :], "dG")
        P.dma("gpsimd", VS1.ap()[0:1, :], arv_o.ap()[1:2, :], "dG")
        P.wait("gpsimd", "dG", P.cnt["dG"])
        arv_ev = ("dG", P.cnt["dG"])
        ckpt("arv")

        wk_vE = load_weight(WA, wkT, pre_tE=v_mm_tE)

        qprep_vE = bn_prep(QST[:, :], GQ, BQ, QSV[:, :], QBV[:, :], arq_ev)
        tap("qsv", QSV[:, :], [128, 8], F32)
        tap("qbv", QBV[:, :], [128, 8], F32)
        for ec in range(DCH):
            P.wait("scalar", "vE", qprep_vE)
            P.run("scalar", lambda e, ec=ec: e.activation(
                out=QK.ap()[:, ec, :], in_=HQ.ap()[:, ec, :], func=AF.Sigmoid,
                scale=QSV[:, ec:ec + 1], bias=QBV[:, ec:ec + 1]), inc="sE")
        qsig_sE = P.cnt["sE"]
        qsig_vE = P.cnt["vE"]
        tap("q", QK.ap()[:, :, :], [128, 8, TOK], FP16)
        ckpt("qsig")

        kst = branch_fm(WA, wk_vE, HK, KST[:, :], pre_sE=evac_done_v[1],
                        ecs=range(6), defer_stats=True)["state"]

        P.wait("sync", "vE", app_ev[NCH - 1])
        P.dma("sync", VGAM.ap()[0:1, :], gv_d[:, :], "dR")
        P.dma("sync", VBET.ap()[0:1, :], bv_d[:, :], "dR")
        P.wait("vector", arv_ev[0], arv_ev[1])
        P.wait("vector", "dR", P.cnt["dR"])
        vs0 = VS0.ap()[0:1, :]
        vs1 = VS1.ap()[0:1, :]
        vt = VTMP.ap()[0:1, :]
        P.run("vector", lambda e: e.tensor_scalar(
            out=vs0, in0=vs0, scalar1=C_BN, scalar2=None, op0=ALU.mult))
        P.run("vector", lambda e: e.tensor_scalar(
            out=vs1, in0=vs1, scalar1=C_BN, scalar2=None, op0=ALU.mult))
        P.run("vector", lambda e: e.tensor_tensor(
            out=vt, in0=vs0, in1=vs0, op=ALU.mult))
        P.run("vector", lambda e: e.tensor_tensor(
            out=vs1, in0=vs1, in1=vt, op=ALU.subtract))
        P.run("vector", lambda e: e.tensor_scalar(
            out=vs1, in0=vs1, scalar1=EPS, scalar2=None, op0=ALU.add), inc="vE")
        P.wait("scalar", "vE", P.cnt["vE"])
        P.run("scalar", lambda e: e.activation(out=vs1, in_=vs1, func=AF.Sqrt), inc="sE")
        P.wait("vector", "sE", P.cnt["sE"])
        P.run("vector", lambda e: e.reciprocal(out=vs1, in_=vs1))
        P.run("vector", lambda e: e.tensor_tensor(
            out=vs1, in0=vs1, in1=VGAM.ap()[0:1, :], op=ALU.mult))
        P.run("vector", lambda e: e.tensor_scalar(
            out=vs1, in0=vs1, scalar1=4.0, scalar2=None, op0=ALU.mult))
        P.run("vector", lambda e: e.tensor_tensor(
            out=vt, in0=vs0, in1=vs1, op=ALU.mult))
        P.run("vector", lambda e: e.tensor_scalar(
            out=VBET.ap()[0:1, :], in0=VBET.ap()[0:1, :], scalar1=4.0,
            scalar2=None, op0=ALU.mult))
        P.run("vector", lambda e: e.tensor_tensor(
            out=vt, in0=vt, in1=VBET.ap()[0:1, :], op=ALU.subtract))
        P.run("vector", lambda e: e.tensor_scalar(
            out=vt, in0=vt, scalar1=-1.0, scalar2=-4.0, op0=ALU.mult,
            op1=ALU.add))
        P.run("vector", lambda e: e.tensor_copy(out=SVB0.ap()[0:1, :], in_=vs1))
        P.run("vector", lambda e: e.tensor_copy(out=SVB1.ap()[0:1, :], in_=vt),
              inc="vE")
        svb_vE = P.cnt["vE"]
        ckpt("vprep")
        P.wait("tensor", "vE", svb_vE)
        P.wait("tensor", "vE", vstat_vE)
        for h in range(2):
            mm(pt[3][:, 512 * h:512 * (h + 1)], ONE1[0:1, :],
               SVB0.ap()[0:1, 512 * h:512 * (h + 1)], start=True, stop=True,
               inc=(h == 1))
        srep_tE = P.cnt["tE"]
        P.wait("vector", "tE", srep_tE)
        P.wait("vector", "dT", tr_ev[3])
        P.run("vector", lambda e: e.tensor_copy(out=SREP[:, :], in_=pt[3][:, :]), inc="vE")
        srep_vE = P.cnt["vE"]
        P.wait("tensor", "vE", srep_vE)
        for h in range(2):
            mm(pt[3][:, 512 * h:512 * (h + 1)], ONE1[0:1, :],
               SVB1.ap()[0:1, 512 * h:512 * (h + 1)], start=True, stop=True,
               inc=(h == 1))
        brep_tE = P.cnt["tE"]
        P.wait("vector", "tE", brep_tE)
        P.run("vector", lambda e: e.tensor_copy(out=BREP[:, :], in_=pt[3][:, :]), inc="vE")
        brep_vE = P.cnt["vE"]
        ckpt("brep")

        for j in range(NCH):
            P.wait("vector", "vE", brep_vE)
            P.run("vector", lambda e, j=j: e.tensor_tensor(
                out=HV.ap()[:, j, :], in0=HV.ap()[:, j, :], in1=SREP[:, :],
                op=ALU.mult))
            P.run("vector", lambda e, j=j: e.tensor_tensor(
                out=HV.ap()[:, j, :], in0=HV.ap()[:, j, :], in1=BREP[:, :],
                op=ALU.add), inc="vE")
            P.wait("scalar", "vE", P.cnt["vE"])
            P.run("scalar", lambda e, j=j: e.activation(
                out=VTOK.ap()[:, j, :], in_=HV.ap()[:, j, :], func=AF.Sigmoid),
                inc="sE")
        vsig_sE = P.cnt["sE"]
        ckpt("vsig")
        tap("v", VTOK.ap()[:, :, :], [128, NCH, D], FP16)
        emit_branch_stats(HK, kst, range(6))

        P.wait("tensor", "sE", vsig_sE)
        P.wait("tensor", "vE", brep_vE)
        for s in range(DCH):
            for j in range(NCH):
                mm(pt[2][:, 128 * s:128 * (s + 1)],
                   VTOK.ap()[:, j, 128 * s:128 * (s + 1)],
                   VTOK.ap()[:, j, 128 * s:128 * (s + 1)],
                   start=(j == 0), stop=(j == NCH - 1), inc=(j == NCH - 1))
        vv_tE = P.cnt["tE"]
        P.wait("vector", "tE", vv_tE)
        P.run("vector", lambda e: e.tensor_copy(
            out=VVF.ap().rearrange("p a b -> p (a b)"), in_=pt[2][:, :]), inc="vE")
        vv_loc_vE = P.cnt["vE"]
        tap("vv_loc", VVF.ap()[:, :, :], [128, 8, 128], F32)
        arw_ev = allreduce(arw_i.ap(), arw_o.ap(),
                           VVF.ap().rearrange("p a b -> p (a b)"),
                           ("vE", vv_loc_vE), rg4)
        P.wait("vector", arw_ev[0], arw_ev[1])
        P.run("vector", lambda e: e.memset(VVB.ap().rearrange("p a b -> p (a b)"), 0.0))
        for s in range(DCH):
            P.run("vector", lambda e, s=s: e.tensor_copy(
                out=VVB.ap()[0:64, s, 0:64], in_=VVF.ap()[0:64, s, 0:64]))
            P.run("vector", lambda e, s=s: e.tensor_copy(
                out=VVB.ap()[64:128, s, 64:128], in_=VVF.ap()[64:128, s, 64:128]))
        P.run("vector", lambda e: e.nop(), inc="vE")
        vvb_vE = P.cnt["vE"]
        tap("vv", VVB.ap()[:, :, :], [128, 8, 128], FP16)
        ckpt("vv_loc0")

        kb = branch_fm(WA, wk_vE, HK, KST[:, :], pre_sE=0, ecs=range(6, 8),
                       state=kst, defer_stats=True)
        emit_branch_stats(HK, kst, range(6, 8))
        P.run("vector", lambda e: e.tensor_reduce(
            out=KST[:, 0:8], in_=ACC4[:, :, :], axis=AX.X, op=ALU.add))
        P.run("vector", lambda e: e.tensor_tensor(
            out=KST[:, 8:16], in0=T8A[:, :], in1=T8B[:, :], op=ALU.add), inc="vE")
        kb["vE"] = P.cnt["vE"]
        ckpt("k_mm")
        tap("hk", HK.ap()[:, :, :], [128, 8, TOK], FP16)
        ark_ev = allreduce(ark_i.ap(), ark_o.ap(), KST[:, :], ("vE", kb["vE"]), rg8)
        ckpt("ark")

        wp_vE = load_weight(WB, wpT, scale=0.125, pre_tE=qb["tE"])
        ckpt("wp_w")

        P.wait("sync", "vE", ws_free[0])
        P.wait("sync", "vE", ws_free[1])
        P.dma("sync", WS[0:100, 0, 0:4], ldT_d[:, :], "dW0")
        P.dma("sync", WS[0:4, 1, 0:50], luT_d[:, :], "dW1")
        P.wait("vector", "dW0", P.cnt["dW0"])
        P.wait("vector", "dW1", P.cnt["dW1"])
        P.wait("vector", "vE", ws_free[0])
        P.wait("vector", "vE", ws_free[1])
        P.run("vector", lambda e: e.tensor_copy(out=LDTB.ap()[:, :], in_=WS[0:100, 0, 0:4]))
        P.run("vector", lambda e: e.tensor_copy(out=LUTB.ap()[:, :], in_=WS[0:4, 1, 0:50]),
              inc="vE")
        ws_free[0] = ws_free[1] = P.cnt["vE"]
        for c, (dst, src, rows) in enumerate(((DAXB, dax_d, 100), (UAXB, uaxT_d, 50))):
            slot = c % 2
            P.wait("sync", "vE", ws_free[slot])
            P.dma("sync", WS[0:rows, slot, :], src[:, :], f"dW{slot}")
            P.wait("vector", f"dW{slot}", P.cnt[f"dW{slot}"])
            P.wait("vector", "tE", vv_tE)
            P.run("vector", lambda e, dst=dst, slot=slot, rows=rows: e.tensor_copy(
                out=dst.ap()[:, :], in_=WS[0:rows, slot, :]), inc="vE")
            ws_free[slot] = P.cnt["vE"]
        lor_vE = P.cnt["vE"]
        P.wait("tensor", "vE", lor_vE)
        P.wait("tensor", "vE", brep_vE)
        for h in range(2):
            mm(pt[3][0:4, 512 * h:512 * (h + 1)], LDTB.ap()[:, :],
               DAXB.ap()[:, 512 * h:512 * (h + 1)], start=True, stop=True, inc=(h == 1))
        a_tE = P.cnt["tE"]
        P.wait("vector", "tE", a_tE)
        P.run("vector", lambda e: e.tensor_copy(out=AB_.ap()[:, :], in_=pt[3][0:4, :]),
              inc="vE")
        ab_vE = P.cnt["vE"]
        P.wait("tensor", "vE", ab_vE)
        for h in range(2):
            mm(pt[3][0:50, 512 * h:512 * (h + 1)], LUTB.ap()[:, :],
               AB_.ap()[:, 512 * h:512 * (h + 1)], start=True, stop=True, inc=(h == 1))
        b_tE = P.cnt["tE"]
        P.wait("vector", "tE", b_tE)
        P.run("vector", lambda e: e.tensor_copy(out=BB_.ap()[:, :], in_=pt[3][0:50, :]),
              inc="vE")
        bb_vE = P.cnt["vE"]
        ckpt("lab")
        for c in range(DCH):
            slot = c % 2
            P.wait("sync", "vE", ws_free[slot])
            P.dma("sync", WS[:, slot, :], wT[128 * c:128 * (c + 1), :], f"dW{slot}")
            P.wait("tensor", "vE", bb_vE)
            P.wait("tensor", "vE", ws_free[(c + 1) % 2])
            for h in range(2):
                mm(pt[3][:, 512 * h:512 * (h + 1)], BB_.ap()[:, 128 * c:128 * (c + 1)],
                   UAXB.ap()[:, 512 * h:512 * (h + 1)], start=True, stop=True,
                   inc=(h == 1))
            ct_tE = P.cnt["tE"]
            P.wait("vector", "tE", ct_tE)
            P.wait("vector", "tE", kb["tE"])
            P.wait("vector", f"dW{slot}", P.cnt[f"dW{slot}"])
            P.run("vector", lambda e, c=c, slot=slot: e.tensor_tensor(
                out=WA[:, c, :], in0=WS[:, slot, :], in1=pt[3][:, :], op=ALU.add),
                inc="vE")
            ws_free[slot] = P.cnt["vE"]
        weff_vE = P.cnt["vE"]
        ckpt("weff")

        kprep_vE = bn_prep(KST[:, :], GK, BK, KSV[:, :], KBV[:, :], ark_ev)
        qk_ev = {}
        m_ev = {}
        for s in range(DCH):
            P.wait("scalar", "vE", kprep_vE)
            P.run("scalar", lambda e, s=s: e.activation(
                out=KB.ap()[:, s, :], in_=HK.ap()[:, s, :], func=AF.Sigmoid,
                scale=KSV[:, s:s + 1], bias=KBV[:, s:s + 1]), inc="sE")
            ks_sE = P.cnt["sE"]
            P.wait("vector", "sE", ks_sE)
            P.wait("vector", "sE", qsig_sE)
            P.run("vector", lambda e, s=s: e.tensor_tensor(
                out=QK.ap()[:, s, :], in0=QK.ap()[:, s, :], in1=KB.ap()[:, s, :],
                op=ALU.mult), inc="vE")
            qk_ev[s] = P.cnt["vE"]
            if s % 2 == 1:
                for t in (s - 1, s):
                    P.wait("tensor", "vE", vvb_vE)
                    P.wait("tensor", "vE", wp_vE)
                    if t >= 1:
                        P.wait("tensor", "vE", m_ev[t - 1])
                    for h in range(2):
                        mm(pt[2][:, 512 * h:512 * (h + 1)], VVB.ap()[:, t, :],
                           WB[:, t, 512 * h:512 * (h + 1)], start=True, stop=True,
                           inc=(h == 1))
                    mt = P.cnt["tE"]
                    P.wait("vector", "tE", mt)
                    P.run("vector", lambda e, t=t: e.tensor_copy(
                        out=M_sb.ap()[:, t, :], in_=pt[2][:, :]), inc="vE")
                    m_ev[t] = P.cnt["vE"]
        tap("qk", QK.ap()[:, :, :], [128, 8, TOK], FP16)
        tap("M", M_sb.ap()[:, :, :], [128, 8, D], FP16)
        ckpt("m")

        xs_load = {}

        def load_xs(j):
            if j >= 2:
                P.wait("sync", "vE", xprime_ev[j - 2])
            else:
                P.wait("sync", "vE", vvb_vE)
            P.dma("sync", XS[:, j % 2, :], x_in[128 * j:128 * (j + 1), :],
                  f"dXS{j % 2}")

        proj_evac = {}
        xprime_ev = {}
        xp_free = {}
        ffn_evac = {}
        ffn_done_v = {}
        tr2_ev = {}
        load_xs(0)
        load_xs(1)

        def group_prep(lmu, lr, lsum, lsq, sl, ready_sE):
            P.wait("vector", "sE", ready_sE)
            P.run("vector", lambda e: e.tensor_scalar(
                out=lmu[:, sl], in0=lsum[:, sl], scalar1=C_LN, scalar2=None,
                op0=ALU.mult))
            P.run("vector", lambda e: e.tensor_scalar(
                out=lr[:, sl], in0=lsq[:, sl], scalar1=C_LN, scalar2=None,
                op0=ALU.mult))
            P.run("vector", lambda e: e.tensor_tensor(
                out=T8A[:, 0:4], in0=lmu[:, sl], in1=lmu[:, sl], op=ALU.mult))
            P.run("vector", lambda e: e.tensor_tensor(
                out=lr[:, sl], in0=lr[:, sl], in1=T8A[:, 0:4], op=ALU.subtract))
            P.run("vector", lambda e: e.tensor_scalar(
                out=lr[:, sl], in0=lr[:, sl], scalar1=EPS, scalar2=None,
                op0=ALU.add), inc="vE")
            P.wait("scalar", "vE", P.cnt["vE"])
            P.run("scalar", lambda e: e.activation(
                out=lr[:, sl], in_=lr[:, sl], func=AF.Sqrt), inc="sE")
            P.wait("vector", "sE", P.cnt["sE"])
            P.run("vector", lambda e: e.reciprocal(out=lr[:, sl], in_=lr[:, sl]),
                  inc="vE")
            return P.cnt["vE"]

        for g in range(4):
            sl = slice(4 * g, 4 * g + 4)
            for j in range(4 * g, 4 * g + 4):
                ps = pt[j % 2]
                ckpt(f"wp{j}")
                if j >= 2:
                    P.wait("tensor", "sE", proj_evac[j - 2])
                else:
                    P.wait("tensor", "sE", kb["sE"])
                if j >= 7:
                    P.wait("tensor", "vE", xp_free[j - 7])
                for s in range(DCH):
                    P.wait("tensor", "vE", qk_ev[s])
                    P.wait("tensor", "vE", m_ev[s])
                    for h in range(2):
                        mm(ps[:, 512 * h:512 * (h + 1)],
                           QK.ap()[:, s, 128 * j:128 * (j + 1)],
                           M_sb.ap()[:, s, 512 * h:512 * (h + 1)],
                           start=(s == 0), stop=False)
                P.wait("tensor", "vE", rows_vE)
                for h in range(2):
                    mm(ps[:, 512 * h:512 * (h + 1)], ONE1[0:1, :],
                       ROWP.ap()[0:1, 512 * h:512 * (h + 1)], start=False,
                       stop=True, inc=(h == 1))
                P.wait("scalar", "tE", P.cnt["tE"])
                P.run("scalar", lambda e, ps=ps, j=j: e.activation(
                    out=XP.ap()[:, j % 7, :], in_=ps[:, :], func=AF.Copy,
                    accum_out=LSUM2[:, j:j + 1]), inc="sE")
                proj_evac[j] = P.cnt["sE"]
                P.run("scalar", lambda e, j=j: e.activation(
                    out=JK1[:, :], in_=XP.ap()[:, j % 7, :], func=AF.Square,
                    accum_out=LSQ2[:, j:j + 1]), inc="sE")
            group_prep(LMU2, LR2, LSUM2, LSQ2, sl, P.cnt["sE"])
            for j in range(4 * g, 4 * g + 4):
                xpj = XP.ap()[:, j % 7, :]
                P.run("vector", lambda e, xpj=xpj, j=j: e.tensor_scalar(
                    out=xpj, in0=xpj, scalar1=LMU2[:, j:j + 1],
                    scalar2=LR2[:, j:j + 1], op0=ALU.subtract, op1=ALU.mult))
                P.wait("vector", f"dXS{j % 2}", P.cnt[f"dXS{j % 2}"])
                P.run("vector", lambda e, xpj=xpj, j=j: e.scalar_tensor_tensor(
                    out=xpj, in0=xpj, scalar=1.0, in1=XS[:, j % 2, :], op0=ALU.mult,
                    op1=ALU.add, accum_out=LSUM3[:, j:j + 1]), inc="vE")
                xprime_ev[j] = P.cnt["vE"]
                if j + 2 < NCH:
                    load_xs(j + 2)
                P.wait("scalar", "vE", xprime_ev[j])
                P.run("scalar", lambda e, xpj=xpj, j=j: e.activation(
                    out=JK2[:, :], in_=xpj, func=AF.Square,
                    accum_out=LSQ3[:, j:j + 1]), inc="sE")
            group_prep(LMU3, LR3, LSUM3, LSQ3, sl, P.cnt["sE"])
            if g >= 1:
                P.wait("vector", "dT", tr2_ev[g - 1])
            for j in range(4 * g, 4 * g + 4):
                xpj = XP.ap()[:, j % 7, :]
                P.run("vector", lambda e, xpj=xpj, j=j: e.tensor_scalar(
                    out=RING[:, j % 4, :], in0=xpj, scalar1=LMU3[:, j:j + 1],
                    scalar2=LR3[:, j:j + 1], op0=ALU.subtract, op1=ALU.mult),
                    inc="vE")
            ln2_vE = P.cnt["vE"]
            for jj in range(4 * g, 4 * g + 4):
                P.wait("sync", "vE", ln2_vE)
                for cl in DMA_SEMS:
                    P.wait("sync", cl, P.cnt[cl])
                P.dma("sync", LNT.ap()[:, :, jj, :], RING[:, jj % 4, :], "dT",
                      transpose=True)
                P.wait("sync", "dT", P.cnt["dT"])
            tr2_ev[g] = P.cnt["dT"]
            for jj in range(4 * g, 4 * g + 4):
                psf = pt[2 + (jj % 2)]
                P.wait("tensor", "dT", tr2_ev[g])
                P.wait("tensor", "vE", weff_vE)
                P.wait("tensor", "vE", m_ev[7])
                if jj >= 2:
                    P.wait("tensor", "sE", ffn_evac[jj - 2])
                for dc in range(DCH):
                    for h in range(2):
                        mm(psf[:, 512 * h:512 * (h + 1)], LNT.ap()[:, dc, jj, :],
                           WA[:, dc, 512 * h:512 * (h + 1)],
                           start=(dc == 0), stop=False)
                for h in range(2):
                    mm(psf[:, 512 * h:512 * (h + 1)], ONE1[0:1, :],
                       ROWB.ap()[0:1, 512 * h:512 * (h + 1)], start=False,
                       stop=True, inc=(h == 1))
                P.wait("scalar", "tE", P.cnt["tE"])
                if jj >= 4:
                    P.wait("scalar", "vE", ffn_done_v[jj - 4])
                    P.wait("scalar", f"dSt{jj % 4}", P.cnt[f"dSt{jj % 4}"])
                P.run("scalar", lambda e, psf=psf, jj=jj: e.activation(
                    out=FROLL.ap()[:, jj % 4, :], in_=psf[:, :], func=AF.Copy,
                    accum_out=LSUM4[:, jj:jj + 1]), inc="sE")
                ffn_evac[jj] = P.cnt["sE"]
                P.run("scalar", lambda e, jj=jj: e.activation(
                    out=JK1[:, :], in_=FROLL.ap()[:, jj % 4, :], func=AF.Square,
                    accum_out=LSQ4[:, jj:jj + 1]), inc="sE")
            group_prep(LMU4, LR4, LSUM4, LSQ4, sl, P.cnt["sE"])
            for jj in range(4 * g, 4 * g + 4):
                fr = FROLL.ap()[:, jj % 4, :]
                P.wait("vector", "sE", ffn_evac[jj])
                P.run("vector", lambda e, fr=fr, jj=jj: e.tensor_scalar(
                    out=fr, in0=fr, scalar1=LMU4[:, jj:jj + 1],
                    scalar2=LR4[:, jj:jj + 1], op0=ALU.subtract, op1=ALU.mult))
                P.run("vector", lambda e, fr=fr, jj=jj: e.tensor_tensor(
                    out=fr, in0=fr, in1=XP.ap()[:, jj % 7, :], op=ALU.add),
                    inc="vE")
                xp_free[jj] = P.cnt["vE"]
                ffn_done_v[jj] = P.cnt["vE"]
                P.wait("sync", "vE", xp_free[jj])
                P.dma("sync", y_out[128 * jj:128 * (jj + 1), :], fr,
                      f"dSt{jj % 4}")

    except _StopBuild:
        for s in DMA_SEMS + ("vE", "sE", "tE", "cc"):
            P.wait("sync", s, P.cnt[s])
        for j in range(NCH):
            P.dma("sync", y_out[128 * j:128 * (j + 1), :], WS[:, 0, :])
    for cl in DMA_SEMS:
        P.wait("sync", cl, P.cnt[cl])
    P.wait("gpsimd", "dSt3", P.cnt["dSt3"])

    with ctx, nc.Block() as block:
        for eng_name in ENGS:
            fns = P.ops[eng_name]

            def body(e, fns=fns):
                for f in fns:
                    f(e)
            getattr(block, eng_name)(body)

    return nc, tap_done


def kernel(**inputs):
    x = np.ascontiguousarray(np.asarray(inputs["L0_attn_x"], np.float32).reshape(16384, D))

    def T(a):
        return np.ascontiguousarray(np.asarray(a, np.float32).T)

    def chunkvec(v):
        return np.ascontiguousarray(np.asarray(v, np.float32).reshape(8, 128).T)

    common = {
        "wqT": T(inputs["L0_attn_Wq"]),
        "wkT": T(inputs["L0_attn_Wk"]),
        "wvT": T(inputs["L0_attn_Wv"]),
        "wpT": T(inputs["L0_attn_Wp"]),
        "wT": T(inputs["L0_ffn_W"]),
        "gq": chunkvec(inputs["L0_attn_gq"]),
        "bq": chunkvec(inputs["L0_attn_bq"]),
        "gk": chunkvec(inputs["L0_attn_gk"]),
        "bk": chunkvec(inputs["L0_attn_bk"]),
        "gv": np.ascontiguousarray(np.asarray(inputs["L0_attn_gv"], np.float32).reshape(1, D)),
        "bv": np.ascontiguousarray(np.asarray(inputs["L0_attn_bv"], np.float32).reshape(1, D)),
        "bp": np.ascontiguousarray(np.asarray(inputs["L0_attn_bp"], np.float32).reshape(1, D)),
        "bb": np.ascontiguousarray(np.asarray(inputs["L0_ffn_b"], np.float32).reshape(1, D)),
        "ldT": T(inputs["L0_ffn_lora_down"]),
        "luT": T(inputs["L0_ffn_lora_up"]),
        "dax": np.ascontiguousarray(np.asarray(inputs["L0_ffn_down_aux"], np.float32)),
        "uaxT": T(inputs["L0_ffn_up_aux"]),
    }
    nc, _ = build()
    in_maps = []
    for c in range(NCORE):
        m = dict(common)
        m["x_in"] = np.ascontiguousarray(x[TOK * c:TOK * (c + 1)])
        in_maps.append(m)
    res = run_bass_kernel_spmd(nc, in_maps, core_ids=list(range(NCORE)))
    out = np.concatenate([np.asarray(res.results[c]["y"]) for c in range(NCORE)],
                         axis=0)
    return out.reshape(2, 8192, D).astype(np.float32)


# version 08
# speedup vs baseline: 1.0237x; 1.0237x over previous
import numpy as np
import contextlib
import concourse.bass as bass
import concourse.mybir as mybir
from concourse.bass_utils import run_bass_kernel_spmd
from concourse.replica_groups import maybe_share_collective_output_space

F32 = mybir.dt.float32
BF16 = mybir.dt.bfloat16
FP16 = mybir.dt.float16
AF = mybir.ActivationFunctionType
ALU = mybir.AluOpType
AX = mybir.AxisListType

NCORE = 8
TOK = 2048
NCH = 16
D = 1024
DCH = 8
EPS = 1e-5
C_BN = 1.0 / 16384.0
C_LN = 1.0 / 1024.0

ENGS = ("sync", "scalar", "vector", "tensor", "gpsimd")


class Prog:

    def __init__(self, sems):
        self.ops = {e: [] for e in ENGS}
        self.sems = sems
        self.cnt = {n: 0 for n in sems}
        self.last_wait = {}

    def wait(self, eng, sem, val):
        assert val <= self.cnt[sem], f"wait {sem}>={val} but only {self.cnt[sem]}"
        key = (eng, sem)
        if self.last_wait.get(key, -1) >= val:
            return
        self.last_wait[key] = val
        h = self.sems[sem]
        self.ops[eng].append(lambda e, h=h, v=val: e.wait_ge(h, v))

    AUTO = {"vector": "vE", "scalar": "sE"}

    def run(self, eng, fn, inc=None, n=1):
        own = self.AUTO.get(eng)
        if own is not None:
            self.wait(eng, own, self.cnt[own])
            h = self.sems[own]
            self.cnt[own] += 1
            self.ops[eng].append(lambda e, f=fn, h=h: f(e).then_inc(h, 1))
            return self.cnt[own]
        if inc is None:
            self.ops[eng].append(fn)
            return None
        h = self.sems[inc]
        self.cnt[inc] += n
        self.ops[eng].append(lambda e, f=fn, h=h, n=n: f(e).then_inc(h, n))
        return self.cnt[inc]

    def dma(self, eng, out, in_, sem="dS", **kw):
        return self.run(eng, lambda e: e.dma_start(out=out, in_=in_, **kw),
                        inc=sem, n=16)


class _StopBuild(Exception):
    pass


def build(taps=None, stop=None):
    nc = bass.Bass("TRN2", target_bir_lowering=False)
    tap_list = list(taps or [])
    tap_done = []

    x_in = nc.dram_tensor("x_in", [TOK, D], F32, kind="ExternalInput").ap()
    wqT = nc.dram_tensor("wqT", [D, D], F32, kind="ExternalInput").ap()
    wkT = nc.dram_tensor("wkT", [D, D], F32, kind="ExternalInput").ap()
    wvT = nc.dram_tensor("wvT", [D, D], F32, kind="ExternalInput").ap()
    wpT = nc.dram_tensor("wpT", [D, D], F32, kind="ExternalInput").ap()
    wT = nc.dram_tensor("wT", [D, D], F32, kind="ExternalInput").ap()
    gq_d = nc.dram_tensor("gq", [128, 8], F32, kind="ExternalInput").ap()
    bq_d = nc.dram_tensor("bq", [128, 8], F32, kind="ExternalInput").ap()
    gk_d = nc.dram_tensor("gk", [128, 8], F32, kind="ExternalInput").ap()
    bk_d = nc.dram_tensor("bk", [128, 8], F32, kind="ExternalInput").ap()
    gv_d = nc.dram_tensor("gv", [1, D], F32, kind="ExternalInput").ap()
    bv_d = nc.dram_tensor("bv", [1, D], F32, kind="ExternalInput").ap()
    bp_d = nc.dram_tensor("bp", [1, D], F32, kind="ExternalInput").ap()
    bb_d = nc.dram_tensor("bb", [1, D], F32, kind="ExternalInput").ap()
    ldT_d = nc.dram_tensor("ldT", [100, 4], F32, kind="ExternalInput").ap()
    luT_d = nc.dram_tensor("luT", [4, 50], F32, kind="ExternalInput").ap()
    dax_d = nc.dram_tensor("dax", [100, D], F32, kind="ExternalInput").ap()
    uaxT_d = nc.dram_tensor("uaxT", [50, D], F32, kind="ExternalInput").ap()
    y_out = nc.dram_tensor("y", [TOK, D], F32, kind="ExternalOutput").ap()

    rg8 = [list(range(8))]
    rg4 = [[0, 1, 2, 3], [4, 5, 6, 7]]
    sp8 = maybe_share_collective_output_space("AllReduce", rg8)
    sp4 = maybe_share_collective_output_space("AllReduce", rg4)
    arq_i = nc.dram_tensor("arq_i", [128, 16], F32)
    arq_o = nc.dram_tensor("arq_o", [128, 16], F32, addr_space=sp8)
    ark_i = nc.dram_tensor("ark_i", [128, 16], F32)
    ark_o = nc.dram_tensor("ark_o", [128, 16], F32, addr_space=sp8)
    arv_i = nc.dram_tensor("arv_i", [2, D], F32)
    arv_o = nc.dram_tensor("arv_o", [2, D], F32, addr_space=sp8)
    arw_i = nc.dram_tensor("arw_i", [128, D], F32)
    arw_o = nc.dram_tensor("arw_o", [128, D], F32, addr_space=sp4)

    A0 = (nc.sbuf_base + 31) // 32 * 32
    NEED = 0x33840
    assert A0 + NEED <= nc.sbuf_top, (A0, NEED, nc.sbuf_top)
    nc.alloc_sbuf_tensor("arena", [128, NEED + 64], mybir.dt.uint8)

    def at(name, shape, dtype, off):
        return nc.alloc_sbuf_tensor_at(name, shape, dtype, offset=A0 + off)

    LNT = at("LNT", [128, 8, 16, 128], FP16, 0x00000)
    RING = at("RING", [128, 4, D], FP16, 0x08000)
    SREP = at("SREP", [128, D], FP16, 0x08000)
    BREP = at("BREP", [128, D], FP16, 0x08800)
    HQ = at("HQ", [128, 8, TOK], FP16, 0x0A000)
    QK = at("QK", [128, 8, TOK], FP16, 0x0A000)
    HK = at("HK", [128, 8, TOK], FP16, 0x12000)
    KB = at("KB", [128, 8, TOK], FP16, 0x12000)
    M_sb = at("M_sb", [128, 8, D], FP16, 0x12000)
    FROLL = at("FROLL", [128, 4, D], F32, 0x16000)
    HV = at("HV", [128, NCH, D], FP16, 0x1A000)
    VTOK = at("VTOK", [128, NCH, D], FP16, 0x1A000)
    XS = at("XS", [128, 2, D], F32, 0x1A000)
    VVF = at("VVF", [128, 8, 128], F32, 0x1A000)
    VVB = at("VVB", [128, 8, 128], FP16, 0x1C000)
    DAXB = at("DAXB", [100, D], FP16, 0x1C800)
    UAXB = at("UAXB", [50, D], FP16, 0x1D000)
    AB_ = at("AB_", [4, D], FP16, 0x1D800)
    BB_ = at("BB_", [50, D], FP16, 0x1E000)
    XP = at("XP", [128, 6, D], FP16, 0x22000)
    XI = at("XI", [128, 3, D], F32, 0x22000)
    WA = at("WA", [128, 8, D], FP16, 0x25000)
    WB = at("WB", [128, 8, D], FP16, 0x29000)
    WS = at("WS", [128, 2, D], F32, 0x2D000)
    JK1 = at("JK1", [128, D], FP16, 0x2F000)
    JK2 = at("JK2", [128, D], FP16, 0x2F800)
    VS0 = at("VS0", [1, D], F32, 0x30000)
    VS1 = at("VS1", [1, D], F32, 0x29000)
    VTMP = at("VTMP", [1, D], F32, 0x22000)
    VGAM = at("VGAM", [1, D], F32, 0x23000)
    VBET = at("VBET", [1, D], F32, 0x24000)
    SVB0 = at("SVB0", [1, D], FP16, 0x31000)
    SVB1 = at("SVB1", [1, D], FP16, 0x31800)
    ROWP = at("ROWP", [1, D], FP16, 0x32000)
    ROWB = at("ROWB", [1, D], FP16, 0x32800)
    GQ = at("GQ", [128, 8], F32, 0x33000)
    BQ = at("BQ", [128, 8], F32, 0x33020)
    GK = at("GK", [128, 8], F32, 0x33040)
    BK = at("BK", [128, 8], F32, 0x33060)
    QST = at("QST", [128, 16], F32, 0x33080)
    KST = at("KST", [128, 16], F32, 0x330C0)
    ACC4 = at("ACC4", [128, 8, 4], F32, 0x33100)
    QSV = at("QSV", [128, 8], F32, 0x33180)
    QBV = at("QBV", [128, 8], F32, 0x331A0)
    KSV = at("KSV", [128, 8], F32, 0x331C0)
    KBV = at("KBV", [128, 8], F32, 0x331E0)
    LSUM = at("LSUM", [128, 16], F32, 0x33200)
    LSQ = at("LSQ", [128, 16], F32, 0x33240)
    LMU = at("LMU", [128, 16], F32, 0x33280)
    LR = at("LR", [128, 16], F32, 0x332C0)
    T8A = at("T8A", [128, 8], F32, 0x33300)
    T8B = at("T8B", [128, 8], F32, 0x33320)
    ONE1 = at("ONE1", [1, 128], FP16, 0x33340)
    ONEC = at("ONEC", [128, 1], FP16, 0x33440)
    LDTB = at("LDTB", [100, 4], FP16, 0x33460)
    LUTB = at("LUTB", [4, 50], FP16, 0x33480)
    LSUM2 = at("LSUM2", [128, 16], F32, 0x33500)
    LSQ2 = at("LSQ2", [128, 16], F32, 0x33540)
    LMU2 = at("LMU2", [128, 16], F32, 0x33580)
    LR2 = at("LR2", [128, 16], F32, 0x335C0)
    LSUM3 = at("LSUM3", [128, 16], F32, 0x33600)
    LSQ3 = at("LSQ3", [128, 16], F32, 0x33640)
    LMU3 = at("LMU3", [128, 16], F32, 0x33680)
    LR3 = at("LR3", [128, 16], F32, 0x336C0)
    LSUM4 = at("LSUM4", [128, 16], F32, 0x33700)
    LSQ4 = at("LSQ4", [128, 16], F32, 0x33740)
    LMU4 = at("LMU4", [128, 16], F32, 0x33780)
    LR4 = at("LR4", [128, 16], F32, 0x337C0)

    ctx = contextlib.ExitStack()
    pt = [ctx.enter_context(nc.psum_tensor(f"pt{i}", [128, D], F32)) for i in range(4)]
    DMA_SEMS = ("dS", "dG", "dT", "dW0", "dW1", "dX0", "dX1", "dX2",
                "dXS0", "dXS1", "dSt0", "dSt1", "dSt2", "dSt3", "dR")
    sems = {n: ctx.enter_context(nc.semaphore(n)) for n in
            DMA_SEMS + ("cc", "tE", "sE", "vE")}
    P = Prog(sems)

    def tap(name, ap, shape, dtype):
        if name not in tap_list:
            return
        for s in DMA_SEMS + ("vE", "sE", "tE"):
            P.wait("sync", s, P.cnt[s])
        t = nc.dram_tensor("tap_" + name, shape, dtype, kind="ExternalOutput").ap()
        P.dma("sync", t[...], ap)
        tap_done.append(name)

    def ckpt(name):
        if stop == name:
            raise _StopBuild

    def mm(psum_ap, lhsT, rhs, start, stop, inc=False):
        return P.run("tensor",
                     lambda e, o=psum_ap, l=lhsT, r=rhs, s1=start, s2=stop:
                     e.matmul(out=o, lhsT=l, rhs=r, start=s1, stop=s2),
                     inc=("tE" if inc else None))

    def lnT_rhs(dc, nt):
        return LNT.ap()[:, dc, 4 * nt:4 * nt + 4, :].rearrange("p a b -> p (a b)")

    try:
        P.run("vector", lambda e: e.memset(ONE1[:, :], 1.0))
        P.run("vector", lambda e: e.memset(ONEC[:, :], 1.0), inc="vE")

        for dst, src in ((GQ, gq_d), (BQ, bq_d), (GK, gk_d), (BK, bk_d)):
            P.dma("sync", dst[:, :], src[:, :], "dW0")
        P.dma("sync", WS[0:1, 0, :], bp_d[:, :], "dW0")
        P.dma("sync", WS[0:1, 1, :], bb_d[:, :], "dW0")
        P.wait("vector", "dW0", P.cnt["dW0"])
        P.run("vector", lambda e: e.tensor_copy(out=ROWP.ap()[0:1, :], in_=WS[0:1, 0, :]))
        P.run("vector", lambda e: e.tensor_copy(out=ROWB.ap()[0:1, :], in_=WS[0:1, 1, :]),
              inc="vE")
        rows_vE = P.cnt["vE"]

        ws_free = {0: rows_vE, 1: rows_vE}

        def load_weight(dst, src_dram, scale=None, pre_tE=0):
            for c in range(DCH):
                slot = c % 2
                P.wait("sync", "vE", ws_free[slot])
                P.dma("sync", WS[:, slot, :], src_dram[128 * c:128 * (c + 1), :],
                      f"dW{slot}")
                P.wait("vector", f"dW{slot}", P.cnt[f"dW{slot}"])
                if pre_tE:
                    P.wait("vector", "tE", pre_tE)
                if scale is None:
                    P.run("vector",
                          lambda e, s=slot, c=c: e.tensor_copy(out=dst[:, c, :],
                                                               in_=WS[:, s, :]), inc="vE")
                else:
                    P.run("vector",
                          lambda e, s=slot, c=c: e.tensor_scalar(
                              out=dst[:, c, :], in0=WS[:, s, :], scalar1=scale,
                              scalar2=None, op0=ALU.mult), inc="vE")
                ws_free[slot] = P.cnt["vE"]
            return P.cnt["vE"]

        wq_vE = load_weight(WB, wqT)

        xload = {}
        app_ev = {}
        trj_ev = {}
        tr_ev = {}
        for j in range(3):
            P.dma("sync", XI[:, j, :], x_in[128 * j:128 * (j + 1), :], f"dX{j}")
        for j in range(NCH):
            P.wait("vector", f"dX{j % 3}", P.cnt[f"dX{j % 3}"])
            xc = XI[:, j % 3, :]
            P.run("vector", lambda e, xc=xc, j=j: e.tensor_scalar(
                out=JK1[:, :], in0=xc, scalar1=1.0, scalar2=0.0, op0=ALU.mult,
                op1=ALU.add, accum_out=LSUM[:, j:j + 1]))
            P.run("vector", lambda e, xc=xc, j=j: e.scalar_tensor_tensor(
                out=JK1[:, :], in0=xc, scalar=1.0, in1=xc, op0=ALU.mult,
                op1=ALU.mult, accum_out=LSQ[:, j:j + 1]), inc="vE")
            P.run("vector", lambda e, j=j: e.tensor_scalar(
                out=LMU[:, j:j + 1], in0=LSUM[:, j:j + 1], scalar1=C_LN, scalar2=None,
                op0=ALU.mult))
            P.run("vector", lambda e, j=j: e.tensor_scalar(
                out=LR[:, j:j + 1], in0=LSQ[:, j:j + 1], scalar1=C_LN, scalar2=None,
                op0=ALU.mult))
            P.run("vector", lambda e, j=j: e.tensor_tensor(
                out=T8A[:, 0:1], in0=LMU[:, j:j + 1], in1=LMU[:, j:j + 1], op=ALU.mult))
            P.run("vector", lambda e, j=j: e.tensor_tensor(
                out=LR[:, j:j + 1], in0=LR[:, j:j + 1], in1=T8A[:, 0:1],
                op=ALU.subtract))
            P.run("vector", lambda e, j=j: e.tensor_scalar(
                out=LR[:, j:j + 1], in0=LR[:, j:j + 1], scalar1=EPS, scalar2=None,
                op0=ALU.add), inc="vE")
            P.wait("scalar", "vE", P.cnt["vE"])
            P.run("scalar", lambda e, j=j: e.activation(
                out=LR[:, j:j + 1], in_=LR[:, j:j + 1], func=AF.Sqrt), inc="sE")
            P.wait("vector", "sE", P.cnt["sE"])
            P.run("vector", lambda e, j=j: e.reciprocal(
                out=LR[:, j:j + 1], in_=LR[:, j:j + 1]))
            if j >= 4:
                P.wait("vector", "dT", trj_ev[j - 4])
            P.run("vector", lambda e, xc=xc, j=j: e.tensor_scalar(
                out=RING[:, j % 4, :], in0=xc, scalar1=LMU[:, j:j + 1],
                scalar2=LR[:, j:j + 1], op0=ALU.subtract, op1=ALU.mult), inc="vE")
            app_ev[j] = P.cnt["vE"]
            P.wait("sync", "vE", app_ev[j])
            for cl in DMA_SEMS:
                P.wait("sync", cl, P.cnt[cl])
            P.dma("sync", LNT.ap()[:, :, j, :], RING[:, j % 4, :], "dT", transpose=True)
            trj_ev[j] = P.cnt["dT"]
            P.wait("sync", "dT", P.cnt["dT"])
            if j % 4 == 3:
                tr_ev[j // 4] = P.cnt["dT"]
            if j + 3 < NCH:
                P.wait("sync", "vE", app_ev[j])
                P.dma("sync", XI[:, (j + 3) % 3, :],
                      x_in[128 * (j + 3):128 * (j + 4), :], f"dX{j % 3}")

        tap("ln1T", LNT.ap()[:, :, :, :], [128, 8, 16, 128], FP16)
        ckpt("ln1")

        def branch_fm(w_sb, w_vE, h_sb, stat_sb, pre_sE):
            regions = [(pt[0][:, 0:512], 0), (pt[0][:, 512:1024], 1),
                       (pt[1][:, 0:512], 2), (pt[1][:, 512:1024], 3)]
            evac_done = {}
            k = 0
            for ec in range(DCH):
                for nt in range(4):
                    ps, ridx = regions[k % 4]
                    if k >= 4:
                        P.wait("tensor", "sE", evac_done[ridx])
                    elif pre_sE:
                        P.wait("tensor", "sE", pre_sE)
                    P.wait("tensor", "vE", w_vE)
                    P.wait("tensor", "dT", tr_ev[nt])
                    for dc in range(DCH):
                        mm(ps, w_sb[:, dc, 128 * ec:128 * (ec + 1)], lnT_rhs(dc, nt),
                           start=(dc == 0), stop=(dc == DCH - 1), inc=(dc == DCH - 1))
                    mm_ev = P.cnt["tE"]
                    P.wait("scalar", "tE", mm_ev)
                    P.run("scalar", lambda e, ps=ps, ec=ec, nt=nt: e.activation(
                        out=h_sb.ap()[:, ec, 512 * nt:512 * (nt + 1)], in_=ps,
                        func=AF.Copy, accum_out=ACC4[:, ec, nt:nt + 1]), inc="sE")
                    evac_done[ridx] = P.cnt["sE"]
                    k += 1
                P.wait("vector", "sE", P.cnt["sE"])
                P.run("vector", lambda e, ec=ec: e.scalar_tensor_tensor(
                    out=JK2[:, :], in0=h_sb.ap()[:, ec, 0:1024], scalar=1.0,
                    in1=h_sb.ap()[:, ec, 0:1024], op0=ALU.mult, op1=ALU.mult,
                    accum_out=T8A[:, ec:ec + 1]))
                P.run("vector", lambda e, ec=ec: e.scalar_tensor_tensor(
                    out=JK2[:, :], in0=h_sb.ap()[:, ec, 1024:2048], scalar=1.0,
                    in1=h_sb.ap()[:, ec, 1024:2048], op0=ALU.mult, op1=ALU.mult,
                    accum_out=T8B[:, ec:ec + 1]), inc="vE")
            P.run("vector", lambda e: e.tensor_reduce(
                out=stat_sb[:, 0:8], in_=ACC4[:, :, :], axis=AX.X, op=ALU.add))
            P.run("vector", lambda e: e.tensor_tensor(
                out=stat_sb[:, 8:16], in0=T8A[:, :], in1=T8B[:, :], op=ALU.add), inc="vE")
            return {"vE": P.cnt["vE"], "tE": P.cnt["tE"], "sE": P.cnt["sE"]}

        def allreduce(in_ap, out_ap, sb_ap, ready, rg):
            P.wait("gpsimd", ready[0], ready[1])
            P.wait("gpsimd", "dT", P.cnt["dT"])
            P.dma("gpsimd", in_ap, sb_ap, "dG")
            P.wait("gpsimd", "dG", P.cnt["dG"])
            P.run("gpsimd", lambda e, i=in_ap, o=out_ap, rg=rg: e.collective_compute(
                "AllReduce", ALU.add, replica_groups=rg, ins=[i], outs=[o]), inc="cc")
            P.wait("gpsimd", "cc", P.cnt["cc"])
            P.dma("gpsimd", sb_ap, out_ap, "dG")
            P.wait("gpsimd", "dG", P.cnt["dG"])
            return ("dG", P.cnt["dG"])

        def bn_prep(stat, g_sb, b_sb, sv, bv, ar_ev):
            P.wait("vector", ar_ev[0], ar_ev[1])
            P.run("vector", lambda e: e.tensor_scalar(
                out=T8A[:, :], in0=stat[:, 0:8], scalar1=C_BN, scalar2=None, op0=ALU.mult))
            P.run("vector", lambda e: e.tensor_scalar(
                out=T8B[:, :], in0=stat[:, 8:16], scalar1=C_BN, scalar2=None, op0=ALU.mult))
            P.run("vector", lambda e: e.tensor_tensor(
                out=sv, in0=T8A[:, :], in1=T8A[:, :], op=ALU.mult))
            P.run("vector", lambda e: e.tensor_tensor(
                out=T8B[:, :], in0=T8B[:, :], in1=sv, op=ALU.subtract))
            P.run("vector", lambda e: e.tensor_scalar(
                out=sv, in0=T8B[:, :], scalar1=EPS, scalar2=None, op0=ALU.add), inc="vE")
            P.wait("scalar", "vE", P.cnt["vE"])
            P.run("scalar", lambda e: e.activation(out=sv, in_=sv, func=AF.Sqrt), inc="sE")
            P.wait("vector", "sE", P.cnt["sE"])
            P.run("vector", lambda e: e.reciprocal(out=sv, in_=sv))
            P.run("vector", lambda e: e.tensor_tensor(
                out=sv, in0=sv, in1=g_sb[:, :], op=ALU.mult))
            P.run("vector", lambda e: e.tensor_scalar(
                out=sv, in0=sv, scalar1=4.0, scalar2=None, op0=ALU.mult))
            P.run("vector", lambda e: e.tensor_tensor(
                out=bv, in0=T8A[:, :], in1=sv, op=ALU.mult))
            P.run("vector", lambda e: e.tensor_scalar(
                out=T8B[:, :], in0=b_sb[:, :], scalar1=4.0, scalar2=None, op0=ALU.mult))
            P.run("vector", lambda e: e.tensor_tensor(
                out=T8B[:, :], in0=T8B[:, :], in1=bv, op=ALU.subtract))
            P.run("vector", lambda e: e.tensor_scalar(
                out=bv, in0=T8B[:, :], scalar1=-4.0, scalar2=None, op0=ALU.add), inc="vE")
            return P.cnt["vE"]

        qb = branch_fm(WB, wq_vE, HQ, QST[:, :], pre_sE=0)
        tap("hq", HQ.ap()[:, :, :], [128, 8, TOK], FP16)
        tap("qstat_loc", QST[:, :], [128, 16], F32)
        ckpt("q_mm")
        arq_ev = allreduce(arq_i.ap(), arq_o.ap(), QST[:, :], ("vE", qb["vE"]), rg8)
        if "qstat_g" in tap_list:
            P.wait("sync", arq_ev[0], arq_ev[1])
        tap("qstat_g", QST[:, :], [128, 16], F32)
        ckpt("arq")

        wv_vE = load_weight(WA, wvT)

        vprod = [pt[0], pt[1]]
        evac_done_v = {}
        sqmm_ev = {}
        for j in range(NCH):
            ps = vprod[j % 2]
            if j >= 2:
                P.wait("tensor", "sE", evac_done_v[j % 2])
            else:
                P.wait("tensor", "sE", qb["sE"])
            P.wait("tensor", "vE", wv_vE)
            P.wait("tensor", "dT", tr_ev[j // 4])
            for dc in range(DCH):
                for h in range(2):
                    mm(ps[:, 512 * h:512 * (h + 1)], LNT.ap()[:, dc, j, :],
                       WA[:, dc, 512 * h:512 * (h + 1)],
                       start=(dc == 0), stop=(dc == DCH - 1),
                       inc=(dc == DCH - 1 and h == 1))
            mm_ev = P.cnt["tE"]
            P.wait("scalar", "tE", mm_ev)
            P.run("scalar", lambda e, ps=ps, j=j: e.activation(
                out=HV.ap()[:, j, :], in_=ps[:, :], func=AF.Copy), inc="sE")
            evac_done_v[j % 2] = P.cnt["sE"]
            hv_ev = P.cnt["sE"]
            jk = JK1 if j % 2 == 0 else JK2
            if j >= 2:
                P.wait("vector", "tE", sqmm_ev[j - 2])
            P.wait("vector", "sE", hv_ev)
            P.run("vector", lambda e, j=j, jk=jk: e.scalar_tensor_tensor(
                out=jk[:, :], in0=HV.ap()[:, j, :], scalar=1.0,
                in1=HV.ap()[:, j, :], op0=ALU.mult, op1=ALU.mult), inc="vE")
            sq_ev = P.cnt["vE"]
            P.wait("tensor", "sE", hv_ev)
            for h in range(2):
                mm(pt[2][0:1, 512 * h:512 * (h + 1)], ONEC[:, 0:1],
                   HV.ap()[:, j, 512 * h:512 * (h + 1)],
                   start=(j == 0), stop=(j == NCH - 1))
            P.wait("tensor", "vE", sq_ev)
            for h in range(2):
                mm(pt[3][0:1, 512 * h:512 * (h + 1)], ONEC[:, 0:1],
                   jk[:, 512 * h:512 * (h + 1)],
                   start=(j == 0), stop=(j == NCH - 1), inc=(h == 1))
            sqmm_ev[j] = P.cnt["tE"]
        v_mm_tE = P.cnt["tE"]
        P.wait("vector", "tE", v_mm_tE)
        P.run("vector", lambda e: e.tensor_copy(out=VS0.ap()[0:1, :], in_=pt[2][0:1, :]))
        P.run("vector", lambda e: e.tensor_copy(out=VS1.ap()[0:1, :], in_=pt[3][0:1, :]),
              inc="vE")
        vstat_vE = P.cnt["vE"]
        ckpt("v_mm")
        tap("hv", HV.ap()[:, :, :], [128, NCH, D], FP16)
        tap("vstat_loc", VS0.ap()[0:1, :], [1, D], F32)
        P.wait("gpsimd", "vE", vstat_vE)
        P.wait("gpsimd", "dT", P.cnt["dT"])
        P.dma("gpsimd", arv_i.ap()[0:1, :], VS0.ap()[0:1, :], "dG")
        P.dma("gpsimd", arv_i.ap()[1:2, :], VS1.ap()[0:1, :], "dG")
        P.wait("gpsimd", "dG", P.cnt["dG"])
        P.run("gpsimd", lambda e: e.collective_compute(
            "AllReduce", ALU.add, replica_groups=rg8, ins=[arv_i.ap()],
            outs=[arv_o.ap()]), inc="cc")
        P.wait("gpsimd", "cc", P.cnt["cc"])
        P.dma("gpsimd", VS0.ap()[0:1, :], arv_o.ap()[0:1, :], "dG")
        P.dma("gpsimd", VS1.ap()[0:1, :], arv_o.ap()[1:2, :], "dG")
        P.wait("gpsimd", "dG", P.cnt["dG"])
        arv_ev = ("dG", P.cnt["dG"])
        ckpt("arv")

        wk_vE = load_weight(WA, wkT, pre_tE=v_mm_tE)

        qprep_vE = bn_prep(QST[:, :], GQ, BQ, QSV[:, :], QBV[:, :], arq_ev)
        tap("qsv", QSV[:, :], [128, 8], F32)
        tap("qbv", QBV[:, :], [128, 8], F32)
        for ec in range(DCH):
            P.wait("scalar", "vE", qprep_vE)
            P.run("scalar", lambda e, ec=ec: e.activation(
                out=QK.ap()[:, ec, :], in_=HQ.ap()[:, ec, :], func=AF.Sigmoid,
                scale=QSV[:, ec:ec + 1], bias=QBV[:, ec:ec + 1]), inc="sE")
        qsig_sE = P.cnt["sE"]
        qsig_vE = P.cnt["vE"]
        tap("q", QK.ap()[:, :, :], [128, 8, TOK], FP16)
        ckpt("qsig")

        P.wait("sync", "vE", app_ev[NCH - 1])
        P.dma("sync", VGAM.ap()[0:1, :], gv_d[:, :], "dR")
        P.dma("sync", VBET.ap()[0:1, :], bv_d[:, :], "dR")
        P.wait("vector", arv_ev[0], arv_ev[1])
        P.wait("vector", "dR", P.cnt["dR"])
        vs0 = VS0.ap()[0:1, :]
        vs1 = VS1.ap()[0:1, :]
        vt = VTMP.ap()[0:1, :]
        P.run("vector", lambda e: e.tensor_scalar(
            out=vs0, in0=vs0, scalar1=C_BN, scalar2=None, op0=ALU.mult))
        P.run("vector", lambda e: e.tensor_scalar(
            out=vs1, in0=vs1, scalar1=C_BN, scalar2=None, op0=ALU.mult))
        P.run("vector", lambda e: e.tensor_tensor(
            out=vt, in0=vs0, in1=vs0, op=ALU.mult))
        P.run("vector", lambda e: e.tensor_tensor(
            out=vs1, in0=vs1, in1=vt, op=ALU.subtract))
        P.run("vector", lambda e: e.tensor_scalar(
            out=vs1, in0=vs1, scalar1=EPS, scalar2=None, op0=ALU.add), inc="vE")
        P.wait("scalar", "vE", P.cnt["vE"])
        P.run("scalar", lambda e: e.activation(out=vs1, in_=vs1, func=AF.Sqrt), inc="sE")
        P.wait("vector", "sE", P.cnt["sE"])
        P.run("vector", lambda e: e.reciprocal(out=vs1, in_=vs1))
        P.run("vector", lambda e: e.tensor_tensor(
            out=vs1, in0=vs1, in1=VGAM.ap()[0:1, :], op=ALU.mult))
        P.run("vector", lambda e: e.tensor_scalar(
            out=vs1, in0=vs1, scalar1=4.0, scalar2=None, op0=ALU.mult))
        P.run("vector", lambda e: e.tensor_tensor(
            out=vt, in0=vs0, in1=vs1, op=ALU.mult))
        P.run("vector", lambda e: e.tensor_scalar(
            out=VBET.ap()[0:1, :], in0=VBET.ap()[0:1, :], scalar1=4.0,
            scalar2=None, op0=ALU.mult))
        P.run("vector", lambda e: e.tensor_tensor(
            out=vt, in0=vt, in1=VBET.ap()[0:1, :], op=ALU.subtract))
        P.run("vector", lambda e: e.tensor_scalar(
            out=vt, in0=vt, scalar1=-1.0, scalar2=-4.0, op0=ALU.mult,
            op1=ALU.add))
        P.run("vector", lambda e: e.tensor_copy(out=SVB0.ap()[0:1, :], in_=vs1))
        P.run("vector", lambda e: e.tensor_copy(out=SVB1.ap()[0:1, :], in_=vt),
              inc="vE")
        svb_vE = P.cnt["vE"]
        ckpt("vprep")
        P.wait("tensor", "vE", svb_vE)
        P.wait("tensor", "vE", vstat_vE)
        for h in range(2):
            mm(pt[3][:, 512 * h:512 * (h + 1)], ONE1[0:1, :],
               SVB0.ap()[0:1, 512 * h:512 * (h + 1)], start=True, stop=True,
               inc=(h == 1))
        srep_tE = P.cnt["tE"]
        P.wait("vector", "tE", srep_tE)
        P.wait("vector", "dT", tr_ev[3])
        P.run("vector", lambda e: e.tensor_copy(out=SREP[:, :], in_=pt[3][:, :]), inc="vE")
        srep_vE = P.cnt["vE"]
        P.wait("tensor", "vE", srep_vE)
        for h in range(2):
            mm(pt[3][:, 512 * h:512 * (h + 1)], ONE1[0:1, :],
               SVB1.ap()[0:1, 512 * h:512 * (h + 1)], start=True, stop=True,
               inc=(h == 1))
        brep_tE = P.cnt["tE"]
        P.wait("vector", "tE", brep_tE)
        P.run("vector", lambda e: e.tensor_copy(out=BREP[:, :], in_=pt[3][:, :]), inc="vE")
        brep_vE = P.cnt["vE"]
        ckpt("brep")

        sig_sE = {}
        for j in range(NCH):
            P.wait("vector", "vE", brep_vE)
            P.run("vector", lambda e, j=j: e.tensor_tensor(
                out=HV.ap()[:, j, :], in0=HV.ap()[:, j, :], in1=SREP[:, :],
                op=ALU.mult))
            P.run("vector", lambda e, j=j: e.tensor_tensor(
                out=HV.ap()[:, j, :], in0=HV.ap()[:, j, :], in1=BREP[:, :],
                op=ALU.add), inc="vE")
            P.wait("scalar", "vE", P.cnt["vE"])
            P.run("scalar", lambda e, j=j: e.activation(
                out=VTOK.ap()[:, j, :], in_=HV.ap()[:, j, :], func=AF.Sigmoid),
                inc="sE")
            sig_sE[j] = P.cnt["sE"]
        vsig_sE = P.cnt["sE"]
        ckpt("vsig")
        tap("v", VTOK.ap()[:, :, :], [128, NCH, D], FP16)

        P.wait("tensor", "vE", brep_vE)
        for s in range(DCH):
            for j in range(NCH):
                if s == 0:
                    P.wait("tensor", "sE", sig_sE[j])
                mm(pt[2][:, 128 * s:128 * (s + 1)],
                   VTOK.ap()[:, j, 128 * s:128 * (s + 1)],
                   VTOK.ap()[:, j, 128 * s:128 * (s + 1)],
                   start=(j == 0), stop=(j == NCH - 1), inc=(j == NCH - 1))
        vv_tE = P.cnt["tE"]
        P.wait("vector", "tE", vv_tE)
        P.run("vector", lambda e: e.tensor_copy(
            out=VVF.ap().rearrange("p a b -> p (a b)"), in_=pt[2][:, :]), inc="vE")
        vv_loc_vE = P.cnt["vE"]
        tap("vv_loc", VVF.ap()[:, :, :], [128, 8, 128], F32)
        arw_ev = allreduce(arw_i.ap(), arw_o.ap(),
                           VVF.ap().rearrange("p a b -> p (a b)"),
                           ("vE", vv_loc_vE), rg4)
        P.wait("vector", arw_ev[0], arw_ev[1])
        P.run("vector", lambda e: e.memset(VVB.ap().rearrange("p a b -> p (a b)"), 0.0))
        for s in range(DCH):
            P.run("vector", lambda e, s=s: e.tensor_copy(
                out=VVB.ap()[0:64, s, 0:64], in_=VVF.ap()[0:64, s, 0:64]))
            P.run("vector", lambda e, s=s: e.tensor_copy(
                out=VVB.ap()[64:128, s, 64:128], in_=VVF.ap()[64:128, s, 64:128]))
        P.run("vector", lambda e: e.nop(), inc="vE")
        vvb_vE = P.cnt["vE"]
        tap("vv", VVB.ap()[:, :, :], [128, 8, 128], FP16)
        ckpt("vv_loc0")

        kb = branch_fm(WA, wk_vE, HK, KST[:, :], pre_sE=evac_done_v[1])
        ckpt("k_mm")
        tap("hk", HK.ap()[:, :, :], [128, 8, TOK], FP16)
        ark_ev = allreduce(ark_i.ap(), ark_o.ap(), KST[:, :], ("vE", kb["vE"]), rg8)
        ckpt("ark")

        wp_vE = load_weight(WB, wpT, scale=0.125, pre_tE=qb["tE"])
        ckpt("wp_w")

        P.wait("sync", "vE", ws_free[0])
        P.wait("sync", "vE", ws_free[1])
        P.dma("sync", WS[0:100, 0, 0:4], ldT_d[:, :], "dW0")
        P.dma("sync", WS[0:4, 1, 0:50], luT_d[:, :], "dW1")
        P.wait("vector", "dW0", P.cnt["dW0"])
        P.wait("vector", "dW1", P.cnt["dW1"])
        P.wait("vector", "vE", ws_free[0])
        P.wait("vector", "vE", ws_free[1])
        P.run("vector", lambda e: e.tensor_copy(out=LDTB.ap()[:, :], in_=WS[0:100, 0, 0:4]))
        P.run("vector", lambda e: e.tensor_copy(out=LUTB.ap()[:, :], in_=WS[0:4, 1, 0:50]),
              inc="vE")
        ws_free[0] = ws_free[1] = P.cnt["vE"]
        for c, (dst, src, rows) in enumerate(((DAXB, dax_d, 100), (UAXB, uaxT_d, 50))):
            slot = c % 2
            P.wait("sync", "vE", ws_free[slot])
            P.dma("sync", WS[0:rows, slot, :], src[:, :], f"dW{slot}")
            P.wait("vector", f"dW{slot}", P.cnt[f"dW{slot}"])
            P.wait("vector", "tE", vv_tE)
            P.run("vector", lambda e, dst=dst, slot=slot, rows=rows: e.tensor_copy(
                out=dst.ap()[:, :], in_=WS[0:rows, slot, :]), inc="vE")
            ws_free[slot] = P.cnt["vE"]
        lor_vE = P.cnt["vE"]
        P.wait("tensor", "vE", lor_vE)
        P.wait("tensor", "vE", brep_vE)
        for h in range(2):
            mm(pt[3][0:4, 512 * h:512 * (h + 1)], LDTB.ap()[:, :],
               DAXB.ap()[:, 512 * h:512 * (h + 1)], start=True, stop=True, inc=(h == 1))
        a_tE = P.cnt["tE"]
        P.wait("vector", "tE", a_tE)
        P.run("vector", lambda e: e.tensor_copy(out=AB_.ap()[:, :], in_=pt[3][0:4, :]),
              inc="vE")
        ab_vE = P.cnt["vE"]
        P.wait("tensor", "vE", ab_vE)
        for h in range(2):
            mm(pt[3][0:50, 512 * h:512 * (h + 1)], LUTB.ap()[:, :],
               AB_.ap()[:, 512 * h:512 * (h + 1)], start=True, stop=True, inc=(h == 1))
        b_tE = P.cnt["tE"]
        P.wait("vector", "tE", b_tE)
        P.run("vector", lambda e: e.tensor_copy(out=BB_.ap()[:, :], in_=pt[3][0:50, :]),
              inc="vE")
        bb_vE = P.cnt["vE"]
        ckpt("lab")
        for c in range(DCH):
            slot = c % 2
            P.wait("sync", "vE", ws_free[slot])
            P.dma("sync", WS[:, slot, :], wT[128 * c:128 * (c + 1), :], f"dW{slot}")
            P.wait("tensor", "vE", bb_vE)
            P.wait("tensor", "vE", ws_free[(c + 1) % 2])
            for h in range(2):
                mm(pt[3][:, 512 * h:512 * (h + 1)], BB_.ap()[:, 128 * c:128 * (c + 1)],
                   UAXB.ap()[:, 512 * h:512 * (h + 1)], start=True, stop=True,
                   inc=(h == 1))
            ct_tE = P.cnt["tE"]
            P.wait("vector", "tE", ct_tE)
            P.wait("vector", "tE", kb["tE"])
            P.wait("vector", f"dW{slot}", P.cnt[f"dW{slot}"])
            P.run("vector", lambda e, c=c, slot=slot: e.tensor_tensor(
                out=WA[:, c, :], in0=WS[:, slot, :], in1=pt[3][:, :], op=ALU.add),
                inc="vE")
            ws_free[slot] = P.cnt["vE"]
        weff_vE = P.cnt["vE"]
        ckpt("weff")

        kprep_vE = bn_prep(KST[:, :], GK, BK, KSV[:, :], KBV[:, :], ark_ev)
        qk_ev = {}
        m_ev = {}
        for s in range(DCH):
            P.wait("scalar", "vE", kprep_vE)
            P.run("scalar", lambda e, s=s: e.activation(
                out=KB.ap()[:, s, :], in_=HK.ap()[:, s, :], func=AF.Sigmoid,
                scale=KSV[:, s:s + 1], bias=KBV[:, s:s + 1]), inc="sE")
            ks_sE = P.cnt["sE"]
            P.wait("vector", "sE", ks_sE)
            P.wait("vector", "sE", qsig_sE)
            P.run("vector", lambda e, s=s: e.tensor_tensor(
                out=QK.ap()[:, s, :], in0=QK.ap()[:, s, :], in1=KB.ap()[:, s, :],
                op=ALU.mult), inc="vE")
            qk_ev[s] = P.cnt["vE"]
            if s % 2 == 1:
                for t in (s - 1, s):
                    P.wait("tensor", "vE", vvb_vE)
                    P.wait("tensor", "vE", wp_vE)
                    if t >= 1:
                        P.wait("tensor", "vE", m_ev[t - 1])
                    for h in range(2):
                        mm(pt[2][:, 512 * h:512 * (h + 1)], VVB.ap()[:, t, :],
                           WB[:, t, 512 * h:512 * (h + 1)], start=True, stop=True,
                           inc=(h == 1))
                    mt = P.cnt["tE"]
                    P.wait("vector", "tE", mt)
                    P.run("vector", lambda e, t=t: e.tensor_copy(
                        out=M_sb.ap()[:, t, :], in_=pt[2][:, :]), inc="vE")
                    m_ev[t] = P.cnt["vE"]
        tap("qk", QK.ap()[:, :, :], [128, 8, TOK], FP16)
        tap("M", M_sb.ap()[:, :, :], [128, 8, D], FP16)
        ckpt("m")

        xs_load = {}

        def load_xs(j):
            if j >= 2:
                P.wait("sync", "vE", xprime_ev[j - 2])
            else:
                P.wait("sync", "vE", vvb_vE)
            P.dma("sync", XS[:, j % 2, :], x_in[128 * j:128 * (j + 1), :],
                  f"dXS{j % 2}")

        proj_evac = {}
        xprime_ev = {}
        xp_free = {}
        ffn_evac = {}
        ffn_done_v = {}
        tr2_ev = {}
        load_xs(0)
        load_xs(1)

        def group_prep(lmu, lr, lsum, lsq, sl, ready_sE):
            P.wait("vector", "sE", ready_sE)
            P.run("vector", lambda e: e.tensor_scalar(
                out=lmu[:, sl], in0=lsum[:, sl], scalar1=C_LN, scalar2=None,
                op0=ALU.mult))
            P.run("vector", lambda e: e.tensor_scalar(
                out=lr[:, sl], in0=lsq[:, sl], scalar1=C_LN, scalar2=None,
                op0=ALU.mult))
            P.run("vector", lambda e: e.tensor_tensor(
                out=T8A[:, 0:4], in0=lmu[:, sl], in1=lmu[:, sl], op=ALU.mult))
            P.run("vector", lambda e: e.tensor_tensor(
                out=lr[:, sl], in0=lr[:, sl], in1=T8A[:, 0:4], op=ALU.subtract))
            P.run("vector", lambda e: e.tensor_scalar(
                out=lr[:, sl], in0=lr[:, sl], scalar1=EPS, scalar2=None,
                op0=ALU.add), inc="vE")
            P.wait("scalar", "vE", P.cnt["vE"])
            P.run("scalar", lambda e: e.activation(
                out=lr[:, sl], in_=lr[:, sl], func=AF.Sqrt), inc="sE")
            P.wait("vector", "sE", P.cnt["sE"])
            P.run("vector", lambda e: e.reciprocal(out=lr[:, sl], in_=lr[:, sl]),
                  inc="vE")
            return P.cnt["vE"]

        for g in range(4):
            sl = slice(4 * g, 4 * g + 4)
            for j in range(4 * g, 4 * g + 4):
                ps = pt[j % 2]
                ckpt(f"wp{j}")
                if j >= 2:
                    P.wait("tensor", "sE", proj_evac[j - 2])
                else:
                    P.wait("tensor", "sE", kb["sE"])
                if j >= 6:
                    P.wait("tensor", "vE", xp_free[j - 6])
                for s in range(DCH):
                    P.wait("tensor", "vE", qk_ev[s])
                    P.wait("tensor", "vE", m_ev[s])
                    for h in range(2):
                        mm(ps[:, 512 * h:512 * (h + 1)],
                           QK.ap()[:, s, 128 * j:128 * (j + 1)],
                           M_sb.ap()[:, s, 512 * h:512 * (h + 1)],
                           start=(s == 0), stop=False)
                P.wait("tensor", "vE", rows_vE)
                for h in range(2):
                    mm(ps[:, 512 * h:512 * (h + 1)], ONE1[0:1, :],
                       ROWP.ap()[0:1, 512 * h:512 * (h + 1)], start=False,
                       stop=True, inc=(h == 1))
                P.wait("scalar", "tE", P.cnt["tE"])
                P.run("scalar", lambda e, ps=ps, j=j: e.activation(
                    out=XP.ap()[:, j % 6, :], in_=ps[:, :], func=AF.Copy,
                    accum_out=LSUM2[:, j:j + 1]), inc="sE")
                proj_evac[j] = P.cnt["sE"]
                P.run("scalar", lambda e, j=j: e.activation(
                    out=JK1[:, :], in_=XP.ap()[:, j % 6, :], func=AF.Square,
                    accum_out=LSQ2[:, j:j + 1]), inc="sE")
            group_prep(LMU2, LR2, LSUM2, LSQ2, sl, P.cnt["sE"])
            for j in range(4 * g, 4 * g + 4):
                xpj = XP.ap()[:, j % 6, :]
                P.run("vector", lambda e, xpj=xpj, j=j: e.tensor_scalar(
                    out=xpj, in0=xpj, scalar1=LMU2[:, j:j + 1],
                    scalar2=LR2[:, j:j + 1], op0=ALU.subtract, op1=ALU.mult))
                P.wait("vector", f"dXS{j % 2}", P.cnt[f"dXS{j % 2}"])
                P.run("vector", lambda e, xpj=xpj, j=j: e.scalar_tensor_tensor(
                    out=xpj, in0=xpj, scalar=1.0, in1=XS[:, j % 2, :], op0=ALU.mult,
                    op1=ALU.add, accum_out=LSUM3[:, j:j + 1]), inc="vE")
                xprime_ev[j] = P.cnt["vE"]
                if j + 2 < NCH:
                    load_xs(j + 2)
                P.wait("scalar", "vE", xprime_ev[j])
                P.run("scalar", lambda e, xpj=xpj, j=j: e.activation(
                    out=JK2[:, :], in_=xpj, func=AF.Square,
                    accum_out=LSQ3[:, j:j + 1]), inc="sE")
            group_prep(LMU3, LR3, LSUM3, LSQ3, sl, P.cnt["sE"])
            if g >= 1:
                P.wait("vector", "dT", tr2_ev[g - 1])
            for j in range(4 * g, 4 * g + 4):
                xpj = XP.ap()[:, j % 6, :]
                P.run("vector", lambda e, xpj=xpj, j=j: e.tensor_scalar(
                    out=RING[:, j % 4, :], in0=xpj, scalar1=LMU3[:, j:j + 1],
                    scalar2=LR3[:, j:j + 1], op0=ALU.subtract, op1=ALU.mult),
                    inc="vE")
            ln2_vE = P.cnt["vE"]
            for jj in range(4 * g, 4 * g + 4):
                P.wait("sync", "vE", ln2_vE)
                for cl in DMA_SEMS:
                    P.wait("sync", cl, P.cnt[cl])
                P.dma("sync", LNT.ap()[:, :, jj, :], RING[:, jj % 4, :], "dT",
                      transpose=True)
                P.wait("sync", "dT", P.cnt["dT"])
            tr2_ev[g] = P.cnt["dT"]
            for jj in range(4 * g, 4 * g + 4):
                psf = pt[2 + (jj % 2)]
                P.wait("tensor", "dT", tr2_ev[g])
                P.wait("tensor", "vE", weff_vE)
                P.wait("tensor", "vE", m_ev[7])
                if jj >= 2:
                    P.wait("tensor", "sE", ffn_evac[jj - 2])
                for dc in range(DCH):
                    for h in range(2):
                        mm(psf[:, 512 * h:512 * (h + 1)], LNT.ap()[:, dc, jj, :],
                           WA[:, dc, 512 * h:512 * (h + 1)],
                           start=(dc == 0), stop=False)
                for h in range(2):
                    mm(psf[:, 512 * h:512 * (h + 1)], ONE1[0:1, :],
                       ROWB.ap()[0:1, 512 * h:512 * (h + 1)], start=False,
                       stop=True, inc=(h == 1))
                P.wait("scalar", "tE", P.cnt["tE"])
                if jj >= 4:
                    P.wait("scalar", "vE", ffn_done_v[jj - 4])
                    P.wait("scalar", f"dSt{jj % 4}", P.cnt[f"dSt{jj % 4}"])
                P.run("scalar", lambda e, psf=psf, jj=jj: e.activation(
                    out=FROLL.ap()[:, jj % 4, :], in_=psf[:, :], func=AF.Copy,
                    accum_out=LSUM4[:, jj:jj + 1]), inc="sE")
                ffn_evac[jj] = P.cnt["sE"]
                P.run("scalar", lambda e, jj=jj: e.activation(
                    out=JK1[:, :], in_=FROLL.ap()[:, jj % 4, :], func=AF.Square,
                    accum_out=LSQ4[:, jj:jj + 1]), inc="sE")
            group_prep(LMU4, LR4, LSUM4, LSQ4, sl, P.cnt["sE"])
            for jj in range(4 * g, 4 * g + 4):
                fr = FROLL.ap()[:, jj % 4, :]
                P.wait("vector", "sE", ffn_evac[jj])
                P.run("vector", lambda e, fr=fr, jj=jj: e.tensor_scalar(
                    out=fr, in0=fr, scalar1=LMU4[:, jj:jj + 1],
                    scalar2=LR4[:, jj:jj + 1], op0=ALU.subtract, op1=ALU.mult))
                P.run("vector", lambda e, fr=fr, jj=jj: e.tensor_tensor(
                    out=fr, in0=fr, in1=XP.ap()[:, jj % 6, :], op=ALU.add),
                    inc="vE")
                xp_free[jj] = P.cnt["vE"]
                ffn_done_v[jj] = P.cnt["vE"]
                P.wait("sync", "vE", xp_free[jj])
                P.dma("sync", y_out[128 * jj:128 * (jj + 1), :], fr,
                      f"dSt{jj % 4}")

    except _StopBuild:
        for s in DMA_SEMS + ("vE", "sE", "tE", "cc"):
            P.wait("sync", s, P.cnt[s])
        for j in range(NCH):
            P.dma("sync", y_out[128 * j:128 * (j + 1), :], WS[:, 0, :])
    for cl in DMA_SEMS:
        P.wait("sync", cl, P.cnt[cl])
    P.wait("gpsimd", "dSt3", P.cnt["dSt3"])

    with ctx, nc.Block() as block:
        for eng_name in ENGS:
            fns = P.ops[eng_name]

            def body(e, fns=fns):
                for f in fns:
                    f(e)
            getattr(block, eng_name)(body)

    return nc, tap_done


def kernel(**inputs):
    x = np.ascontiguousarray(np.asarray(inputs["L0_attn_x"], np.float32).reshape(16384, D))

    def T(a):
        return np.ascontiguousarray(np.asarray(a, np.float32).T)

    def chunkvec(v):
        return np.ascontiguousarray(np.asarray(v, np.float32).reshape(8, 128).T)

    common = {
        "wqT": T(inputs["L0_attn_Wq"]),
        "wkT": T(inputs["L0_attn_Wk"]),
        "wvT": T(inputs["L0_attn_Wv"]),
        "wpT": T(inputs["L0_attn_Wp"]),
        "wT": T(inputs["L0_ffn_W"]),
        "gq": chunkvec(inputs["L0_attn_gq"]),
        "bq": chunkvec(inputs["L0_attn_bq"]),
        "gk": chunkvec(inputs["L0_attn_gk"]),
        "bk": chunkvec(inputs["L0_attn_bk"]),
        "gv": np.ascontiguousarray(np.asarray(inputs["L0_attn_gv"], np.float32).reshape(1, D)),
        "bv": np.ascontiguousarray(np.asarray(inputs["L0_attn_bv"], np.float32).reshape(1, D)),
        "bp": np.ascontiguousarray(np.asarray(inputs["L0_attn_bp"], np.float32).reshape(1, D)),
        "bb": np.ascontiguousarray(np.asarray(inputs["L0_ffn_b"], np.float32).reshape(1, D)),
        "ldT": T(inputs["L0_ffn_lora_down"]),
        "luT": T(inputs["L0_ffn_lora_up"]),
        "dax": np.ascontiguousarray(np.asarray(inputs["L0_ffn_down_aux"], np.float32)),
        "uaxT": T(inputs["L0_ffn_up_aux"]),
    }
    nc, _ = build()
    in_maps = []
    for c in range(NCORE):
        m = dict(common)
        m["x_in"] = np.ascontiguousarray(x[TOK * c:TOK * (c + 1)])
        in_maps.append(m)
    res = run_bass_kernel_spmd(nc, in_maps, core_ids=list(range(NCORE)))
    out = np.concatenate([np.asarray(res.results[c]["y"]) for c in range(NCORE)],
                         axis=0)
    return out.reshape(2, 8192, D).astype(np.float32)


# version 09
# speedup vs baseline: 1.1027x; 1.0772x over previous
import numpy as np
import contextlib
import concourse.bass as bass
import concourse.mybir as mybir
from concourse.bass_utils import run_bass_kernel_spmd
from concourse.replica_groups import maybe_share_collective_output_space

F32 = mybir.dt.float32
BF16 = mybir.dt.bfloat16
FP16 = mybir.dt.float16
AF = mybir.ActivationFunctionType
ALU = mybir.AluOpType
AX = mybir.AxisListType

NCORE = 8
TOK = 2048
NCH = 16
D = 1024
DCH = 8
EPS = 1e-5
C_BN = 1.0 / 16384.0
C_LN = 1.0 / 1024.0

ENGS = ("sync", "scalar", "vector", "tensor", "gpsimd")


class Prog:

    def __init__(self, sems):
        self.ops = {e: [] for e in ENGS}
        self.sems = sems
        self.cnt = {n: 0 for n in sems}
        self.last_wait = {}

    def wait(self, eng, sem, val):
        assert val <= self.cnt[sem], f"wait {sem}>={val} but only {self.cnt[sem]}"
        key = (eng, sem)
        if self.last_wait.get(key, -1) >= val:
            return
        self.last_wait[key] = val
        h = self.sems[sem]
        self.ops[eng].append(lambda e, h=h, v=val: e.wait_ge(h, v))

    AUTO = {"vector": "vE", "scalar": "sE"}

    def run(self, eng, fn, inc=None, n=1):
        own = self.AUTO.get(eng)
        if own is not None:
            self.wait(eng, own, self.cnt[own])
            h = self.sems[own]
            self.cnt[own] += 1
            self.ops[eng].append(lambda e, f=fn, h=h: f(e).then_inc(h, 1))
            return self.cnt[own]
        if inc is None:
            self.ops[eng].append(fn)
            return None
        h = self.sems[inc]
        self.cnt[inc] += n
        self.ops[eng].append(lambda e, f=fn, h=h, n=n: f(e).then_inc(h, n))
        return self.cnt[inc]

    def dma(self, eng, out, in_, sem="dS", **kw):
        return self.run(eng, lambda e: e.dma_start(out=out, in_=in_, **kw),
                        inc=sem, n=16)


class _StopBuild(Exception):
    pass


def build(taps=None, stop=None):
    nc = bass.Bass("TRN2", target_bir_lowering=False)
    tap_list = list(taps or [])
    tap_done = []

    x_in = nc.dram_tensor("x_in", [TOK, D], F32, kind="ExternalInput").ap()
    wqT = nc.dram_tensor("wqT", [D, D], F32, kind="ExternalInput").ap()
    wkT = nc.dram_tensor("wkT", [D, D], F32, kind="ExternalInput").ap()
    wvT = nc.dram_tensor("wvT", [D, D], F32, kind="ExternalInput").ap()
    wpT = nc.dram_tensor("wpT", [D, D], F32, kind="ExternalInput").ap()
    wT = nc.dram_tensor("wT", [D, D], F32, kind="ExternalInput").ap()
    gq_d = nc.dram_tensor("gq", [128, 8], F32, kind="ExternalInput").ap()
    bq_d = nc.dram_tensor("bq", [128, 8], F32, kind="ExternalInput").ap()
    gk_d = nc.dram_tensor("gk", [128, 8], F32, kind="ExternalInput").ap()
    bk_d = nc.dram_tensor("bk", [128, 8], F32, kind="ExternalInput").ap()
    gv_d = nc.dram_tensor("gv", [1, D], F32, kind="ExternalInput").ap()
    bv_d = nc.dram_tensor("bv", [1, D], F32, kind="ExternalInput").ap()
    bp_d = nc.dram_tensor("bp", [1, D], F32, kind="ExternalInput").ap()
    bb_d = nc.dram_tensor("bb", [1, D], F32, kind="ExternalInput").ap()
    ldT_d = nc.dram_tensor("ldT", [100, 4], F32, kind="ExternalInput").ap()
    luT_d = nc.dram_tensor("luT", [4, 50], F32, kind="ExternalInput").ap()
    dax_d = nc.dram_tensor("dax", [100, D], F32, kind="ExternalInput").ap()
    uaxT_d = nc.dram_tensor("uaxT", [50, D], F32, kind="ExternalInput").ap()
    idm_d = nc.dram_tensor("idm", [128, 128], F32, kind="ExternalInput").ap()
    y_out = nc.dram_tensor("y", [TOK, D], F32, kind="ExternalOutput").ap()

    rg8 = [list(range(8))]
    rg4 = [[0, 1, 2, 3], [4, 5, 6, 7]]
    sp8 = maybe_share_collective_output_space("AllReduce", rg8)
    sp4 = maybe_share_collective_output_space("AllReduce", rg4)
    arq_i = nc.dram_tensor("arq_i", [128, 16], F32)
    arq_o = nc.dram_tensor("arq_o", [128, 16], F32, addr_space=sp8)
    ark_i = nc.dram_tensor("ark_i", [128, 16], F32)
    ark_o = nc.dram_tensor("ark_o", [128, 16], F32, addr_space=sp8)
    arv_i = nc.dram_tensor("arv_i", [2, D], F32)
    arv_o = nc.dram_tensor("arv_o", [2, D], F32, addr_space=sp8)
    arw_i = nc.dram_tensor("arw_i", [128, D], F32)
    arw_o = nc.dram_tensor("arw_o", [128, D], F32, addr_space=sp4)

    A0 = (nc.sbuf_base + 31) // 32 * 32
    NEED = 0x33940
    assert A0 + NEED <= nc.sbuf_top, (A0, NEED, nc.sbuf_top)
    nc.alloc_sbuf_tensor("arena", [128, NEED + 64], mybir.dt.uint8)

    def at(name, shape, dtype, off):
        return nc.alloc_sbuf_tensor_at(name, shape, dtype, offset=A0 + off)

    LNT = at("LNT", [128, 8, 16, 128], FP16, 0x00000)
    RING = at("RING", [128, 4, D], FP16, 0x08000)
    SREP = at("SREP", [128, D], FP16, 0x08000)
    BREP = at("BREP", [128, D], FP16, 0x08800)
    HQ = at("HQ", [128, 8, TOK], FP16, 0x0A000)
    QK = at("QK", [128, 8, TOK], FP16, 0x0A000)
    HK = at("HK", [128, 8, TOK], FP16, 0x12000)
    KB = at("KB", [128, 8, TOK], FP16, 0x12000)
    M_sb = at("M_sb", [128, 8, D], FP16, 0x12000)
    FROLL = at("FROLL", [128, 4, D], F32, 0x16000)
    HV = at("HV", [128, NCH, D], FP16, 0x1A000)
    VTOK = at("VTOK", [128, NCH, D], FP16, 0x1A000)
    XS = at("XS", [128, 2, D], F32, 0x1A000)
    VVF = at("VVF", [128, 8, 128], F32, 0x1A000)
    VVB = at("VVB", [128, 8, 128], FP16, 0x1C000)
    DAXB = at("DAXB", [100, D], FP16, 0x1C800)
    UAXB = at("UAXB", [50, D], FP16, 0x1D000)
    AB_ = at("AB_", [4, D], FP16, 0x1D800)
    BB_ = at("BB_", [50, D], FP16, 0x1E000)
    XP = at("XP", [128, 6, D], FP16, 0x22000)
    XI = at("XI", [128, 3, D], F32, 0x22000)
    WA = at("WA", [128, 8, D], FP16, 0x25000)
    WB = at("WB", [128, 8, D], FP16, 0x29000)
    WS = at("WS", [128, 2, D], F32, 0x2D000)
    JK1 = at("JK1", [128, D], FP16, 0x2F000)
    JK2 = at("JK2", [128, D], FP16, 0x2F800)
    VS0 = at("VS0", [1, D], F32, 0x30000)
    VS1 = at("VS1", [1, D], F32, 0x29000)
    VTMP = at("VTMP", [1, D], F32, 0x22000)
    VGAM = at("VGAM", [1, D], F32, 0x23000)
    VBET = at("VBET", [1, D], F32, 0x24000)
    SVB0 = at("SVB0", [1, D], FP16, 0x31000)
    SVB1 = at("SVB1", [1, D], FP16, 0x31800)
    ROWP = at("ROWP", [1, D], FP16, 0x32000)
    ROWB = at("ROWB", [1, D], FP16, 0x32800)
    GQ = at("GQ", [128, 8], F32, 0x33000)
    BQ = at("BQ", [128, 8], F32, 0x33020)
    GK = at("GK", [128, 8], F32, 0x33040)
    BK = at("BK", [128, 8], F32, 0x33060)
    QST = at("QST", [128, 16], F32, 0x33080)
    KST = at("KST", [128, 16], F32, 0x330C0)
    ACC4 = at("ACC4", [128, 8, 4], F32, 0x33100)
    QSV = at("QSV", [128, 8], F32, 0x33180)
    QBV = at("QBV", [128, 8], F32, 0x331A0)
    KSV = at("KSV", [128, 8], F32, 0x331C0)
    KBV = at("KBV", [128, 8], F32, 0x331E0)
    LSUM = at("LSUM", [128, 16], F32, 0x33200)
    LSQ = at("LSQ", [128, 16], F32, 0x33240)
    LMU = at("LMU", [128, 16], F32, 0x33280)
    LR = at("LR", [128, 16], F32, 0x332C0)
    T8A = at("T8A", [128, 8], F32, 0x33300)
    T8B = at("T8B", [128, 8], F32, 0x33320)
    ONE1 = at("ONE1", [1, 128], FP16, 0x33340)
    ONEC = at("ONEC", [128, 1], FP16, 0x33440)
    IDM = at("IDM", [128, 128], FP16, 0x33840)
    LDTB = at("LDTB", [100, 4], FP16, 0x33460)
    LUTB = at("LUTB", [4, 50], FP16, 0x33480)
    LSUM2 = at("LSUM2", [128, 16], F32, 0x33500)
    LSQ2 = at("LSQ2", [128, 16], F32, 0x33540)
    LMU2 = at("LMU2", [128, 16], F32, 0x33580)
    LR2 = at("LR2", [128, 16], F32, 0x335C0)
    LSUM3 = at("LSUM3", [128, 16], F32, 0x33600)
    LSQ3 = at("LSQ3", [128, 16], F32, 0x33640)
    LMU3 = at("LMU3", [128, 16], F32, 0x33680)
    LR3 = at("LR3", [128, 16], F32, 0x336C0)
    LSUM4 = at("LSUM4", [128, 16], F32, 0x33700)
    LSQ4 = at("LSQ4", [128, 16], F32, 0x33740)
    LMU4 = at("LMU4", [128, 16], F32, 0x33780)
    LR4 = at("LR4", [128, 16], F32, 0x337C0)

    ctx = contextlib.ExitStack()
    pt = [ctx.enter_context(nc.psum_tensor(f"pt{i}", [128, D], F32)) for i in range(4)]
    DMA_SEMS = ("dS", "dG", "dT", "dW0", "dW1", "dX0", "dX1", "dX2",
                "dXS0", "dXS1", "dSt0", "dSt1", "dSt2", "dSt3", "dR")
    sems = {n: ctx.enter_context(nc.semaphore(n)) for n in
            DMA_SEMS + ("cc", "tE", "sE", "vE")}
    P = Prog(sems)

    def tap(name, ap, shape, dtype):
        if name not in tap_list:
            return
        for s in DMA_SEMS + ("vE", "sE", "tE"):
            P.wait("sync", s, P.cnt[s])
        t = nc.dram_tensor("tap_" + name, shape, dtype, kind="ExternalOutput").ap()
        P.dma("sync", t[...], ap)
        tap_done.append(name)

    def ckpt(name):
        if stop == name:
            raise _StopBuild

    def mm(psum_ap, lhsT, rhs, start, stop, inc=False):
        return P.run("tensor",
                     lambda e, o=psum_ap, l=lhsT, r=rhs, s1=start, s2=stop:
                     e.matmul(out=o, lhsT=l, rhs=r, start=s1, stop=s2),
                     inc=("tE" if inc else None))

    def lnT_rhs(dc, nt):
        return LNT.ap()[:, dc, 4 * nt:4 * nt + 4, :].rearrange("p a b -> p (a b)")

    try:
        P.run("vector", lambda e: e.memset(ONE1[:, :], 1.0))
        P.run("vector", lambda e: e.memset(ONEC[:, :], 1.0), inc="vE")
        P.dma("sync", FROLL.ap()[:, 0, 0:128], idm_d[:, :], "dR")
        P.wait("vector", "dR", P.cnt["dR"])
        P.run("vector", lambda e: e.tensor_copy(out=IDM[:, :],
                                                in_=FROLL.ap()[:, 0, 0:128]),
              inc="vE")

        for dst, src in ((GQ, gq_d), (BQ, bq_d), (GK, gk_d), (BK, bk_d)):
            P.dma("sync", dst[:, :], src[:, :], "dW0")
        P.dma("sync", WS[0:1, 0, :], bp_d[:, :], "dW0")
        P.dma("sync", WS[0:1, 1, :], bb_d[:, :], "dW0")
        P.wait("vector", "dW0", P.cnt["dW0"])
        P.run("vector", lambda e: e.tensor_copy(out=ROWP.ap()[0:1, :], in_=WS[0:1, 0, :]))
        P.run("vector", lambda e: e.tensor_copy(out=ROWB.ap()[0:1, :], in_=WS[0:1, 1, :]),
              inc="vE")
        rows_vE = P.cnt["vE"]

        ws_free = {0: rows_vE, 1: rows_vE}

        def load_weight(dst, src_dram, scale=None, pre_tE=0):
            for c in range(DCH):
                slot = c % 2
                P.wait("sync", "vE", ws_free[slot])
                P.dma("sync", WS[:, slot, :], src_dram[128 * c:128 * (c + 1), :],
                      f"dW{slot}")
                P.wait("vector", f"dW{slot}", P.cnt[f"dW{slot}"])
                if pre_tE:
                    P.wait("vector", "tE", pre_tE)
                if scale is None:
                    P.run("vector",
                          lambda e, s=slot, c=c: e.tensor_copy(out=dst[:, c, :],
                                                               in_=WS[:, s, :]), inc="vE")
                else:
                    P.run("vector",
                          lambda e, s=slot, c=c: e.tensor_scalar(
                              out=dst[:, c, :], in0=WS[:, s, :], scalar1=scale,
                              scalar2=None, op0=ALU.mult), inc="vE")
                ws_free[slot] = P.cnt["vE"]
            return P.cnt["vE"]

        wq_vE = load_weight(WB, wqT)

        xload = {}
        app_ev = {}
        trj_ev = {}
        tr_ev = {}
        for j in range(3):
            P.dma("sync", XI[:, j, :], x_in[128 * j:128 * (j + 1), :], f"dX{j}")
        for j in range(NCH):
            P.wait("vector", f"dX{j % 3}", P.cnt[f"dX{j % 3}"])
            xc = XI[:, j % 3, :]
            P.run("vector", lambda e, xc=xc, j=j: e.tensor_scalar(
                out=JK1[:, :], in0=xc, scalar1=1.0, scalar2=0.0, op0=ALU.mult,
                op1=ALU.add, accum_out=LSUM[:, j:j + 1]))
            P.run("vector", lambda e, xc=xc, j=j: e.scalar_tensor_tensor(
                out=JK1[:, :], in0=xc, scalar=1.0, in1=xc, op0=ALU.mult,
                op1=ALU.mult, accum_out=LSQ[:, j:j + 1]), inc="vE")
            P.run("vector", lambda e, j=j: e.tensor_scalar(
                out=LMU[:, j:j + 1], in0=LSUM[:, j:j + 1], scalar1=C_LN, scalar2=None,
                op0=ALU.mult))
            P.run("vector", lambda e, j=j: e.tensor_scalar(
                out=LR[:, j:j + 1], in0=LSQ[:, j:j + 1], scalar1=C_LN, scalar2=None,
                op0=ALU.mult))
            P.run("vector", lambda e, j=j: e.tensor_tensor(
                out=T8A[:, 0:1], in0=LMU[:, j:j + 1], in1=LMU[:, j:j + 1], op=ALU.mult))
            P.run("vector", lambda e, j=j: e.tensor_tensor(
                out=LR[:, j:j + 1], in0=LR[:, j:j + 1], in1=T8A[:, 0:1],
                op=ALU.subtract))
            P.run("vector", lambda e, j=j: e.tensor_scalar(
                out=LR[:, j:j + 1], in0=LR[:, j:j + 1], scalar1=EPS, scalar2=None,
                op0=ALU.add), inc="vE")
            P.wait("scalar", "vE", P.cnt["vE"])
            P.run("scalar", lambda e, j=j: e.activation(
                out=LR[:, j:j + 1], in_=LR[:, j:j + 1], func=AF.Sqrt), inc="sE")
            P.wait("vector", "sE", P.cnt["sE"])
            P.run("vector", lambda e, j=j: e.reciprocal(
                out=LR[:, j:j + 1], in_=LR[:, j:j + 1]))
            if j >= 4:
                P.wait("vector", "dT", trj_ev[j - 4])
            P.run("vector", lambda e, xc=xc, j=j: e.tensor_scalar(
                out=RING[:, j % 4, :], in0=xc, scalar1=LMU[:, j:j + 1],
                scalar2=LR[:, j:j + 1], op0=ALU.subtract, op1=ALU.mult), inc="vE")
            app_ev[j] = P.cnt["vE"]
            P.wait("sync", "vE", app_ev[j])
            for cl in DMA_SEMS:
                P.wait("sync", cl, P.cnt[cl])
            P.dma("sync", LNT.ap()[:, :, j, :], RING[:, j % 4, :], "dT", transpose=True)
            trj_ev[j] = P.cnt["dT"]
            P.wait("sync", "dT", P.cnt["dT"])
            if j % 4 == 3:
                tr_ev[j // 4] = P.cnt["dT"]
            if j + 3 < NCH:
                P.wait("sync", "vE", app_ev[j])
                P.dma("sync", XI[:, (j + 3) % 3, :],
                      x_in[128 * (j + 3):128 * (j + 4), :], f"dX{j % 3}")

        tap("ln1T", LNT.ap()[:, :, :, :], [128, 8, 16, 128], FP16)
        ckpt("ln1")

        def branch_fm(w_sb, w_vE, h_sb, stat_sb, pre_sE):
            regions = [(pt[0][:, 0:512], 0), (pt[0][:, 512:1024], 1),
                       (pt[1][:, 0:512], 2), (pt[1][:, 512:1024], 3)]
            evac_done = {}
            k = 0
            for ec in range(DCH):
                for nt in range(4):
                    ps, ridx = regions[k % 4]
                    if k >= 4:
                        P.wait("tensor", "sE", evac_done[ridx])
                    elif pre_sE:
                        P.wait("tensor", "sE", pre_sE)
                    P.wait("tensor", "vE", w_vE)
                    P.wait("tensor", "dT", tr_ev[nt])
                    for dc in range(DCH):
                        mm(ps, w_sb[:, dc, 128 * ec:128 * (ec + 1)], lnT_rhs(dc, nt),
                           start=(dc == 0), stop=(dc == DCH - 1), inc=(dc == DCH - 1))
                    mm_ev = P.cnt["tE"]
                    P.wait("scalar", "tE", mm_ev)
                    P.run("scalar", lambda e, ps=ps, ec=ec, nt=nt: e.activation(
                        out=h_sb.ap()[:, ec, 512 * nt:512 * (nt + 1)], in_=ps,
                        func=AF.Copy, accum_out=ACC4[:, ec, nt:nt + 1]), inc="sE")
                    evac_done[ridx] = P.cnt["sE"]
                    k += 1
                P.wait("vector", "sE", P.cnt["sE"])
                P.run("vector", lambda e, ec=ec: e.scalar_tensor_tensor(
                    out=JK2[:, :], in0=h_sb.ap()[:, ec, 0:1024], scalar=1.0,
                    in1=h_sb.ap()[:, ec, 0:1024], op0=ALU.mult, op1=ALU.mult,
                    accum_out=T8A[:, ec:ec + 1]))
                P.run("vector", lambda e, ec=ec: e.scalar_tensor_tensor(
                    out=JK2[:, :], in0=h_sb.ap()[:, ec, 1024:2048], scalar=1.0,
                    in1=h_sb.ap()[:, ec, 1024:2048], op0=ALU.mult, op1=ALU.mult,
                    accum_out=T8B[:, ec:ec + 1]), inc="vE")
            P.run("vector", lambda e: e.tensor_reduce(
                out=stat_sb[:, 0:8], in_=ACC4[:, :, :], axis=AX.X, op=ALU.add))
            P.run("vector", lambda e: e.tensor_tensor(
                out=stat_sb[:, 8:16], in0=T8A[:, :], in1=T8B[:, :], op=ALU.add), inc="vE")
            return {"vE": P.cnt["vE"], "tE": P.cnt["tE"], "sE": P.cnt["sE"]}

        def allreduce(in_ap, out_ap, sb_ap, ready, rg):
            P.wait("gpsimd", ready[0], ready[1])
            P.wait("gpsimd", "dT", P.cnt["dT"])
            P.dma("gpsimd", in_ap, sb_ap, "dG")
            P.wait("gpsimd", "dG", P.cnt["dG"])
            P.run("gpsimd", lambda e, i=in_ap, o=out_ap, rg=rg: e.collective_compute(
                "AllReduce", ALU.add, replica_groups=rg, ins=[i], outs=[o]), inc="cc")
            P.wait("gpsimd", "cc", P.cnt["cc"])
            P.dma("gpsimd", sb_ap, out_ap, "dG")
            P.wait("gpsimd", "dG", P.cnt["dG"])
            return ("dG", P.cnt["dG"])

        def bn_prep(stat, g_sb, b_sb, sv, bv, ar_ev):
            P.wait("vector", ar_ev[0], ar_ev[1])
            P.run("vector", lambda e: e.tensor_scalar(
                out=T8A[:, :], in0=stat[:, 0:8], scalar1=C_BN, scalar2=None, op0=ALU.mult))
            P.run("vector", lambda e: e.tensor_scalar(
                out=T8B[:, :], in0=stat[:, 8:16], scalar1=C_BN, scalar2=None, op0=ALU.mult))
            P.run("vector", lambda e: e.tensor_tensor(
                out=sv, in0=T8A[:, :], in1=T8A[:, :], op=ALU.mult))
            P.run("vector", lambda e: e.tensor_tensor(
                out=T8B[:, :], in0=T8B[:, :], in1=sv, op=ALU.subtract))
            P.run("vector", lambda e: e.tensor_scalar(
                out=sv, in0=T8B[:, :], scalar1=EPS, scalar2=None, op0=ALU.add), inc="vE")
            P.wait("scalar", "vE", P.cnt["vE"])
            P.run("scalar", lambda e: e.activation(out=sv, in_=sv, func=AF.Sqrt), inc="sE")
            P.wait("vector", "sE", P.cnt["sE"])
            P.run("vector", lambda e: e.reciprocal(out=sv, in_=sv))
            P.run("vector", lambda e: e.tensor_tensor(
                out=sv, in0=sv, in1=g_sb[:, :], op=ALU.mult))
            P.run("vector", lambda e: e.tensor_scalar(
                out=sv, in0=sv, scalar1=4.0, scalar2=None, op0=ALU.mult))
            P.run("vector", lambda e: e.tensor_tensor(
                out=bv, in0=T8A[:, :], in1=sv, op=ALU.mult))
            P.run("vector", lambda e: e.tensor_scalar(
                out=T8B[:, :], in0=b_sb[:, :], scalar1=4.0, scalar2=None, op0=ALU.mult))
            P.run("vector", lambda e: e.tensor_tensor(
                out=T8B[:, :], in0=T8B[:, :], in1=bv, op=ALU.subtract))
            P.run("vector", lambda e: e.tensor_scalar(
                out=bv, in0=T8B[:, :], scalar1=-4.0, scalar2=None, op0=ALU.add), inc="vE")
            return P.cnt["vE"]

        qb = branch_fm(WB, wq_vE, HQ, QST[:, :], pre_sE=0)
        tap("hq", HQ.ap()[:, :, :], [128, 8, TOK], FP16)
        tap("qstat_loc", QST[:, :], [128, 16], F32)
        ckpt("q_mm")
        arq_ev = allreduce(arq_i.ap(), arq_o.ap(), QST[:, :], ("vE", qb["vE"]), rg8)
        if "qstat_g" in tap_list:
            P.wait("sync", arq_ev[0], arq_ev[1])
        tap("qstat_g", QST[:, :], [128, 16], F32)
        ckpt("arq")

        wv_vE = load_weight(WA, wvT)

        vprod = [pt[0], pt[1]]
        evac_done_v = {}
        sqmm_ev = {}
        for j in range(NCH):
            ps = vprod[j % 2]
            if j >= 2:
                P.wait("tensor", "sE", evac_done_v[j % 2])
            else:
                P.wait("tensor", "sE", qb["sE"])
            P.wait("tensor", "vE", wv_vE)
            P.wait("tensor", "dT", tr_ev[j // 4])
            for dc in range(DCH):
                for h in range(2):
                    mm(ps[:, 512 * h:512 * (h + 1)], LNT.ap()[:, dc, j, :],
                       WA[:, dc, 512 * h:512 * (h + 1)],
                       start=(dc == 0), stop=(dc == DCH - 1),
                       inc=(dc == DCH - 1 and h == 1))
            mm_ev = P.cnt["tE"]
            P.wait("scalar", "tE", mm_ev)
            P.run("scalar", lambda e, ps=ps, j=j: e.activation(
                out=HV.ap()[:, j, :], in_=ps[:, :], func=AF.Copy), inc="sE")
            evac_done_v[j % 2] = P.cnt["sE"]
            hv_ev = P.cnt["sE"]
            jk = JK1 if j % 2 == 0 else JK2
            if j >= 2:
                P.wait("vector", "tE", sqmm_ev[j - 2])
            P.wait("vector", "sE", hv_ev)
            P.run("vector", lambda e, j=j, jk=jk: e.scalar_tensor_tensor(
                out=jk[:, :], in0=HV.ap()[:, j, :], scalar=1.0,
                in1=HV.ap()[:, j, :], op0=ALU.mult, op1=ALU.mult), inc="vE")
            sq_ev = P.cnt["vE"]
            P.wait("tensor", "sE", hv_ev)
            for h in range(2):
                mm(pt[2][0:1, 512 * h:512 * (h + 1)], ONEC[:, 0:1],
                   HV.ap()[:, j, 512 * h:512 * (h + 1)],
                   start=(j == 0), stop=(j == NCH - 1))
            P.wait("tensor", "vE", sq_ev)
            for h in range(2):
                mm(pt[3][0:1, 512 * h:512 * (h + 1)], ONEC[:, 0:1],
                   jk[:, 512 * h:512 * (h + 1)],
                   start=(j == 0), stop=(j == NCH - 1), inc=(h == 1))
            sqmm_ev[j] = P.cnt["tE"]
        v_mm_tE = P.cnt["tE"]
        P.wait("vector", "tE", v_mm_tE)
        P.run("vector", lambda e: e.tensor_copy(out=VS0.ap()[0:1, :], in_=pt[2][0:1, :]))
        P.run("vector", lambda e: e.tensor_copy(out=VS1.ap()[0:1, :], in_=pt[3][0:1, :]),
              inc="vE")
        vstat_vE = P.cnt["vE"]
        ckpt("v_mm")
        tap("hv", HV.ap()[:, :, :], [128, NCH, D], FP16)
        tap("vstat_loc", VS0.ap()[0:1, :], [1, D], F32)
        P.wait("gpsimd", "vE", vstat_vE)
        P.wait("gpsimd", "dT", P.cnt["dT"])
        P.dma("gpsimd", arv_i.ap()[0:1, :], VS0.ap()[0:1, :], "dG")
        P.dma("gpsimd", arv_i.ap()[1:2, :], VS1.ap()[0:1, :], "dG")
        P.wait("gpsimd", "dG", P.cnt["dG"])
        P.run("gpsimd", lambda e: e.collective_compute(
            "AllReduce", ALU.add, replica_groups=rg8, ins=[arv_i.ap()],
            outs=[arv_o.ap()]), inc="cc")
        P.wait("gpsimd", "cc", P.cnt["cc"])
        P.dma("gpsimd", VS0.ap()[0:1, :], arv_o.ap()[0:1, :], "dG")
        P.dma("gpsimd", VS1.ap()[0:1, :], arv_o.ap()[1:2, :], "dG")
        P.wait("gpsimd", "dG", P.cnt["dG"])
        arv_ev = ("dG", P.cnt["dG"])
        ckpt("arv")

        wk_vE = load_weight(WA, wkT, pre_tE=v_mm_tE)

        qprep_vE = bn_prep(QST[:, :], GQ, BQ, QSV[:, :], QBV[:, :], arq_ev)
        tap("qsv", QSV[:, :], [128, 8], F32)
        tap("qbv", QBV[:, :], [128, 8], F32)
        for ec in range(DCH):
            P.wait("scalar", "vE", qprep_vE)
            P.run("scalar", lambda e, ec=ec: e.activation(
                out=QK.ap()[:, ec, :], in_=HQ.ap()[:, ec, :], func=AF.Sigmoid,
                scale=QSV[:, ec:ec + 1], bias=QBV[:, ec:ec + 1]), inc="sE")
        qsig_sE = P.cnt["sE"]
        qsig_vE = P.cnt["vE"]
        tap("q", QK.ap()[:, :, :], [128, 8, TOK], FP16)
        ckpt("qsig")

        P.wait("sync", "vE", app_ev[NCH - 1])
        P.dma("sync", VGAM.ap()[0:1, :], gv_d[:, :], "dR")
        P.dma("sync", VBET.ap()[0:1, :], bv_d[:, :], "dR")
        P.wait("vector", arv_ev[0], arv_ev[1])
        P.wait("vector", "dR", P.cnt["dR"])
        vs0 = VS0.ap()[0:1, :]
        vs1 = VS1.ap()[0:1, :]
        vt = VTMP.ap()[0:1, :]
        P.run("vector", lambda e: e.tensor_scalar(
            out=vs0, in0=vs0, scalar1=C_BN, scalar2=None, op0=ALU.mult))
        P.run("vector", lambda e: e.tensor_scalar(
            out=vs1, in0=vs1, scalar1=C_BN, scalar2=None, op0=ALU.mult))
        P.run("vector", lambda e: e.tensor_tensor(
            out=vt, in0=vs0, in1=vs0, op=ALU.mult))
        P.run("vector", lambda e: e.tensor_tensor(
            out=vs1, in0=vs1, in1=vt, op=ALU.subtract))
        P.run("vector", lambda e: e.tensor_scalar(
            out=vs1, in0=vs1, scalar1=EPS, scalar2=None, op0=ALU.add), inc="vE")
        P.wait("scalar", "vE", P.cnt["vE"])
        P.run("scalar", lambda e: e.activation(out=vs1, in_=vs1, func=AF.Sqrt), inc="sE")
        P.wait("vector", "sE", P.cnt["sE"])
        P.run("vector", lambda e: e.reciprocal(out=vs1, in_=vs1))
        P.run("vector", lambda e: e.tensor_tensor(
            out=vs1, in0=vs1, in1=VGAM.ap()[0:1, :], op=ALU.mult))
        P.run("vector", lambda e: e.tensor_scalar(
            out=vs1, in0=vs1, scalar1=4.0, scalar2=None, op0=ALU.mult))
        P.run("vector", lambda e: e.tensor_tensor(
            out=vt, in0=vs0, in1=vs1, op=ALU.mult))
        P.run("vector", lambda e: e.tensor_scalar(
            out=VBET.ap()[0:1, :], in0=VBET.ap()[0:1, :], scalar1=4.0,
            scalar2=None, op0=ALU.mult))
        P.run("vector", lambda e: e.tensor_tensor(
            out=vt, in0=vt, in1=VBET.ap()[0:1, :], op=ALU.subtract))
        P.run("vector", lambda e: e.tensor_scalar(
            out=vt, in0=vt, scalar1=-1.0, scalar2=-4.0, op0=ALU.mult,
            op1=ALU.add))
        P.run("vector", lambda e: e.tensor_copy(out=SVB0.ap()[0:1, :], in_=vs1))
        P.run("vector", lambda e: e.tensor_copy(out=SVB1.ap()[0:1, :], in_=vt),
              inc="vE")
        svb_vE = P.cnt["vE"]
        ckpt("vprep")
        P.wait("tensor", "vE", svb_vE)
        P.wait("tensor", "vE", vstat_vE)
        for h in range(2):
            mm(pt[3][:, 512 * h:512 * (h + 1)], ONE1[0:1, :],
               SVB0.ap()[0:1, 512 * h:512 * (h + 1)], start=True, stop=True,
               inc=(h == 1))
        srep_tE = P.cnt["tE"]
        P.wait("vector", "tE", srep_tE)
        P.wait("vector", "dT", tr_ev[3])
        P.run("vector", lambda e: e.tensor_copy(out=SREP[:, :], in_=pt[3][:, :]), inc="vE")
        srep_vE = P.cnt["vE"]
        P.wait("tensor", "vE", srep_vE)
        for h in range(2):
            mm(pt[3][:, 512 * h:512 * (h + 1)], ONE1[0:1, :],
               SVB1.ap()[0:1, 512 * h:512 * (h + 1)], start=True, stop=True,
               inc=(h == 1))
        brep_tE = P.cnt["tE"]
        P.wait("vector", "tE", brep_tE)
        P.run("vector", lambda e: e.tensor_copy(out=BREP[:, :], in_=pt[3][:, :]), inc="vE")
        brep_vE = P.cnt["vE"]
        ckpt("brep")

        for j in range(NCH):
            P.wait("vector", "vE", brep_vE)
            P.run("vector", lambda e, j=j: e.tensor_tensor(
                out=HV.ap()[:, j, :], in0=HV.ap()[:, j, :], in1=SREP[:, :],
                op=ALU.mult))
            P.run("vector", lambda e, j=j: e.tensor_tensor(
                out=HV.ap()[:, j, :], in0=HV.ap()[:, j, :], in1=BREP[:, :],
                op=ALU.add), inc="vE")
            P.wait("scalar", "vE", P.cnt["vE"])
            P.run("scalar", lambda e, j=j: e.activation(
                out=VTOK.ap()[:, j, :], in_=HV.ap()[:, j, :], func=AF.Sigmoid),
                inc="sE")
        vsig_sE = P.cnt["sE"]
        ckpt("vsig")
        tap("v", VTOK.ap()[:, :, :], [128, NCH, D], FP16)

        P.wait("tensor", "sE", vsig_sE)
        P.wait("tensor", "vE", brep_vE)
        for s in range(DCH):
            for j in range(NCH):
                mm(pt[2][:, 128 * s:128 * (s + 1)],
                   VTOK.ap()[:, j, 128 * s:128 * (s + 1)],
                   VTOK.ap()[:, j, 128 * s:128 * (s + 1)],
                   start=(j == 0), stop=(j == NCH - 1), inc=(j == NCH - 1))
        vv_tE = P.cnt["tE"]
        P.wait("vector", "tE", vv_tE)
        P.run("vector", lambda e: e.tensor_copy(
            out=VVF.ap().rearrange("p a b -> p (a b)"), in_=pt[2][:, :]), inc="vE")
        vv_loc_vE = P.cnt["vE"]
        tap("vv_loc", VVF.ap()[:, :, :], [128, 8, 128], F32)
        arw_ev = allreduce(arw_i.ap(), arw_o.ap(),
                           VVF.ap().rearrange("p a b -> p (a b)"),
                           ("vE", vv_loc_vE), rg4)
        P.wait("vector", arw_ev[0], arw_ev[1])
        P.run("vector", lambda e: e.memset(VVB.ap().rearrange("p a b -> p (a b)"), 0.0))
        for s in range(DCH):
            P.run("vector", lambda e, s=s: e.tensor_copy(
                out=VVB.ap()[0:64, s, 0:64], in_=VVF.ap()[0:64, s, 0:64]))
            P.run("vector", lambda e, s=s: e.tensor_copy(
                out=VVB.ap()[64:128, s, 64:128], in_=VVF.ap()[64:128, s, 64:128]))
        P.run("vector", lambda e: e.nop(), inc="vE")
        vvb_vE = P.cnt["vE"]
        tap("vv", VVB.ap()[:, :, :], [128, 8, 128], FP16)
        ckpt("vv_loc0")

        kb = branch_fm(WA, wk_vE, HK, KST[:, :], pre_sE=evac_done_v[1])
        ckpt("k_mm")
        tap("hk", HK.ap()[:, :, :], [128, 8, TOK], FP16)
        ark_ev = allreduce(ark_i.ap(), ark_o.ap(), KST[:, :], ("vE", kb["vE"]), rg8)
        ckpt("ark")

        wp_vE = load_weight(WB, wpT, scale=0.125, pre_tE=qb["tE"])
        ckpt("wp_w")

        P.wait("sync", "vE", ws_free[0])
        P.wait("sync", "vE", ws_free[1])
        P.dma("sync", WS[0:100, 0, 0:4], ldT_d[:, :], "dW0")
        P.dma("sync", WS[0:4, 1, 0:50], luT_d[:, :], "dW1")
        P.wait("vector", "dW0", P.cnt["dW0"])
        P.wait("vector", "dW1", P.cnt["dW1"])
        P.wait("vector", "vE", ws_free[0])
        P.wait("vector", "vE", ws_free[1])
        P.run("vector", lambda e: e.tensor_copy(out=LDTB.ap()[:, :], in_=WS[0:100, 0, 0:4]))
        P.run("vector", lambda e: e.tensor_copy(out=LUTB.ap()[:, :], in_=WS[0:4, 1, 0:50]),
              inc="vE")
        ws_free[0] = ws_free[1] = P.cnt["vE"]
        for c, (dst, src, rows) in enumerate(((DAXB, dax_d, 100), (UAXB, uaxT_d, 50))):
            slot = c % 2
            P.wait("sync", "vE", ws_free[slot])
            P.dma("sync", WS[0:rows, slot, :], src[:, :], f"dW{slot}")
            P.wait("vector", f"dW{slot}", P.cnt[f"dW{slot}"])
            P.wait("vector", "tE", vv_tE)
            P.run("vector", lambda e, dst=dst, slot=slot, rows=rows: e.tensor_copy(
                out=dst.ap()[:, :], in_=WS[0:rows, slot, :]), inc="vE")
            ws_free[slot] = P.cnt["vE"]
        lor_vE = P.cnt["vE"]
        P.wait("tensor", "vE", lor_vE)
        P.wait("tensor", "vE", brep_vE)
        for h in range(2):
            mm(pt[3][0:4, 512 * h:512 * (h + 1)], LDTB.ap()[:, :],
               DAXB.ap()[:, 512 * h:512 * (h + 1)], start=True, stop=True, inc=(h == 1))
        a_tE = P.cnt["tE"]
        P.wait("vector", "tE", a_tE)
        P.run("vector", lambda e: e.tensor_copy(out=AB_.ap()[:, :], in_=pt[3][0:4, :]),
              inc="vE")
        ab_vE = P.cnt["vE"]
        P.wait("tensor", "vE", ab_vE)
        for h in range(2):
            mm(pt[3][0:50, 512 * h:512 * (h + 1)], LUTB.ap()[:, :],
               AB_.ap()[:, 512 * h:512 * (h + 1)], start=True, stop=True, inc=(h == 1))
        b_tE = P.cnt["tE"]
        P.wait("vector", "tE", b_tE)
        P.run("vector", lambda e: e.tensor_copy(out=BB_.ap()[:, :], in_=pt[3][0:50, :]),
              inc="vE")
        bb_vE = P.cnt["vE"]
        ckpt("lab")
        for c in range(DCH):
            slot = c % 2
            P.wait("sync", "vE", ws_free[slot])
            P.dma("sync", WS[:, slot, :], wT[128 * c:128 * (c + 1), :], f"dW{slot}")
            P.wait("tensor", "vE", bb_vE)
            P.wait("tensor", "vE", ws_free[(c + 1) % 2])
            for h in range(2):
                mm(pt[3][:, 512 * h:512 * (h + 1)], BB_.ap()[:, 128 * c:128 * (c + 1)],
                   UAXB.ap()[:, 512 * h:512 * (h + 1)], start=True, stop=True,
                   inc=(h == 1))
            ct_tE = P.cnt["tE"]
            P.wait("vector", "tE", ct_tE)
            P.wait("vector", "tE", kb["tE"])
            P.wait("vector", f"dW{slot}", P.cnt[f"dW{slot}"])
            P.run("vector", lambda e, c=c, slot=slot: e.tensor_tensor(
                out=WA[:, c, :], in0=WS[:, slot, :], in1=pt[3][:, :], op=ALU.add),
                inc="vE")
            ws_free[slot] = P.cnt["vE"]
        weff_vE = P.cnt["vE"]
        ckpt("weff")

        kprep_vE = bn_prep(KST[:, :], GK, BK, KSV[:, :], KBV[:, :], ark_ev)
        qk_ev = {}
        m_ev = {}
        for s in range(DCH):
            P.wait("scalar", "vE", kprep_vE)
            P.run("scalar", lambda e, s=s: e.activation(
                out=KB.ap()[:, s, :], in_=HK.ap()[:, s, :], func=AF.Sigmoid,
                scale=KSV[:, s:s + 1], bias=KBV[:, s:s + 1]), inc="sE")
            ks_sE = P.cnt["sE"]
            P.wait("vector", "sE", ks_sE)
            P.wait("vector", "sE", qsig_sE)
            P.run("vector", lambda e, s=s: e.tensor_tensor(
                out=QK.ap()[:, s, :], in0=QK.ap()[:, s, :], in1=KB.ap()[:, s, :],
                op=ALU.mult), inc="vE")
            qk_ev[s] = P.cnt["vE"]
            if s % 2 == 1:
                for t in (s - 1, s):
                    P.wait("tensor", "vE", vvb_vE)
                    P.wait("tensor", "vE", wp_vE)
                    if t >= 1:
                        P.wait("tensor", "vE", m_ev[t - 1])
                    for h in range(2):
                        mm(pt[2][:, 512 * h:512 * (h + 1)], VVB.ap()[:, t, :],
                           WB[:, t, 512 * h:512 * (h + 1)], start=True, stop=True,
                           inc=(h == 1))
                    mt = P.cnt["tE"]
                    P.wait("vector", "tE", mt)
                    P.run("vector", lambda e, t=t: e.tensor_copy(
                        out=M_sb.ap()[:, t, :], in_=pt[2][:, :]), inc="vE")
                    m_ev[t] = P.cnt["vE"]
        tap("qk", QK.ap()[:, :, :], [128, 8, TOK], FP16)
        tap("M", M_sb.ap()[:, :, :], [128, 8, D], FP16)
        ckpt("m")

        xs_load = {}

        def load_xs(j):
            if j >= 2:
                P.wait("sync", "vE", xprime_ev[j - 2])
            else:
                P.wait("sync", "vE", vvb_vE)
            P.dma("sync", XS[:, j % 2, :], x_in[128 * j:128 * (j + 1), :],
                  f"dXS{j % 2}")

        proj_evac = {}
        xprime_ev = {}
        xp_free = {}
        ffn_evac = {}
        ffn_done_v = {}
        tr2_ev = {}
        ln2_ev = {}
        tr_mm_tE = {}
        wp_emitted = set()
        load_xs(0)
        load_xs(1)

        def group_prep(lmu, lr, lsum, lsq, sl, ready_sE):
            P.wait("vector", "sE", ready_sE)
            P.run("vector", lambda e: e.tensor_scalar(
                out=lmu[:, sl], in0=lsum[:, sl], scalar1=C_LN, scalar2=None,
                op0=ALU.mult))
            P.run("vector", lambda e: e.tensor_scalar(
                out=lr[:, sl], in0=lsq[:, sl], scalar1=C_LN, scalar2=None,
                op0=ALU.mult))
            P.run("vector", lambda e: e.tensor_tensor(
                out=T8A[:, 0:4], in0=lmu[:, sl], in1=lmu[:, sl], op=ALU.mult))
            P.run("vector", lambda e: e.tensor_tensor(
                out=lr[:, sl], in0=lr[:, sl], in1=T8A[:, 0:4], op=ALU.subtract))
            P.run("vector", lambda e: e.tensor_scalar(
                out=lr[:, sl], in0=lr[:, sl], scalar1=EPS, scalar2=None,
                op0=ALU.add), inc="vE")
            P.wait("scalar", "vE", P.cnt["vE"])
            P.run("scalar", lambda e: e.activation(
                out=lr[:, sl], in_=lr[:, sl], func=AF.Sqrt), inc="sE")
            P.wait("vector", "sE", P.cnt["sE"])
            P.run("vector", lambda e: e.reciprocal(out=lr[:, sl], in_=lr[:, sl]),
                  inc="vE")
            return P.cnt["vE"]

        def emit_wp(j):
                wp_emitted.add(j)
                ps = pt[j % 2]
                ckpt(f"wp{j}")
                if j >= 2:
                    P.wait("tensor", "sE", proj_evac[j - 2])
                else:
                    P.wait("tensor", "sE", kb["sE"])
                if j >= 6:
                    P.wait("tensor", "vE", xp_free[j - 6])
                for s in range(DCH):
                    P.wait("tensor", "vE", qk_ev[s])
                    P.wait("tensor", "vE", m_ev[s])
                    for h in range(2):
                        mm(ps[:, 512 * h:512 * (h + 1)],
                           QK.ap()[:, s, 128 * j:128 * (j + 1)],
                           M_sb.ap()[:, s, 512 * h:512 * (h + 1)],
                           start=(s == 0), stop=False)
                P.wait("tensor", "vE", rows_vE)
                for h in range(2):
                    mm(ps[:, 512 * h:512 * (h + 1)], ONE1[0:1, :],
                       ROWP.ap()[0:1, 512 * h:512 * (h + 1)], start=False,
                       stop=True, inc=(h == 1))
                P.wait("scalar", "tE", P.cnt["tE"])
                P.run("scalar", lambda e, ps=ps, j=j: e.activation(
                    out=XP.ap()[:, j % 6, :], in_=ps[:, :], func=AF.Copy,
                    accum_out=LSUM2[:, j:j + 1]), inc="sE")
                proj_evac[j] = P.cnt["sE"]
                P.run("scalar", lambda e, j=j: e.activation(
                    out=JK1[:, :], in_=XP.ap()[:, j % 6, :], func=AF.Square,
                    accum_out=LSQ2[:, j:j + 1]), inc="sE")

        for g in range(4):
            sl = slice(4 * g, 4 * g + 4)
            for j in range(4 * g, 4 * g + 4):
                if j not in wp_emitted:
                    emit_wp(j)
            group_prep(LMU2, LR2, LSUM2, LSQ2, sl, P.cnt["sE"])
            for j in range(4 * g, 4 * g + 4):
                xpj = XP.ap()[:, j % 6, :]
                P.run("vector", lambda e, xpj=xpj, j=j: e.tensor_scalar(
                    out=xpj, in0=xpj, scalar1=LMU2[:, j:j + 1],
                    scalar2=LR2[:, j:j + 1], op0=ALU.subtract, op1=ALU.mult))
                P.wait("vector", f"dXS{j % 2}", P.cnt[f"dXS{j % 2}"])
                P.run("vector", lambda e, xpj=xpj, j=j: e.scalar_tensor_tensor(
                    out=xpj, in0=xpj, scalar=1.0, in1=XS[:, j % 2, :], op0=ALU.mult,
                    op1=ALU.add, accum_out=LSUM3[:, j:j + 1]), inc="vE")
                xprime_ev[j] = P.cnt["vE"]
                if j + 2 < NCH:
                    load_xs(j + 2)
                P.wait("scalar", "vE", xprime_ev[j])
                P.run("scalar", lambda e, xpj=xpj, j=j: e.activation(
                    out=JK2[:, :], in_=xpj, func=AF.Square,
                    accum_out=LSQ3[:, j:j + 1]), inc="sE")
            group_prep(LMU3, LR3, LSUM3, LSQ3, sl, P.cnt["sE"])
            if g >= 1:
                P.wait("vector", "tE", tr_mm_tE[g - 1])
            for j in range(4 * g, 4 * g + 4):
                xpj = XP.ap()[:, j % 6, :]
                P.run("vector", lambda e, xpj=xpj, j=j: e.tensor_scalar(
                    out=RING[:, j % 4, :], in0=xpj, scalar1=LMU3[:, j:j + 1],
                    scalar2=LR3[:, j:j + 1], op0=ALU.subtract, op1=ALU.mult),
                    inc="vE")
                ln2_ev[j] = P.cnt["vE"]
            ln2_vE = P.cnt["vE"]
            for j in (4 * g + 4, 4 * g + 5):
                if j < NCH:
                    emit_wp(j)
            for jj in range(4 * g, 4 * g + 4):
                P.wait("tensor", "vE", ln2_ev[jj])
                if jj % 4 == 0:
                    if g == 0:
                        P.wait("tensor", "vE", weff_vE)
                    else:
                        P.wait("tensor", "sE", ffn_evac[4 * g - 1])
                else:
                    P.wait("tensor", "sE", P.cnt["sE"])
                for c in range(DCH):
                    mm(pt[3][:, 128 * c:128 * (c + 1)],
                       RING[:, jj % 4, 128 * c:128 * (c + 1)], IDM[:, :],
                       start=True, stop=True, inc=(c == DCH - 1))
                P.wait("scalar", "tE", P.cnt["tE"])
                P.run("scalar", lambda e, jj=jj: e.activation(
                    out=LNT.ap()[:, :, jj, :],
                    in_=pt[3][:, :].rearrange("p (c t) -> p c t", c=8),
                    func=AF.Copy), inc="sE")
            tr_mm_tE[g] = P.cnt["tE"]
            tr2_ev[g] = P.cnt["sE"]
            for jj in range(4 * g, 4 * g + 4):
                psf = pt[2 + (jj % 2)]
                P.wait("tensor", "sE", tr2_ev[g])
                P.wait("tensor", "vE", weff_vE)
                P.wait("tensor", "vE", m_ev[7])
                if jj >= 2:
                    P.wait("tensor", "sE", ffn_evac[jj - 2])
                for dc in range(DCH):
                    for h in range(2):
                        mm(psf[:, 512 * h:512 * (h + 1)], LNT.ap()[:, dc, jj, :],
                           WA[:, dc, 512 * h:512 * (h + 1)],
                           start=(dc == 0), stop=False)
                for h in range(2):
                    mm(psf[:, 512 * h:512 * (h + 1)], ONE1[0:1, :],
                       ROWB.ap()[0:1, 512 * h:512 * (h + 1)], start=False,
                       stop=True, inc=(h == 1))
                P.wait("scalar", "tE", P.cnt["tE"])
                if jj >= 4:
                    P.wait("scalar", "vE", ffn_done_v[jj - 4])
                    P.wait("scalar", f"dSt{jj % 4}", P.cnt[f"dSt{jj % 4}"])
                P.run("scalar", lambda e, psf=psf, jj=jj: e.activation(
                    out=FROLL.ap()[:, jj % 4, :], in_=psf[:, :], func=AF.Copy,
                    accum_out=LSUM4[:, jj:jj + 1]), inc="sE")
                ffn_evac[jj] = P.cnt["sE"]
                P.run("scalar", lambda e, jj=jj: e.activation(
                    out=JK1[:, :], in_=FROLL.ap()[:, jj % 4, :], func=AF.Square,
                    accum_out=LSQ4[:, jj:jj + 1]), inc="sE")
            group_prep(LMU4, LR4, LSUM4, LSQ4, sl, P.cnt["sE"])
            for jj in range(4 * g, 4 * g + 4):
                fr = FROLL.ap()[:, jj % 4, :]
                P.wait("vector", "sE", ffn_evac[jj])
                P.run("vector", lambda e, fr=fr, jj=jj: e.tensor_scalar(
                    out=fr, in0=fr, scalar1=LMU4[:, jj:jj + 1],
                    scalar2=LR4[:, jj:jj + 1], op0=ALU.subtract, op1=ALU.mult))
                P.run("vector", lambda e, fr=fr, jj=jj: e.tensor_tensor(
                    out=fr, in0=fr, in1=XP.ap()[:, jj % 6, :], op=ALU.add),
                    inc="vE")
                xp_free[jj] = P.cnt["vE"]
                ffn_done_v[jj] = P.cnt["vE"]
                P.wait("sync", "vE", xp_free[jj])
                P.dma("sync", y_out[128 * jj:128 * (jj + 1), :], fr,
                      f"dSt{jj % 4}")

    except _StopBuild:
        for s in DMA_SEMS + ("vE", "sE", "tE", "cc"):
            P.wait("sync", s, P.cnt[s])
        for j in range(NCH):
            P.dma("sync", y_out[128 * j:128 * (j + 1), :], WS[:, 0, :])
    for cl in DMA_SEMS:
        P.wait("sync", cl, P.cnt[cl])
    P.wait("gpsimd", "dSt3", P.cnt["dSt3"])

    with ctx, nc.Block() as block:
        for eng_name in ENGS:
            fns = P.ops[eng_name]

            def body(e, fns=fns):
                for f in fns:
                    f(e)
            getattr(block, eng_name)(body)

    return nc, tap_done


def kernel(**inputs):
    x = np.ascontiguousarray(np.asarray(inputs["L0_attn_x"], np.float32).reshape(16384, D))

    def T(a):
        return np.ascontiguousarray(np.asarray(a, np.float32).T)

    def chunkvec(v):
        return np.ascontiguousarray(np.asarray(v, np.float32).reshape(8, 128).T)

    common = {
        "wqT": T(inputs["L0_attn_Wq"]),
        "wkT": T(inputs["L0_attn_Wk"]),
        "wvT": T(inputs["L0_attn_Wv"]),
        "wpT": T(inputs["L0_attn_Wp"]),
        "wT": T(inputs["L0_ffn_W"]),
        "gq": chunkvec(inputs["L0_attn_gq"]),
        "bq": chunkvec(inputs["L0_attn_bq"]),
        "gk": chunkvec(inputs["L0_attn_gk"]),
        "bk": chunkvec(inputs["L0_attn_bk"]),
        "gv": np.ascontiguousarray(np.asarray(inputs["L0_attn_gv"], np.float32).reshape(1, D)),
        "bv": np.ascontiguousarray(np.asarray(inputs["L0_attn_bv"], np.float32).reshape(1, D)),
        "bp": np.ascontiguousarray(np.asarray(inputs["L0_attn_bp"], np.float32).reshape(1, D)),
        "bb": np.ascontiguousarray(np.asarray(inputs["L0_ffn_b"], np.float32).reshape(1, D)),
        "ldT": T(inputs["L0_ffn_lora_down"]),
        "luT": T(inputs["L0_ffn_lora_up"]),
        "dax": np.ascontiguousarray(np.asarray(inputs["L0_ffn_down_aux"], np.float32)),
        "uaxT": T(inputs["L0_ffn_up_aux"]),
        "idm": np.eye(128, dtype=np.float32),
    }
    nc, _ = build()
    in_maps = []
    for c in range(NCORE):
        m = dict(common)
        m["x_in"] = np.ascontiguousarray(x[TOK * c:TOK * (c + 1)])
        in_maps.append(m)
    res = run_bass_kernel_spmd(nc, in_maps, core_ids=list(range(NCORE)))
    out = np.concatenate([np.asarray(res.results[c]["y"]) for c in range(NCORE)],
                         axis=0)
    return out.reshape(2, 8192, D).astype(np.float32)


# version 10
# speedup vs baseline: 1.1214x; 1.0170x over previous
import numpy as np
import contextlib
import concourse.bass as bass
import concourse.mybir as mybir
from concourse.bass_utils import run_bass_kernel_spmd
from concourse.replica_groups import maybe_share_collective_output_space

F32 = mybir.dt.float32
BF16 = mybir.dt.bfloat16
FP16 = mybir.dt.float16
AF = mybir.ActivationFunctionType
ALU = mybir.AluOpType
AX = mybir.AxisListType

NCORE = 8
TOK = 2048
NCH = 16
D = 1024
DCH = 8
EPS = 1e-5
C_BN = 1.0 / 16384.0
C_LN = 1.0 / 1024.0

ENGS = ("sync", "scalar", "vector", "tensor", "gpsimd")


class Prog:

    def __init__(self, sems):
        self.ops = {e: [] for e in ENGS}
        self.sems = sems
        self.cnt = {n: 0 for n in sems}
        self.last_wait = {}

    def wait(self, eng, sem, val):
        assert val <= self.cnt[sem], f"wait {sem}>={val} but only {self.cnt[sem]}"
        key = (eng, sem)
        if self.last_wait.get(key, -1) >= val:
            return
        self.last_wait[key] = val
        h = self.sems[sem]
        self.ops[eng].append(lambda e, h=h, v=val: e.wait_ge(h, v))

    AUTO = {"vector": "vE", "scalar": "sE"}

    def run(self, eng, fn, inc=None, n=1):
        own = self.AUTO.get(eng)
        if own is not None:
            self.wait(eng, own, self.cnt[own])
            h = self.sems[own]
            self.cnt[own] += 1
            self.ops[eng].append(lambda e, f=fn, h=h: f(e).then_inc(h, 1))
            return self.cnt[own]
        if inc is None:
            self.ops[eng].append(fn)
            return None
        h = self.sems[inc]
        self.cnt[inc] += n
        self.ops[eng].append(lambda e, f=fn, h=h, n=n: f(e).then_inc(h, n))
        return self.cnt[inc]

    def dma(self, eng, out, in_, sem="dS", **kw):
        return self.run(eng, lambda e: e.dma_start(out=out, in_=in_, **kw),
                        inc=sem, n=16)


class _StopBuild(Exception):
    pass


def build(taps=None, stop=None):
    nc = bass.Bass("TRN2", target_bir_lowering=False)
    tap_list = list(taps or [])
    tap_done = []

    x_in = nc.dram_tensor("x_in", [TOK, D], F32, kind="ExternalInput").ap()
    wqT = nc.dram_tensor("wqT", [D, D], F32, kind="ExternalInput").ap()
    wkT = nc.dram_tensor("wkT", [D, D], F32, kind="ExternalInput").ap()
    wvT = nc.dram_tensor("wvT", [D, D], F32, kind="ExternalInput").ap()
    wpT = nc.dram_tensor("wpT", [D, D], F32, kind="ExternalInput").ap()
    wT = nc.dram_tensor("wT", [D, D], F32, kind="ExternalInput").ap()
    gq_d = nc.dram_tensor("gq", [128, 8], F32, kind="ExternalInput").ap()
    bq_d = nc.dram_tensor("bq", [128, 8], F32, kind="ExternalInput").ap()
    gk_d = nc.dram_tensor("gk", [128, 8], F32, kind="ExternalInput").ap()
    bk_d = nc.dram_tensor("bk", [128, 8], F32, kind="ExternalInput").ap()
    gv_d = nc.dram_tensor("gv", [1, D], F32, kind="ExternalInput").ap()
    bv_d = nc.dram_tensor("bv", [1, D], F32, kind="ExternalInput").ap()
    bp_d = nc.dram_tensor("bp", [1, D], F32, kind="ExternalInput").ap()
    bb_d = nc.dram_tensor("bb", [1, D], F32, kind="ExternalInput").ap()
    ldT_d = nc.dram_tensor("ldT", [100, 4], F32, kind="ExternalInput").ap()
    luT_d = nc.dram_tensor("luT", [4, 50], F32, kind="ExternalInput").ap()
    dax_d = nc.dram_tensor("dax", [100, D], F32, kind="ExternalInput").ap()
    uaxT_d = nc.dram_tensor("uaxT", [50, D], F32, kind="ExternalInput").ap()
    idm_d = nc.dram_tensor("idm", [128, 128], F32, kind="ExternalInput").ap()
    y_out = nc.dram_tensor("y", [TOK, D], F32, kind="ExternalOutput").ap()

    rg8 = [list(range(8))]
    rg4 = [[0, 1, 2, 3], [4, 5, 6, 7]]
    sp8 = maybe_share_collective_output_space("AllReduce", rg8)
    sp4 = maybe_share_collective_output_space("AllReduce", rg4)
    arq_i = nc.dram_tensor("arq_i", [128, 16], F32)
    arq_o = nc.dram_tensor("arq_o", [128, 16], F32, addr_space=sp8)
    ark_i = nc.dram_tensor("ark_i", [128, 16], F32)
    ark_o = nc.dram_tensor("ark_o", [128, 16], F32, addr_space=sp8)
    arv_i = nc.dram_tensor("arv_i", [2, D], F32)
    arv_o = nc.dram_tensor("arv_o", [2, D], F32, addr_space=sp8)
    arw_i = nc.dram_tensor("arw_i", [128, D], F32)
    arw_o = nc.dram_tensor("arw_o", [128, D], F32, addr_space=sp4)

    A0 = (nc.sbuf_base + 31) // 32 * 32
    NEED = 0x33940
    assert A0 + NEED <= nc.sbuf_top, (A0, NEED, nc.sbuf_top)
    nc.alloc_sbuf_tensor("arena", [128, NEED + 64], mybir.dt.uint8)

    def at(name, shape, dtype, off):
        return nc.alloc_sbuf_tensor_at(name, shape, dtype, offset=A0 + off)

    LNT = at("LNT", [128, 8, 16, 128], FP16, 0x00000)
    RING = at("RING", [128, 4, D], FP16, 0x08000)
    SREP = at("SREP", [128, D], FP16, 0x08000)
    BREP = at("BREP", [128, D], FP16, 0x08800)
    HQ = at("HQ", [128, 8, TOK], FP16, 0x0A000)
    QK = at("QK", [128, 8, TOK], FP16, 0x0A000)
    HK = at("HK", [128, 8, TOK], FP16, 0x12000)
    KB = at("KB", [128, 8, TOK], FP16, 0x12000)
    M_sb = at("M_sb", [128, 8, D], FP16, 0x12000)
    FROLL = at("FROLL", [128, 4, D], F32, 0x16000)
    HV = at("HV", [128, NCH, D], FP16, 0x1A000)
    VTOK = at("VTOK", [128, NCH, D], FP16, 0x1A000)
    XS = at("XS", [128, 2, D], F32, 0x1A000)
    VVF = at("VVF", [128, 8, 128], F32, 0x1A000)
    VVB = at("VVB", [128, 8, 128], FP16, 0x1C000)
    DAXB = at("DAXB", [100, D], FP16, 0x1C800)
    UAXB = at("UAXB", [50, D], FP16, 0x1D000)
    AB_ = at("AB_", [4, D], FP16, 0x1D800)
    BB_ = at("BB_", [50, D], FP16, 0x1E000)
    XP = at("XP", [128, 6, D], FP16, 0x22000)
    XI = at("XI", [128, 3, D], F32, 0x22000)
    WA = at("WA", [128, 8, D], FP16, 0x25000)
    WB = at("WB", [128, 8, D], FP16, 0x29000)
    WS = at("WS", [128, 2, D], F32, 0x2D000)
    JK1 = at("JK1", [128, D], FP16, 0x2F000)
    JK2 = at("JK2", [128, D], FP16, 0x2F800)
    VS0 = at("VS0", [1, D], F32, 0x30000)
    VS1 = at("VS1", [1, D], F32, 0x29000)
    VTMP = at("VTMP", [1, D], F32, 0x22000)
    VGAM = at("VGAM", [1, D], F32, 0x23000)
    VBET = at("VBET", [1, D], F32, 0x24000)
    SVB0 = at("SVB0", [1, D], FP16, 0x31000)
    SVB1 = at("SVB1", [1, D], FP16, 0x31800)
    ROWP = at("ROWP", [1, D], FP16, 0x32000)
    ROWB = at("ROWB", [1, D], FP16, 0x32800)
    GQ = at("GQ", [128, 8], F32, 0x33000)
    BQ = at("BQ", [128, 8], F32, 0x33020)
    GK = at("GK", [128, 8], F32, 0x33040)
    BK = at("BK", [128, 8], F32, 0x33060)
    QST = at("QST", [128, 16], F32, 0x33080)
    KST = at("KST", [128, 16], F32, 0x330C0)
    ACC4 = at("ACC4", [128, 8, 4], F32, 0x33100)
    QSV = at("QSV", [128, 8], F32, 0x33180)
    QBV = at("QBV", [128, 8], F32, 0x331A0)
    KSV = at("KSV", [128, 8], F32, 0x331C0)
    KBV = at("KBV", [128, 8], F32, 0x331E0)
    LSUM = at("LSUM", [128, 16], F32, 0x33200)
    LSQ = at("LSQ", [128, 16], F32, 0x33240)
    LMU = at("LMU", [128, 16], F32, 0x33280)
    LR = at("LR", [128, 16], F32, 0x332C0)
    T8A = at("T8A", [128, 8], F32, 0x33300)
    T8B = at("T8B", [128, 8], F32, 0x33320)
    ONE1 = at("ONE1", [1, 128], FP16, 0x33340)
    ONEC = at("ONEC", [128, 1], FP16, 0x33440)
    IDM = at("IDM", [128, 128], FP16, 0x33840)
    LDTB = at("LDTB", [100, 4], FP16, 0x33460)
    LUTB = at("LUTB", [4, 50], FP16, 0x33480)
    LSUM2 = at("LSUM2", [128, 16], F32, 0x33500)
    LSQ2 = at("LSQ2", [128, 16], F32, 0x33540)
    LMU2 = at("LMU2", [128, 16], F32, 0x33580)
    LR2 = at("LR2", [128, 16], F32, 0x335C0)
    LSUM3 = at("LSUM3", [128, 16], F32, 0x33600)
    LSQ3 = at("LSQ3", [128, 16], F32, 0x33640)
    LMU3 = at("LMU3", [128, 16], F32, 0x33680)
    LR3 = at("LR3", [128, 16], F32, 0x336C0)
    LSUM4 = at("LSUM4", [128, 16], F32, 0x33700)
    LSQ4 = at("LSQ4", [128, 16], F32, 0x33740)
    LMU4 = at("LMU4", [128, 16], F32, 0x33780)
    LR4 = at("LR4", [128, 16], F32, 0x337C0)

    ctx = contextlib.ExitStack()
    pt = [ctx.enter_context(nc.psum_tensor(f"pt{i}", [128, D], F32)) for i in range(4)]
    DMA_SEMS = ("dS", "dG", "dT", "dW0", "dW1", "dX0", "dX1", "dX2",
                "dXS0", "dXS1", "dSt0", "dSt1", "dSt2", "dSt3", "dR")
    sems = {n: ctx.enter_context(nc.semaphore(n)) for n in
            DMA_SEMS + ("cc", "tE", "sE", "vE")}
    P = Prog(sems)

    def tap(name, ap, shape, dtype):
        if name not in tap_list:
            return
        for s in DMA_SEMS + ("vE", "sE", "tE"):
            P.wait("sync", s, P.cnt[s])
        t = nc.dram_tensor("tap_" + name, shape, dtype, kind="ExternalOutput").ap()
        P.dma("sync", t[...], ap)
        tap_done.append(name)

    def ckpt(name):
        if stop == name:
            raise _StopBuild

    def mm(psum_ap, lhsT, rhs, start, stop, inc=False):
        return P.run("tensor",
                     lambda e, o=psum_ap, l=lhsT, r=rhs, s1=start, s2=stop:
                     e.matmul(out=o, lhsT=l, rhs=r, start=s1, stop=s2),
                     inc=("tE" if inc else None))

    def lnT_rhs(dc, nt):
        return LNT.ap()[:, dc, 4 * nt:4 * nt + 4, :].rearrange("p a b -> p (a b)")

    try:
        P.run("vector", lambda e: e.memset(ONE1[:, :], 1.0))
        P.run("vector", lambda e: e.memset(ONEC[:, :], 1.0), inc="vE")
        P.dma("sync", FROLL.ap()[:, 0, 0:128], idm_d[:, :], "dR")
        P.wait("vector", "dR", P.cnt["dR"])
        P.run("vector", lambda e: e.tensor_copy(out=IDM[:, :],
                                                in_=FROLL.ap()[:, 0, 0:128]),
              inc="vE")

        for dst, src in ((GQ, gq_d), (BQ, bq_d), (GK, gk_d), (BK, bk_d)):
            P.dma("sync", dst[:, :], src[:, :], "dW0")
        P.dma("sync", WS[0:1, 0, :], bp_d[:, :], "dW0")
        P.dma("sync", WS[0:1, 1, :], bb_d[:, :], "dW0")
        P.wait("vector", "dW0", P.cnt["dW0"])
        P.run("vector", lambda e: e.tensor_copy(out=ROWP.ap()[0:1, :], in_=WS[0:1, 0, :]))
        P.run("vector", lambda e: e.tensor_copy(out=ROWB.ap()[0:1, :], in_=WS[0:1, 1, :]),
              inc="vE")
        rows_vE = P.cnt["vE"]

        ws_free = {0: rows_vE, 1: rows_vE}

        def load_weight(dst, src_dram, scale=None, pre_tE=0):
            for c in range(DCH):
                slot = c % 2
                P.wait("sync", "vE", ws_free[slot])
                P.dma("sync", WS[:, slot, :], src_dram[128 * c:128 * (c + 1), :],
                      f"dW{slot}")
                P.wait("vector", f"dW{slot}", P.cnt[f"dW{slot}"])
                if pre_tE:
                    P.wait("vector", "tE", pre_tE)
                if scale is None:
                    P.run("vector",
                          lambda e, s=slot, c=c: e.tensor_copy(out=dst[:, c, :],
                                                               in_=WS[:, s, :]), inc="vE")
                else:
                    P.run("vector",
                          lambda e, s=slot, c=c: e.tensor_scalar(
                              out=dst[:, c, :], in0=WS[:, s, :], scalar1=scale,
                              scalar2=None, op0=ALU.mult), inc="vE")
                ws_free[slot] = P.cnt["vE"]
            return P.cnt["vE"]

        wq_vE = load_weight(WB, wqT)

        xload = {}
        app_ev = {}
        trj_ev = {}
        tr_ev = {}
        for j in range(3):
            P.dma("sync", XI[:, j, :], x_in[128 * j:128 * (j + 1), :], f"dX{j}")
        for j in range(NCH):
            P.wait("vector", f"dX{j % 3}", P.cnt[f"dX{j % 3}"])
            xc = XI[:, j % 3, :]
            P.run("vector", lambda e, xc=xc, j=j: e.tensor_scalar(
                out=JK1[:, :], in0=xc, scalar1=1.0, scalar2=0.0, op0=ALU.mult,
                op1=ALU.add, accum_out=LSUM[:, j:j + 1]))
            P.run("vector", lambda e, xc=xc, j=j: e.scalar_tensor_tensor(
                out=JK1[:, :], in0=xc, scalar=1.0, in1=xc, op0=ALU.mult,
                op1=ALU.mult, accum_out=LSQ[:, j:j + 1]), inc="vE")
            P.run("vector", lambda e, j=j: e.tensor_scalar(
                out=LMU[:, j:j + 1], in0=LSUM[:, j:j + 1], scalar1=C_LN, scalar2=None,
                op0=ALU.mult))
            P.run("vector", lambda e, j=j: e.tensor_scalar(
                out=LR[:, j:j + 1], in0=LSQ[:, j:j + 1], scalar1=C_LN, scalar2=None,
                op0=ALU.mult))
            P.run("vector", lambda e, j=j: e.tensor_tensor(
                out=T8A[:, 0:1], in0=LMU[:, j:j + 1], in1=LMU[:, j:j + 1], op=ALU.mult))
            P.run("vector", lambda e, j=j: e.tensor_tensor(
                out=LR[:, j:j + 1], in0=LR[:, j:j + 1], in1=T8A[:, 0:1],
                op=ALU.subtract))
            P.run("vector", lambda e, j=j: e.tensor_scalar(
                out=LR[:, j:j + 1], in0=LR[:, j:j + 1], scalar1=EPS, scalar2=None,
                op0=ALU.add), inc="vE")
            P.wait("scalar", "vE", P.cnt["vE"])
            P.run("scalar", lambda e, j=j: e.activation(
                out=LR[:, j:j + 1], in_=LR[:, j:j + 1], func=AF.Sqrt), inc="sE")
            P.wait("vector", "sE", P.cnt["sE"])
            P.run("vector", lambda e, j=j: e.reciprocal(
                out=LR[:, j:j + 1], in_=LR[:, j:j + 1]))
            if j >= 4:
                P.wait("vector", "dT", trj_ev[j - 4])
            P.run("vector", lambda e, xc=xc, j=j: e.tensor_scalar(
                out=RING[:, j % 4, :], in0=xc, scalar1=LMU[:, j:j + 1],
                scalar2=LR[:, j:j + 1], op0=ALU.subtract, op1=ALU.mult), inc="vE")
            app_ev[j] = P.cnt["vE"]
            P.wait("sync", "vE", app_ev[j])
            for cl in DMA_SEMS:
                P.wait("sync", cl, P.cnt[cl])
            P.dma("sync", LNT.ap()[:, :, j, :], RING[:, j % 4, :], "dT", transpose=True)
            trj_ev[j] = P.cnt["dT"]
            P.wait("sync", "dT", P.cnt["dT"])
            if j % 4 == 3:
                tr_ev[j // 4] = P.cnt["dT"]
            if j + 3 < NCH:
                P.wait("sync", "vE", app_ev[j])
                P.dma("sync", XI[:, (j + 3) % 3, :],
                      x_in[128 * (j + 3):128 * (j + 4), :], f"dX{j % 3}")

        tap("ln1T", LNT.ap()[:, :, :, :], [128, 8, 16, 128], FP16)
        ckpt("ln1")

        def branch_fm(w_sb, w_vE, h_sb, stat_sb, pre_sE):
            regions = [(pt[0][:, 0:512], 0), (pt[0][:, 512:1024], 1),
                       (pt[1][:, 0:512], 2), (pt[1][:, 512:1024], 3)]
            evac_done = {}
            k = 0
            for ec in range(DCH):
                for nt in range(4):
                    ps, ridx = regions[k % 4]
                    if k >= 4:
                        P.wait("tensor", "sE", evac_done[ridx])
                    elif pre_sE:
                        P.wait("tensor", "sE", pre_sE)
                    P.wait("tensor", "vE", w_vE)
                    P.wait("tensor", "dT", tr_ev[nt])
                    for dc in range(DCH):
                        mm(ps, w_sb[:, dc, 128 * ec:128 * (ec + 1)], lnT_rhs(dc, nt),
                           start=(dc == 0), stop=(dc == DCH - 1), inc=(dc == DCH - 1))
                    mm_ev = P.cnt["tE"]
                    P.wait("scalar", "tE", mm_ev)
                    P.run("scalar", lambda e, ps=ps, ec=ec, nt=nt: e.activation(
                        out=h_sb.ap()[:, ec, 512 * nt:512 * (nt + 1)], in_=ps,
                        func=AF.Copy, accum_out=ACC4[:, ec, nt:nt + 1]), inc="sE")
                    evac_done[ridx] = P.cnt["sE"]
                    k += 1
                P.wait("vector", "sE", P.cnt["sE"])
                P.run("vector", lambda e, ec=ec: e.scalar_tensor_tensor(
                    out=JK2[:, :], in0=h_sb.ap()[:, ec, 0:1024], scalar=1.0,
                    in1=h_sb.ap()[:, ec, 0:1024], op0=ALU.mult, op1=ALU.mult,
                    accum_out=T8A[:, ec:ec + 1]))
                P.run("vector", lambda e, ec=ec: e.scalar_tensor_tensor(
                    out=JK2[:, :], in0=h_sb.ap()[:, ec, 1024:2048], scalar=1.0,
                    in1=h_sb.ap()[:, ec, 1024:2048], op0=ALU.mult, op1=ALU.mult,
                    accum_out=T8B[:, ec:ec + 1]), inc="vE")
            P.run("vector", lambda e: e.tensor_reduce(
                out=stat_sb[:, 0:8], in_=ACC4[:, :, :], axis=AX.X, op=ALU.add))
            P.run("vector", lambda e: e.tensor_tensor(
                out=stat_sb[:, 8:16], in0=T8A[:, :], in1=T8B[:, :], op=ALU.add), inc="vE")
            return {"vE": P.cnt["vE"], "tE": P.cnt["tE"], "sE": P.cnt["sE"]}

        def allreduce(in_ap, out_ap, sb_ap, ready, rg):
            P.wait("gpsimd", ready[0], ready[1])
            P.wait("gpsimd", "dT", P.cnt["dT"])
            P.dma("gpsimd", in_ap, sb_ap, "dG")
            P.wait("gpsimd", "dG", P.cnt["dG"])
            P.run("gpsimd", lambda e, i=in_ap, o=out_ap, rg=rg: e.collective_compute(
                "AllReduce", ALU.add, replica_groups=rg, ins=[i], outs=[o]), inc="cc")
            P.wait("gpsimd", "cc", P.cnt["cc"])
            P.dma("gpsimd", sb_ap, out_ap, "dG")
            P.wait("gpsimd", "dG", P.cnt["dG"])
            return ("dG", P.cnt["dG"])

        def bn_prep(stat, g_sb, b_sb, sv, bv, ar_ev):
            P.wait("vector", ar_ev[0], ar_ev[1])
            P.run("vector", lambda e: e.tensor_scalar(
                out=T8A[:, :], in0=stat[:, 0:8], scalar1=C_BN, scalar2=None, op0=ALU.mult))
            P.run("vector", lambda e: e.tensor_scalar(
                out=T8B[:, :], in0=stat[:, 8:16], scalar1=C_BN, scalar2=None, op0=ALU.mult))
            P.run("vector", lambda e: e.tensor_tensor(
                out=sv, in0=T8A[:, :], in1=T8A[:, :], op=ALU.mult))
            P.run("vector", lambda e: e.tensor_tensor(
                out=T8B[:, :], in0=T8B[:, :], in1=sv, op=ALU.subtract))
            P.run("vector", lambda e: e.tensor_scalar(
                out=sv, in0=T8B[:, :], scalar1=EPS, scalar2=None, op0=ALU.add), inc="vE")
            P.wait("scalar", "vE", P.cnt["vE"])
            P.run("scalar", lambda e: e.activation(out=sv, in_=sv, func=AF.Sqrt), inc="sE")
            P.wait("vector", "sE", P.cnt["sE"])
            P.run("vector", lambda e: e.reciprocal(out=sv, in_=sv))
            P.run("vector", lambda e: e.tensor_tensor(
                out=sv, in0=sv, in1=g_sb[:, :], op=ALU.mult))
            P.run("vector", lambda e: e.tensor_scalar(
                out=sv, in0=sv, scalar1=4.0, scalar2=None, op0=ALU.mult))
            P.run("vector", lambda e: e.tensor_tensor(
                out=bv, in0=T8A[:, :], in1=sv, op=ALU.mult))
            P.run("vector", lambda e: e.tensor_scalar(
                out=T8B[:, :], in0=b_sb[:, :], scalar1=4.0, scalar2=None, op0=ALU.mult))
            P.run("vector", lambda e: e.tensor_tensor(
                out=T8B[:, :], in0=T8B[:, :], in1=bv, op=ALU.subtract))
            P.run("vector", lambda e: e.tensor_scalar(
                out=bv, in0=T8B[:, :], scalar1=-4.0, scalar2=None, op0=ALU.add), inc="vE")
            return P.cnt["vE"]

        qb = branch_fm(WB, wq_vE, HQ, QST[:, :], pre_sE=0)
        tap("hq", HQ.ap()[:, :, :], [128, 8, TOK], FP16)
        tap("qstat_loc", QST[:, :], [128, 16], F32)
        ckpt("q_mm")
        arq_ev = allreduce(arq_i.ap(), arq_o.ap(), QST[:, :], ("vE", qb["vE"]), rg8)
        if "qstat_g" in tap_list:
            P.wait("sync", arq_ev[0], arq_ev[1])
        tap("qstat_g", QST[:, :], [128, 16], F32)
        ckpt("arq")

        wv_vE = load_weight(WA, wvT)

        vprod = [pt[0], pt[1]]
        evac_done_v = {}
        sqmm_ev = {}
        for j in range(NCH):
            ps = vprod[j % 2]
            if j >= 2:
                P.wait("tensor", "sE", evac_done_v[j % 2])
            else:
                P.wait("tensor", "sE", qb["sE"])
            P.wait("tensor", "vE", wv_vE)
            P.wait("tensor", "dT", tr_ev[j // 4])
            for dc in range(DCH):
                for h in range(2):
                    mm(ps[:, 512 * h:512 * (h + 1)], LNT.ap()[:, dc, j, :],
                       WA[:, dc, 512 * h:512 * (h + 1)],
                       start=(dc == 0), stop=(dc == DCH - 1),
                       inc=(dc == DCH - 1 and h == 1))
            mm_ev = P.cnt["tE"]
            P.wait("scalar", "tE", mm_ev)
            P.run("scalar", lambda e, ps=ps, j=j: e.activation(
                out=HV.ap()[:, j, :], in_=ps[:, :], func=AF.Copy), inc="sE")
            evac_done_v[j % 2] = P.cnt["sE"]
            hv_ev = P.cnt["sE"]
            jk = JK1 if j % 2 == 0 else JK2
            if j >= 2:
                P.wait("vector", "tE", sqmm_ev[j - 2])
            P.wait("vector", "sE", hv_ev)
            P.run("vector", lambda e, j=j, jk=jk: e.scalar_tensor_tensor(
                out=jk[:, :], in0=HV.ap()[:, j, :], scalar=1.0,
                in1=HV.ap()[:, j, :], op0=ALU.mult, op1=ALU.mult), inc="vE")
            sq_ev = P.cnt["vE"]
            P.wait("tensor", "sE", hv_ev)
            for h in range(2):
                mm(pt[2][0:1, 512 * h:512 * (h + 1)], ONEC[:, 0:1],
                   HV.ap()[:, j, 512 * h:512 * (h + 1)],
                   start=(j == 0), stop=(j == NCH - 1))
            P.wait("tensor", "vE", sq_ev)
            for h in range(2):
                mm(pt[3][0:1, 512 * h:512 * (h + 1)], ONEC[:, 0:1],
                   jk[:, 512 * h:512 * (h + 1)],
                   start=(j == 0), stop=(j == NCH - 1), inc=(h == 1))
            sqmm_ev[j] = P.cnt["tE"]
        v_mm_tE = P.cnt["tE"]
        P.wait("vector", "tE", v_mm_tE)
        P.run("vector", lambda e: e.tensor_copy(out=VS0.ap()[0:1, :], in_=pt[2][0:1, :]))
        P.run("vector", lambda e: e.tensor_copy(out=VS1.ap()[0:1, :], in_=pt[3][0:1, :]),
              inc="vE")
        vstat_vE = P.cnt["vE"]
        ckpt("v_mm")
        tap("hv", HV.ap()[:, :, :], [128, NCH, D], FP16)
        tap("vstat_loc", VS0.ap()[0:1, :], [1, D], F32)
        P.wait("gpsimd", "vE", vstat_vE)
        P.wait("gpsimd", "dT", P.cnt["dT"])
        P.dma("gpsimd", arv_i.ap()[0:1, :], VS0.ap()[0:1, :], "dG")
        P.dma("gpsimd", arv_i.ap()[1:2, :], VS1.ap()[0:1, :], "dG")
        P.wait("gpsimd", "dG", P.cnt["dG"])
        P.run("gpsimd", lambda e: e.collective_compute(
            "AllReduce", ALU.add, replica_groups=rg8, ins=[arv_i.ap()],
            outs=[arv_o.ap()]), inc="cc")
        P.wait("gpsimd", "cc", P.cnt["cc"])
        P.dma("gpsimd", VS0.ap()[0:1, :], arv_o.ap()[0:1, :], "dG")
        P.dma("gpsimd", VS1.ap()[0:1, :], arv_o.ap()[1:2, :], "dG")
        P.wait("gpsimd", "dG", P.cnt["dG"])
        arv_ev = ("dG", P.cnt["dG"])
        ckpt("arv")

        wk_vE = load_weight(WA, wkT, pre_tE=v_mm_tE)

        qprep_vE = bn_prep(QST[:, :], GQ, BQ, QSV[:, :], QBV[:, :], arq_ev)
        tap("qsv", QSV[:, :], [128, 8], F32)
        tap("qbv", QBV[:, :], [128, 8], F32)
        for ec in range(DCH):
            P.wait("scalar", "vE", qprep_vE)
            P.run("scalar", lambda e, ec=ec: e.activation(
                out=QK.ap()[:, ec, :], in_=HQ.ap()[:, ec, :], func=AF.Sigmoid,
                scale=QSV[:, ec:ec + 1], bias=QBV[:, ec:ec + 1]), inc="sE")
        qsig_sE = P.cnt["sE"]
        qsig_vE = P.cnt["vE"]
        tap("q", QK.ap()[:, :, :], [128, 8, TOK], FP16)
        ckpt("qsig")

        P.wait("sync", "vE", app_ev[NCH - 1])
        P.dma("sync", VGAM.ap()[0:1, :], gv_d[:, :], "dR")
        P.dma("sync", VBET.ap()[0:1, :], bv_d[:, :], "dR")
        P.wait("vector", arv_ev[0], arv_ev[1])
        P.wait("vector", "dR", P.cnt["dR"])
        vs0 = VS0.ap()[0:1, :]
        vs1 = VS1.ap()[0:1, :]
        vt = VTMP.ap()[0:1, :]
        P.run("vector", lambda e: e.tensor_scalar(
            out=vs0, in0=vs0, scalar1=C_BN, scalar2=None, op0=ALU.mult))
        P.run("vector", lambda e: e.tensor_scalar(
            out=vs1, in0=vs1, scalar1=C_BN, scalar2=None, op0=ALU.mult))
        P.run("vector", lambda e: e.tensor_tensor(
            out=vt, in0=vs0, in1=vs0, op=ALU.mult))
        P.run("vector", lambda e: e.tensor_tensor(
            out=vs1, in0=vs1, in1=vt, op=ALU.subtract))
        P.run("vector", lambda e: e.tensor_scalar(
            out=vs1, in0=vs1, scalar1=EPS, scalar2=None, op0=ALU.add), inc="vE")
        P.wait("scalar", "vE", P.cnt["vE"])
        P.run("scalar", lambda e: e.activation(out=vs1, in_=vs1, func=AF.Sqrt), inc="sE")
        P.wait("vector", "sE", P.cnt["sE"])
        P.run("vector", lambda e: e.reciprocal(out=vs1, in_=vs1))
        P.run("vector", lambda e: e.tensor_tensor(
            out=vs1, in0=vs1, in1=VGAM.ap()[0:1, :], op=ALU.mult))
        P.run("vector", lambda e: e.tensor_scalar(
            out=vs1, in0=vs1, scalar1=4.0, scalar2=None, op0=ALU.mult))
        P.run("vector", lambda e: e.tensor_tensor(
            out=vt, in0=vs0, in1=vs1, op=ALU.mult))
        P.run("vector", lambda e: e.tensor_scalar(
            out=VBET.ap()[0:1, :], in0=VBET.ap()[0:1, :], scalar1=4.0,
            scalar2=None, op0=ALU.mult))
        P.run("vector", lambda e: e.tensor_tensor(
            out=vt, in0=vt, in1=VBET.ap()[0:1, :], op=ALU.subtract))
        P.run("vector", lambda e: e.tensor_scalar(
            out=vt, in0=vt, scalar1=-1.0, scalar2=-4.0, op0=ALU.mult,
            op1=ALU.add))
        P.run("vector", lambda e: e.tensor_copy(out=SVB0.ap()[0:1, :], in_=vs1))
        P.run("vector", lambda e: e.tensor_copy(out=SVB1.ap()[0:1, :], in_=vt),
              inc="vE")
        svb_vE = P.cnt["vE"]
        ckpt("vprep")
        P.wait("tensor", "vE", svb_vE)
        P.wait("tensor", "vE", vstat_vE)
        for h in range(2):
            mm(pt[3][:, 512 * h:512 * (h + 1)], ONE1[0:1, :],
               SVB0.ap()[0:1, 512 * h:512 * (h + 1)], start=True, stop=True,
               inc=(h == 1))
        srep_tE = P.cnt["tE"]
        P.wait("vector", "tE", srep_tE)
        P.wait("vector", "dT", tr_ev[3])
        P.run("vector", lambda e: e.tensor_copy(out=SREP[:, :], in_=pt[3][:, :]), inc="vE")
        srep_vE = P.cnt["vE"]
        P.wait("tensor", "vE", srep_vE)
        for h in range(2):
            mm(pt[3][:, 512 * h:512 * (h + 1)], ONE1[0:1, :],
               SVB1.ap()[0:1, 512 * h:512 * (h + 1)], start=True, stop=True,
               inc=(h == 1))
        brep_tE = P.cnt["tE"]
        P.wait("vector", "tE", brep_tE)
        P.run("vector", lambda e: e.tensor_copy(out=BREP[:, :], in_=pt[3][:, :]), inc="vE")
        brep_vE = P.cnt["vE"]
        ckpt("brep")

        for j in range(NCH):
            P.wait("vector", "vE", brep_vE)
            P.run("vector", lambda e, j=j: e.tensor_tensor(
                out=HV.ap()[:, j, :], in0=HV.ap()[:, j, :], in1=SREP[:, :],
                op=ALU.mult))
            P.run("vector", lambda e, j=j: e.tensor_tensor(
                out=HV.ap()[:, j, :], in0=HV.ap()[:, j, :], in1=BREP[:, :],
                op=ALU.add), inc="vE")
            P.wait("scalar", "vE", P.cnt["vE"])
            P.run("scalar", lambda e, j=j: e.activation(
                out=VTOK.ap()[:, j, :], in_=HV.ap()[:, j, :], func=AF.Sigmoid),
                inc="sE")
        vsig_sE = P.cnt["sE"]
        ckpt("vsig")
        tap("v", VTOK.ap()[:, :, :], [128, NCH, D], FP16)

        P.wait("tensor", "sE", vsig_sE)
        P.wait("tensor", "vE", brep_vE)
        for s in range(DCH):
            for j in range(NCH):
                mm(pt[2][:, 128 * s:128 * (s + 1)],
                   VTOK.ap()[:, j, 128 * s:128 * (s + 1)],
                   VTOK.ap()[:, j, 128 * s:128 * (s + 1)],
                   start=(j == 0), stop=(j == NCH - 1), inc=(j == NCH - 1))
        vv_tE = P.cnt["tE"]
        P.wait("vector", "tE", vv_tE)
        P.run("vector", lambda e: e.tensor_copy(
            out=VVF.ap().rearrange("p a b -> p (a b)"), in_=pt[2][:, :]), inc="vE")
        vv_loc_vE = P.cnt["vE"]
        tap("vv_loc", VVF.ap()[:, :, :], [128, 8, 128], F32)
        arw_ev = allreduce(arw_i.ap(), arw_o.ap(),
                           VVF.ap().rearrange("p a b -> p (a b)"),
                           ("vE", vv_loc_vE), rg4)
        P.wait("vector", arw_ev[0], arw_ev[1])
        P.run("vector", lambda e: e.memset(VVB.ap().rearrange("p a b -> p (a b)"), 0.0))
        for s in range(DCH):
            P.run("vector", lambda e, s=s: e.tensor_copy(
                out=VVB.ap()[0:64, s, 0:64], in_=VVF.ap()[0:64, s, 0:64]))
            P.run("vector", lambda e, s=s: e.tensor_copy(
                out=VVB.ap()[64:128, s, 64:128], in_=VVF.ap()[64:128, s, 64:128]))
        P.run("vector", lambda e: e.nop(), inc="vE")
        vvb_vE = P.cnt["vE"]
        tap("vv", VVB.ap()[:, :, :], [128, 8, 128], FP16)
        ckpt("vv_loc0")

        kb = branch_fm(WA, wk_vE, HK, KST[:, :], pre_sE=evac_done_v[1])
        ckpt("k_mm")
        tap("hk", HK.ap()[:, :, :], [128, 8, TOK], FP16)
        ark_ev = allreduce(ark_i.ap(), ark_o.ap(), KST[:, :], ("vE", kb["vE"]), rg8)
        ckpt("ark")

        wp_vE = load_weight(WB, wpT, scale=0.125, pre_tE=qb["tE"])
        ckpt("wp_w")

        P.wait("sync", "vE", ws_free[0])
        P.wait("sync", "vE", ws_free[1])
        P.dma("sync", WS[0:100, 0, 0:4], ldT_d[:, :], "dW0")
        P.dma("sync", WS[0:4, 1, 0:50], luT_d[:, :], "dW1")
        P.wait("vector", "dW0", P.cnt["dW0"])
        P.wait("vector", "dW1", P.cnt["dW1"])
        P.wait("vector", "vE", ws_free[0])
        P.wait("vector", "vE", ws_free[1])
        P.run("vector", lambda e: e.tensor_copy(out=LDTB.ap()[:, :], in_=WS[0:100, 0, 0:4]))
        P.run("vector", lambda e: e.tensor_copy(out=LUTB.ap()[:, :], in_=WS[0:4, 1, 0:50]),
              inc="vE")
        ws_free[0] = ws_free[1] = P.cnt["vE"]
        for c, (dst, src, rows) in enumerate(((DAXB, dax_d, 100), (UAXB, uaxT_d, 50))):
            slot = c % 2
            P.wait("sync", "vE", ws_free[slot])
            P.dma("sync", WS[0:rows, slot, :], src[:, :], f"dW{slot}")
            P.wait("vector", f"dW{slot}", P.cnt[f"dW{slot}"])
            P.wait("vector", "tE", vv_tE)
            P.run("vector", lambda e, dst=dst, slot=slot, rows=rows: e.tensor_copy(
                out=dst.ap()[:, :], in_=WS[0:rows, slot, :]), inc="vE")
            ws_free[slot] = P.cnt["vE"]
        lor_vE = P.cnt["vE"]
        P.wait("tensor", "vE", lor_vE)
        P.wait("tensor", "vE", brep_vE)
        for h in range(2):
            mm(pt[3][0:4, 512 * h:512 * (h + 1)], LDTB.ap()[:, :],
               DAXB.ap()[:, 512 * h:512 * (h + 1)], start=True, stop=True, inc=(h == 1))
        a_tE = P.cnt["tE"]
        P.wait("vector", "tE", a_tE)
        P.run("vector", lambda e: e.tensor_copy(out=AB_.ap()[:, :], in_=pt[3][0:4, :]),
              inc="vE")
        ab_vE = P.cnt["vE"]
        P.wait("tensor", "vE", ab_vE)
        for h in range(2):
            mm(pt[3][0:50, 512 * h:512 * (h + 1)], LUTB.ap()[:, :],
               AB_.ap()[:, 512 * h:512 * (h + 1)], start=True, stop=True, inc=(h == 1))
        b_tE = P.cnt["tE"]
        P.wait("vector", "tE", b_tE)
        P.run("vector", lambda e: e.tensor_copy(out=BB_.ap()[:, :], in_=pt[3][0:50, :]),
              inc="vE")
        bb_vE = P.cnt["vE"]
        ckpt("lab")
        for c in range(DCH):
            slot = c % 2
            P.wait("sync", "vE", ws_free[slot])
            P.dma("sync", WS[:, slot, :], wT[128 * c:128 * (c + 1), :], f"dW{slot}")
            P.wait("tensor", "vE", bb_vE)
            P.wait("tensor", "vE", ws_free[(c + 1) % 2])
            for h in range(2):
                mm(pt[3][:, 512 * h:512 * (h + 1)], BB_.ap()[:, 128 * c:128 * (c + 1)],
                   UAXB.ap()[:, 512 * h:512 * (h + 1)], start=True, stop=True,
                   inc=(h == 1))
            ct_tE = P.cnt["tE"]
            P.wait("vector", "tE", ct_tE)
            P.wait("vector", "tE", kb["tE"])
            P.wait("vector", f"dW{slot}", P.cnt[f"dW{slot}"])
            P.run("vector", lambda e, c=c, slot=slot: e.tensor_tensor(
                out=WA[:, c, :], in0=WS[:, slot, :], in1=pt[3][:, :], op=ALU.add),
                inc="vE")
            ws_free[slot] = P.cnt["vE"]
        weff_vE = P.cnt["vE"]
        ckpt("weff")

        kprep_vE = bn_prep(KST[:, :], GK, BK, KSV[:, :], KBV[:, :], ark_ev)
        qk_ev = {}
        m_ev = {}
        for s in range(DCH):
            P.wait("scalar", "vE", kprep_vE)
            P.run("scalar", lambda e, s=s: e.activation(
                out=KB.ap()[:, s, :], in_=HK.ap()[:, s, :], func=AF.Sigmoid,
                scale=KSV[:, s:s + 1], bias=KBV[:, s:s + 1]), inc="sE")
            ks_sE = P.cnt["sE"]
            P.wait("vector", "sE", ks_sE)
            P.wait("vector", "sE", qsig_sE)
            P.run("vector", lambda e, s=s: e.tensor_tensor(
                out=QK.ap()[:, s, :], in0=QK.ap()[:, s, :], in1=KB.ap()[:, s, :],
                op=ALU.mult), inc="vE")
            qk_ev[s] = P.cnt["vE"]
            if s % 2 == 1:
                for t in (s - 1, s):
                    P.wait("tensor", "vE", vvb_vE)
                    P.wait("tensor", "vE", wp_vE)
                    if t >= 1:
                        P.wait("tensor", "vE", m_ev[t - 1])
                    for h in range(2):
                        mm(pt[2][:, 512 * h:512 * (h + 1)], VVB.ap()[:, t, :],
                           WB[:, t, 512 * h:512 * (h + 1)], start=True, stop=True,
                           inc=(h == 1))
                    mt = P.cnt["tE"]
                    P.wait("vector", "tE", mt)
                    P.run("vector", lambda e, t=t: e.tensor_copy(
                        out=M_sb.ap()[:, t, :], in_=pt[2][:, :]), inc="vE")
                    m_ev[t] = P.cnt["vE"]
        tap("qk", QK.ap()[:, :, :], [128, 8, TOK], FP16)
        tap("M", M_sb.ap()[:, :, :], [128, 8, D], FP16)
        ckpt("m")

        xs_load = {}

        def load_xs(j):
            if j >= 2:
                P.wait("sync", "vE", xprime_ev[j - 2])
            else:
                P.wait("sync", "vE", vvb_vE)
            P.dma("sync", XS[:, j % 2, :], x_in[128 * j:128 * (j + 1), :],
                  f"dXS{j % 2}")

        proj_evac = {}
        xprime_ev = {}
        xp_free = {}
        ffn_evac = {}
        ffn_done_v = {}
        ln2_ev = {}
        tr_mm_tE = {}
        trev_chunk = {}
        wp_emitted = set()
        load_xs(0)
        load_xs(1)

        def group_prep(lmu, lr, lsum, lsq, sl, ready_sE):
            P.wait("vector", "sE", ready_sE)
            P.run("vector", lambda e: e.tensor_scalar(
                out=lmu[:, sl], in0=lsum[:, sl], scalar1=C_LN, scalar2=None,
                op0=ALU.mult))
            P.run("vector", lambda e: e.tensor_scalar(
                out=lr[:, sl], in0=lsq[:, sl], scalar1=C_LN, scalar2=None,
                op0=ALU.mult))
            P.run("vector", lambda e: e.tensor_tensor(
                out=T8A[:, 0:4], in0=lmu[:, sl], in1=lmu[:, sl], op=ALU.mult))
            P.run("vector", lambda e: e.tensor_tensor(
                out=lr[:, sl], in0=lr[:, sl], in1=T8A[:, 0:4], op=ALU.subtract))
            P.run("vector", lambda e: e.tensor_scalar(
                out=lr[:, sl], in0=lr[:, sl], scalar1=EPS, scalar2=None,
                op0=ALU.add), inc="vE")
            P.wait("scalar", "vE", P.cnt["vE"])
            P.run("scalar", lambda e: e.activation(
                out=lr[:, sl], in_=lr[:, sl], func=AF.Sqrt), inc="sE")
            P.wait("vector", "sE", P.cnt["sE"])
            P.run("vector", lambda e: e.reciprocal(out=lr[:, sl], in_=lr[:, sl]),
                  inc="vE")
            return P.cnt["vE"]

        def emit_wp(j):
                wp_emitted.add(j)
                ps = pt[j % 2]
                ckpt(f"wp{j}")
                if j >= 2:
                    P.wait("tensor", "sE", proj_evac[j - 2])
                else:
                    P.wait("tensor", "sE", kb["sE"])
                if j >= 6:
                    P.wait("tensor", "vE", xp_free[j - 6])
                    P.wait("tensor", "vE",
                           trev_chunk[j - 4] if j % 4 >= 2 else trev_chunk[j - 6])
                for s in range(DCH):
                    P.wait("tensor", "vE", qk_ev[s])
                    P.wait("tensor", "vE", m_ev[s])
                    for h in range(2):
                        mm(ps[:, 512 * h:512 * (h + 1)],
                           QK.ap()[:, s, 128 * j:128 * (j + 1)],
                           M_sb.ap()[:, s, 512 * h:512 * (h + 1)],
                           start=(s == 0), stop=False)
                P.wait("tensor", "vE", rows_vE)
                for h in range(2):
                    mm(ps[:, 512 * h:512 * (h + 1)], ONE1[0:1, :],
                       ROWP.ap()[0:1, 512 * h:512 * (h + 1)], start=False,
                       stop=True, inc=(h == 1))
                P.wait("scalar", "tE", P.cnt["tE"])
                P.run("scalar", lambda e, ps=ps, j=j: e.activation(
                    out=XP.ap()[:, j % 6, :], in_=ps[:, :], func=AF.Copy,
                    accum_out=LSUM2[:, j:j + 1]), inc="sE")
                proj_evac[j] = P.cnt["sE"]
                P.run("scalar", lambda e, j=j: e.activation(
                    out=JK1[:, :], in_=XP.ap()[:, j % 6, :], func=AF.Square,
                    accum_out=LSQ2[:, j:j + 1]), inc="sE")

        for g in range(4):
            sl = slice(4 * g, 4 * g + 4)
            for j in range(4 * g, 4 * g + 4):
                if j not in wp_emitted:
                    emit_wp(j)
            group_prep(LMU2, LR2, LSUM2, LSQ2, sl, P.cnt["sE"])
            for j in range(4 * g, 4 * g + 4):
                xpj = XP.ap()[:, j % 6, :]
                P.run("vector", lambda e, xpj=xpj, j=j: e.tensor_scalar(
                    out=xpj, in0=xpj, scalar1=LMU2[:, j:j + 1],
                    scalar2=LR2[:, j:j + 1], op0=ALU.subtract, op1=ALU.mult))
                P.wait("vector", f"dXS{j % 2}", P.cnt[f"dXS{j % 2}"])
                P.run("vector", lambda e, xpj=xpj, j=j: e.scalar_tensor_tensor(
                    out=xpj, in0=xpj, scalar=1.0, in1=XS[:, j % 2, :], op0=ALU.mult,
                    op1=ALU.add, accum_out=LSUM3[:, j:j + 1]), inc="vE")
                xprime_ev[j] = P.cnt["vE"]
                if j + 2 < NCH:
                    load_xs(j + 2)
                P.wait("scalar", "vE", xprime_ev[j])
                P.run("scalar", lambda e, xpj=xpj, j=j: e.activation(
                    out=JK2[:, :], in_=xpj, func=AF.Square,
                    accum_out=LSQ3[:, j:j + 1]), inc="sE")
            group_prep(LMU3, LR3, LSUM3, LSQ3, sl, P.cnt["sE"])
            if g >= 1:
                P.wait("vector", "tE", tr_mm_tE[g - 1])
            for j in range(4 * g, 4 * g + 4):
                xpj = XP.ap()[:, j % 6, :]
                P.run("vector", lambda e, xpj=xpj, j=j: e.tensor_scalar(
                    out=RING[:, j % 4, :], in0=xpj, scalar1=LMU3[:, j:j + 1],
                    scalar2=LR3[:, j:j + 1], op0=ALU.subtract, op1=ALU.mult),
                    inc="vE")
                ln2_ev[j] = P.cnt["vE"]
            ln2_vE = P.cnt["vE"]
            for j in (4 * g + 4, 4 * g + 5):
                if j < NCH:
                    emit_wp(j)
            for jj in range(4 * g, 4 * g + 4):
                P.wait("tensor", "vE", ln2_ev[jj])
                if jj % 4 < 2:
                    pe = 4 * g + 4 + jj % 2 if g < 3 else 4 * g + 2 + jj % 2
                    P.wait("tensor", "sE", proj_evac[pe])
                else:
                    P.wait("tensor", "vE", trev_chunk[jj - 2])
                for c in range(DCH):
                    mm(pt[jj % 2][:, 128 * c:128 * (c + 1)],
                       RING[:, jj % 4, 128 * c:128 * (c + 1)], IDM[:, :],
                       start=True, stop=True, inc=(c == DCH - 1))
                P.wait("vector", "tE", P.cnt["tE"])
                P.run("vector", lambda e, jj=jj: e.tensor_copy(
                    out=LNT.ap()[:, :, jj, :],
                    in_=pt[jj % 2][:, :].rearrange("p (c t) -> p c t", c=8)),
                    inc="vE")
                trev_chunk[jj] = P.cnt["vE"]
            tr_mm_tE[g] = P.cnt["tE"]
            for jj in range(4 * g, 4 * g + 4):
                psf = pt[2 + (jj % 2)]
                P.wait("tensor", "vE", trev_chunk[jj])
                P.wait("tensor", "vE", weff_vE)
                P.wait("tensor", "vE", m_ev[7])
                if jj >= 2:
                    P.wait("tensor", "sE", ffn_evac[jj - 2])
                for dc in range(DCH):
                    for h in range(2):
                        mm(psf[:, 512 * h:512 * (h + 1)], LNT.ap()[:, dc, jj, :],
                           WA[:, dc, 512 * h:512 * (h + 1)],
                           start=(dc == 0), stop=False)
                for h in range(2):
                    mm(psf[:, 512 * h:512 * (h + 1)], ONE1[0:1, :],
                       ROWB.ap()[0:1, 512 * h:512 * (h + 1)], start=False,
                       stop=True, inc=(h == 1))
                P.wait("scalar", "tE", P.cnt["tE"])
                if jj >= 4:
                    P.wait("scalar", "vE", ffn_done_v[jj - 4])
                    P.wait("scalar", f"dSt{jj % 4}", P.cnt[f"dSt{jj % 4}"])
                P.run("scalar", lambda e, psf=psf, jj=jj: e.activation(
                    out=FROLL.ap()[:, jj % 4, :], in_=psf[:, :], func=AF.Copy,
                    accum_out=LSUM4[:, jj:jj + 1]), inc="sE")
                ffn_evac[jj] = P.cnt["sE"]
                P.run("scalar", lambda e, jj=jj: e.activation(
                    out=JK1[:, :], in_=FROLL.ap()[:, jj % 4, :], func=AF.Square,
                    accum_out=LSQ4[:, jj:jj + 1]), inc="sE")
            group_prep(LMU4, LR4, LSUM4, LSQ4, sl, P.cnt["sE"])
            for jj in range(4 * g, 4 * g + 4):
                fr = FROLL.ap()[:, jj % 4, :]
                P.wait("vector", "sE", ffn_evac[jj])
                P.run("vector", lambda e, fr=fr, jj=jj: e.tensor_scalar(
                    out=fr, in0=fr, scalar1=LMU4[:, jj:jj + 1],
                    scalar2=LR4[:, jj:jj + 1], op0=ALU.subtract, op1=ALU.mult))
                P.run("vector", lambda e, fr=fr, jj=jj: e.tensor_tensor(
                    out=fr, in0=fr, in1=XP.ap()[:, jj % 6, :], op=ALU.add),
                    inc="vE")
                xp_free[jj] = P.cnt["vE"]
                ffn_done_v[jj] = P.cnt["vE"]
                P.wait("sync", "vE", xp_free[jj])
                P.dma("sync", y_out[128 * jj:128 * (jj + 1), :], fr,
                      f"dSt{jj % 4}")

    except _StopBuild:
        for s in DMA_SEMS + ("vE", "sE", "tE", "cc"):
            P.wait("sync", s, P.cnt[s])
        for j in range(NCH):
            P.dma("sync", y_out[128 * j:128 * (j + 1), :], WS[:, 0, :])
    for cl in DMA_SEMS:
        P.wait("sync", cl, P.cnt[cl])
    P.wait("gpsimd", "dSt3", P.cnt["dSt3"])

    with ctx, nc.Block() as block:
        for eng_name in ENGS:
            fns = P.ops[eng_name]

            def body(e, fns=fns):
                for f in fns:
                    f(e)
            getattr(block, eng_name)(body)

    return nc, tap_done


def kernel(**inputs):
    x = np.ascontiguousarray(np.asarray(inputs["L0_attn_x"], np.float32).reshape(16384, D))

    def T(a):
        return np.ascontiguousarray(np.asarray(a, np.float32).T)

    def chunkvec(v):
        return np.ascontiguousarray(np.asarray(v, np.float32).reshape(8, 128).T)

    common = {
        "wqT": T(inputs["L0_attn_Wq"]),
        "wkT": T(inputs["L0_attn_Wk"]),
        "wvT": T(inputs["L0_attn_Wv"]),
        "wpT": T(inputs["L0_attn_Wp"]),
        "wT": T(inputs["L0_ffn_W"]),
        "gq": chunkvec(inputs["L0_attn_gq"]),
        "bq": chunkvec(inputs["L0_attn_bq"]),
        "gk": chunkvec(inputs["L0_attn_gk"]),
        "bk": chunkvec(inputs["L0_attn_bk"]),
        "gv": np.ascontiguousarray(np.asarray(inputs["L0_attn_gv"], np.float32).reshape(1, D)),
        "bv": np.ascontiguousarray(np.asarray(inputs["L0_attn_bv"], np.float32).reshape(1, D)),
        "bp": np.ascontiguousarray(np.asarray(inputs["L0_attn_bp"], np.float32).reshape(1, D)),
        "bb": np.ascontiguousarray(np.asarray(inputs["L0_ffn_b"], np.float32).reshape(1, D)),
        "ldT": T(inputs["L0_ffn_lora_down"]),
        "luT": T(inputs["L0_ffn_lora_up"]),
        "dax": np.ascontiguousarray(np.asarray(inputs["L0_ffn_down_aux"], np.float32)),
        "uaxT": T(inputs["L0_ffn_up_aux"]),
        "idm": np.eye(128, dtype=np.float32),
    }
    nc, _ = build()
    in_maps = []
    for c in range(NCORE):
        m = dict(common)
        m["x_in"] = np.ascontiguousarray(x[TOK * c:TOK * (c + 1)])
        in_maps.append(m)
    res = run_bass_kernel_spmd(nc, in_maps, core_ids=list(range(NCORE)))
    out = np.concatenate([np.asarray(res.results[c]["y"]) for c in range(NCORE)],
                         axis=0)
    return out.reshape(2, 8192, D).astype(np.float32)
